# version 26
# baseline (speedup 1.0000x reference)
import contextlib
import os
from concourse.bass_core_rust import IndirectOffsetOnAxis
import numpy as np
import concourse.bass as bass
import concourse.mybir as mybir
from concourse.bass_utils import run_bass_kernel_spmd

F32 = mybir.dt.float32
BF16 = mybir.dt.bfloat16
AF = mybir.ActivationFunctionType
ALU = mybir.AluOpType
AX = mybir.AxisListType
I32 = mybir.dt.int32

D = 1024
DC = 8
NH = 16
DH = 64
S = 8192
SH = 4096
NPT = 8448
NCH = 66
OWN_C0 = 1
KV_CHUNKS = [c for c in range(NCH) if (1 <= c <= 32) or (34 <= c <= 65)]
T1 = 512
NCK1 = 4
NE = 8
DI = 2048
LN_EPS = 1e-5

DEV = int(os.environ.get("KDEV", "0"))


class Prog:
    def __init__(self):
        self.steps = []
        self.counts = {}

    def val(self, name):
        return self.counts.get(name, 0)

    def step(self, engine, emit, waits=(), inc=None):
        val = None
        if inc is not None:
            name, by = inc
            self.counts[name] = self.counts.get(name, 0) + by
            val = self.counts[name]
        ws = tuple((sn, sv) for (sn, sv) in waits if sv and sv > 0)
        self.steps.append((engine, ws, emit, inc))
        return val

    def emit_for(self, engine_name, eng, sems):
        for engine, waits, emit, inc in self.steps:
            if engine != engine_name:
                continue
            for (sn, sv) in waits:
                eng.wait_ge(sems[sn], sv)
            ins = emit(eng)
            if inc is not None:
                name, by = inc
                assert ins is not None, f"step with inc {name} returned None"
                ins.then_inc(sems[name], by)


def build_program():
    nc = bass.Bass("TRN2", target_bir_lowering=False, debug=False)

    ap = {}
    def din(name, shape, dt=F32):
        ap[name] = nc.dram_tensor(name, list(shape), dt, kind="ExternalInput").ap()
        return ap[name]

    xp = din("xp", [NPT, D])
    for w in ["wq_lin", "wk_lin", "wv_lin", "wq_loc", "wk_loc", "wv_loc", "wo", "wg2"]:
        din(w, [D, D])
    din("wg1", [3 * D, D])
    for b in ["bq_lin", "bk_lin", "bv_lin", "bq_loc", "bk_loc", "bv_loc",
              "bo", "bg1", "bg2", "ln_g", "ln_b", "fln_w", "fln_b"]:
        din(b, [1, D])
    din("rel_bias", [129, NH])
    din("router_w", [D, NE])
    din("gate_w", [NE, D, DI])
    din("up_w", [NE, D, DI])
    din("down_w", [NE, DI, D])
    din("sh_gate", [D, DI])
    din("sh_up", [D, DI])
    din("sh_down", [DI, D])
    din("lmask", [1, 192])
    din("rmask", [1, 192])

    y = nc.dram_tensor("y", [SH, D], F32, kind="ExternalOutput").ap()

    h_scr = nc.dram_tensor("h_scr", [DC, 128, 34 * 128], BF16, kind="Internal").ap()
    xn_scr = nc.dram_tensor("xn_scr", [SH, D], F32, kind="Internal").ap()
    nrm_scr = nc.dram_tensor("nrm_scr", [DC, 128, SH], BF16, kind="Internal").ap()
    win_scr = nc.dram_tensor("win_scr", [DC, 128, SH], BF16, kind="Internal").ap()
    winf_scr = nc.dram_tensor("winf_scr", [DC, 128, SH], F32, kind="Internal").ap()
    s_scr = nc.dram_tensor("s_scr", [SH, D], F32, kind="Internal").ap()
    n_tm_scr = nc.dram_tensor("n_tm_scr", [SH, D], BF16, kind="Internal").ap()

    dbg = {}
    if DEV:
        def dout(name, shape, dt=F32):
            dbg[name] = nc.dram_tensor(name, list(shape), dt, kind="ExternalOutput").ap()
        dout("dbg_kv", [128, DC * D])
        dout("dbg_st", [128, 12 * NCK1])
        dout("dbg_ksum", [128, DC])
        dout("dbg_h", [DC, 128, 34 * 128], BF16)
        dout("dbg_xn", [SH, D])
        dout("dbg_nrm", [DC, 128, SH], BF16)
        dout("dbg_win", [DC, 128, SH])
        dout("dbg_lin", [DC, 128, SH])
        dout("dbg_s", [SH, D])
        dout("dbg_moe", [DC, 128, SH])
        dout("dbg_moe2", [SH, D])
        dout("dbg_route", [128, 32])
        dout("dbg_rc", [NE * 256, D])
        dout("dbg_xg", [128, 2 * D], BF16)
        dout("dbg_hm", [128, 16 * 256], BF16)
        dout("dbg_rcfm", [128, 2048])

    es = contextlib.ExitStack()
    def sb(name, shape, dt):
        return es.enter_context(nc.sbuf_tensor(name, list(shape), dt))
    def psum(name, shape, dt=F32):
        return es.enter_context(nc.psum_tensor(name, list(shape), dt))

    c_id_bf = sb("c_id_bf", [128, 128], BF16)
    c_id_f = sb("c_id_f", [128, 128], F32)
    c_bias = sb("c_bias", [128, 13 * DC], F32)
    c_ones = sb("c_ones", [128, 8], BF16)
    c_bkv_bc = sb("c_bkv_bc", [128, 2 * D], F32)
    c_bo_bc = sb("c_bo_bc", [128, D], F32)
    c_lt = sb("c_lt", [128, 128], BF16)
    c_flnw_bc = sb("c_flnw_bc", [128, D], BF16)
    c_flnb_bc = sb("c_flnb_bc", [128, D], BF16)

    BIAS_ORDER = ["bq_lin", "bk_lin", "bv_lin", "bq_loc", "bk_loc", "bv_loc",
                  "bo", "bg1", "bg2", "ln_g", "ln_b", "fln_w", "fln_b"]
    BIDX = {n: i for i, n in enumerate(BIAS_ORDER)}
    def bias_col(name, j):
        o = BIDX[name] * DC + j
        return c_bias[:, o:o + 1]

    NF_AF = 24064
    NF_AB = 40960
    ARF = sb("ARF", [128, NF_AF], F32).ap()
    ARB = sb("ARB", [128, NF_AB], BF16).ap()

    def af(off, n):
        assert off + n <= NF_AF
        return ARF[:, off:off + n]
    def afb(off, n):
        assert off + n <= NF_AF and off % 8 == 0
        return ARF[:, off:off + n].bitcast(BF16)
    def ab(off, n):
        assert off + n <= NF_AB
        return ARB[:, off:off + n]

    x_tm = [af(0, 4096), af(4096, 4096)]
    xn_tm = af(8192, 4096)
    kv_sb = af(12288, 8192)
    elu_a = af(20480, 1024)
    elu_b = af(21504, 1024)
    sq_scr = af(22528, 1024)
    st_sum = af(23552, 16)
    st_m = af(23568, 32)
    n_bf = ab(0, 4096)
    h_fm = [ab(4096, 4096), ab(8192, 4096)]
    nrm_fm = ab(12288, 4096)
    ke_tm = ab(16384, 4096)
    v_tm = ab(20480, 4096)
    w_kl = ab(24576, 8192)
    w_vl = ab(32768, 8192)
    kv_bf = sb("kv_bf", [128, DC * D], BF16)
    ksum_bf = sb("ksum_bf", [128, DC], BF16)

    PS = psum("PS", [128, 4096], F32).ap()
    ps_p = PS[:, 0:1024]
    ps_kv = [PS[:, 1024:2048], PS[:, 2048:3072]]
    ps_ks = PS[:, 3072:3072 + DC]
    ps_t = PS[:, 3584:4096].bitcast(BF16)

    sems = {}
    for s_ in ["sX", "sXH", "sWT", "sWS", "sST", "sMM", "sVE", "sSE", "sGP", "sDM",
               "sOUT", "sPC", "sCV", "sCS", "sCG"]:
        sems[s_] = es.enter_context(nc.semaphore(s_))

    chain_cnt = {"sCV": 0, "sCS": 0, "sCG": 0}
    def guard(eng, ins, cs):
        chain_cnt[cs] += 1
        ins.then_inc(sems[cs], 1)
        eng.wait_ge(sems[cs], chain_cnt[cs])
        return ins

    P = Prog()

    def em_idmat(g):
        guard(g, g.iota(out=c_id_f[:], pattern=[[1, 128]], base=0,
                        channel_multiplier=-1,
                        allow_small_or_imprecise_dtypes=True), "sCG")
        guard(g, g.tensor_single_scalar(out=c_id_f[:], in_=c_id_f[:],
                                        scalar=0.0, op=ALU.is_equal), "sCG")
        return g.tensor_copy(out=c_id_bf[:], in_=c_id_f[:])
    gp_id = P.step("gpsimd", em_idmat, inc=("sGP", 1))
    ve_ones = P.step("vector", lambda v: v.memset(c_ones[:], 1.0), inc=("sVE", 1))

    def em_bias(name):
        def f(e):
            src = ap[name].rearrange("o (j p) -> (o p) j", p=128)
            return e.dma_start(out=c_bias[:, BIDX[name] * DC:(BIDX[name] + 1) * DC],
                               in_=src, allow_slow_non_contiguous=True)
        return f
    for bn in BIAS_ORDER:
        dm_t = P.step("sync", em_bias(bn), inc=("sDM", 16))
    dm_t = P.step("sync", lambda e: e.dma_start(
        out=c_bkv_bc[:, 0:D], in_=ap["bk_lin"].broadcast_to((128, D))), inc=("sDM", 16))
    dm_t = P.step("sync", lambda e: e.dma_start(
        out=c_bkv_bc[:, D:2 * D], in_=ap["bv_lin"].broadcast_to((128, D))), inc=("sDM", 16))
    dm_t = P.step("sync", lambda e: e.dma_start(
        out=c_bo_bc[:], in_=ap["bo"].broadcast_to((128, D))), inc=("sDM", 16))
    dm_t2 = P.step("gpsimd", lambda g: g.dma_start(
        out=c_flnw_bc[:], in_=ap["fln_w"].broadcast_to((128, D))), inc=("sWT", 16))
    dm_t2 = P.step("gpsimd", lambda g: g.dma_start(
        out=c_flnb_bc[:], in_=ap["fln_b"].broadcast_to((128, D))), inc=("sWT", 16))
    SETUP_DM = dm_t

    def em_wload(dst, src_name):
        def f(g):
            src = ap[src_name].rearrange("(i p) n -> p i n", p=128)
            nI = src.shape[1]
            return g.dma_start(out=dst[:].rearrange("p (i n) -> p i n", i=nI), in_=src)
        return f
    P.step("gpsimd", em_wload(w_kl, "wk_lin"), inc=("sWT", 16))
    PH1_WT = P.step("gpsimd", em_wload(w_vl, "wv_lin"), inc=("sWT", 16))

    gwb = nc.dram_tensor("gwb", [NE, D, DI], BF16, kind="Internal").ap()
    uwb = nc.dram_tensor("uwb", [NE, D, DI], BF16, kind="Internal").ap()
    dwb = nc.dram_tensor("dwb", [NE, DI, D], BF16, kind="Internal").ap()
    sgb = nc.dram_tensor("sgb", [D, DI], BF16, kind="Internal").ap()
    sub_ = nc.dram_tensor("sub", [D, DI], BF16, kind="Internal").ap()
    sdb = nc.dram_tensor("sdb", [DI, D], BF16, kind="Internal").ap()
    N_PC = 0
    for e9 in range(NE):
        for dst9, nm9 in [(gwb, "gate_w"), (uwb, "up_w"), (dwb, "down_w")]:
            P.step("gpsimd", lambda g, dst9=dst9, nm9=nm9, e9=e9: g.dma_start(
                out=dst9[e9], in_=ap[nm9][e9]), inc=("sPC", 16))
            N_PC += 16
    for dst9, nm9 in [(sgb, "sh_gate"), (sub_, "sh_up"), (sdb, "sh_down")]:
        P.step("gpsimd", lambda g, dst9=dst9, nm9=nm9: g.dma_start(
            out=dst9, in_=ap[nm9]), inc=("sPC", 16))
        N_PC += 16

    groups = []
    c0 = 0
    while c0 < NCH:
        n = min(NCK1, NCH - c0)
        groups.append((c0, n))
        c0 += n

    st = {"x_rd": {}, "h_rd": {}, "kv_ve": 0, "ks_started": False,
          "xn_st": 0, "proj_rd": 0, "kv_mm": {}, "h_ready": {}, "ke_rd": 0,
          "kv_j_seen": 0, "x_rd_se": 0}
    n_kv_total = len(KV_CHUNKS)
    kv_seen = 0

    for g_idx, (c_start, nck) in enumerate(groups):
        buf = g_idx % 2
        xt = x_tm[buf]
        hfm = h_fm[buf]
        tok0 = c_start * 128

        waits = []
        if g_idx >= 2:
            waits.append(("sVE", st["x_rd"].get(g_idx - 2, 0)))
            waits.append(("sSE", st["x_rd_se"]))
        ld_x = P.step("sync", lambda e, xt=xt, tok0=tok0, nck=nck: e.dma_start(
            out=xt[:, 0:nck * D].rearrange("p (c d) -> p c d", c=nck),
            in_=xp[tok0:tok0 + nck * 128, :].rearrange("(c p) d -> p c d", p=128)),
            waits=waits, inc=("sX", 16))

        for cc in range(nck):
            xa = xt[:, cc * D:(cc + 1) * D]
            ve_red1 = P.step("vector", lambda v, xa=xa, cc=cc: v.reduce_sum(
                out=st_sum[:, cc:cc + 1], in_=xa, axis=AX.X),
                waits=([("sX", 16 * (g_idx + 1))] if cc == 0 else ()),
                inc=(("sVE", 1) if cc == nck - 1 else None))
            se_sq = P.step("scalar", lambda s, xa=xa, cc=cc: s.activation(
                out=sq_scr[:], in_=xa, func=AF.Square,
                accum_out=st_sum[:, NCK1 + cc:NCK1 + cc + 1]),
                waits=([("sX", 16 * (g_idx + 1))] if cc == 0 else ()),
                inc=(("sSE", 1) if cc == nck - 1 else None))

        def em_stats(dst_mean, dst_var, tmp, src_off=0):
            def f(v):
                guard(v, v.tensor_scalar(out=st_m[:, dst_mean:dst_mean + NCK1],
                                         in0=st_sum[:, src_off:src_off + NCK1],
                                         scalar1=1.0 / D, scalar2=None, op0=ALU.mult), "sCV")
                guard(v, v.tensor_tensor(out=st_m[:, tmp:tmp + NCK1],
                                         in0=st_m[:, dst_mean:dst_mean + NCK1],
                                         in1=st_m[:, dst_mean:dst_mean + NCK1],
                                         op=ALU.mult), "sCV")
                return guard(v, v.scalar_tensor_tensor(
                    out=st_m[:, dst_var:dst_var + NCK1],
                    in0=st_sum[:, src_off + NCK1:src_off + 2 * NCK1],
                    scalar=1.0 / D, in1=st_m[:, tmp:tmp + NCK1],
                    op0=ALU.mult, op1=ALU.subtract), "sCV")
            return f

        def em_rsqrt(dst, src, tmp):
            def f(v):
                guard(v, v.memset(st_m[:, dst:dst + NCK1], 1.0), "sCV")
                ins = None
                for _ in range(5):
                    guard(v, v.tensor_tensor(out=st_m[:, tmp:tmp + NCK1],
                                             in0=st_m[:, dst:dst + NCK1],
                                             in1=st_m[:, dst:dst + NCK1], op=ALU.mult), "sCV")
                    guard(v, v.scalar_tensor_tensor(out=st_m[:, tmp:tmp + NCK1],
                                                    in0=st_m[:, src:src + NCK1], scalar=LN_EPS,
                                                    in1=st_m[:, tmp:tmp + NCK1],
                                                    op0=ALU.add, op1=ALU.mult), "sCV")
                    guard(v, v.tensor_scalar(out=st_m[:, tmp:tmp + NCK1],
                                             in0=st_m[:, tmp:tmp + NCK1],
                                             scalar1=-0.5, scalar2=1.5,
                                             op0=ALU.mult, op1=ALU.add), "sCV")
                    ins = v.tensor_tensor(out=st_m[:, dst:dst + NCK1],
                                          in0=st_m[:, dst:dst + NCK1],
                                          in1=st_m[:, tmp:tmp + NCK1], op=ALU.mult)
                    if _ < 4:
                        guard(v, ins, "sCV")
                return ins
            return f

        P.step("vector", em_stats(0, 1 * NCK1, 6 * NCK1, 0),
               waits=[("sSE", se_sq), ("sVE", ve_red1)])
        ve_r1 = P.step("vector", em_rsqrt(2 * NCK1, 1 * NCK1, 7 * NCK1), inc=("sVE", 1))

        xw = [("sST", st["xn_st"]), ("sSE", st.get("xn_se", 0)), ("sVE", ve_r1)]
        for cc in range(nck):
            ve_xn = P.step("vector", lambda v, cc=cc, xt=xt: v.tensor_scalar(
                out=xn_tm[:, cc * D:(cc + 1) * D], in0=xt[:, cc * D:(cc + 1) * D],
                scalar1=st_m[:, cc:cc + 1], scalar2=st_m[:, 2 * NCK1 + cc:2 * NCK1 + cc + 1],
                op0=ALU.subtract, op1=ALU.mult), waits=(xw if cc == 0 else ()),
                inc=(("sVE", 1) if cc == nck - 1 else None))
        st["x_rd"][g_idx] = ve_xn
        st["x_rd_se"] = se_sq

        for cc in range(nck):
            xa = xn_tm[:, cc * D:(cc + 1) * D]
            ve_red2 = P.step("vector", lambda v, xa=xa, cc=cc: v.reduce_sum(
                out=st_sum[:, 2 * NCK1 + cc:2 * NCK1 + cc + 1], in_=xa, axis=AX.X),
                waits=([("sVE", ve_xn)] if cc == 0 else ()),
                inc=(("sVE", 1) if cc == nck - 1 else None))
            se_sq2 = P.step("scalar", lambda s, xa=xa, cc=cc: s.activation(
                out=sq_scr[:], in_=xa, func=AF.Square,
                accum_out=st_sum[:, 3 * NCK1 + cc:3 * NCK1 + cc + 1]),
                waits=([("sVE", ve_xn)] if cc == 0 else ()),
                inc=(("sSE", 1) if cc == nck - 1 else None))
        st["xn_se"] = se_sq2
        P.step("vector", em_stats(3 * NCK1, 4 * NCK1, 6 * NCK1, 2 * NCK1),
               waits=[("sSE", se_sq2), ("sVE", ve_red2)])
        ve_r2 = P.step("vector", em_rsqrt(5 * NCK1, 4 * NCK1, 7 * NCK1), inc=("sVE", 1))
        if DEV and g_idx == 0:
            P.step("sync", lambda e: e.dma_start(out=dbg["dbg_st"][:, 0:8 * NCK1],
                                                 in_=st_m[:]),
                   waits=[("sVE", ve_r2)], inc=("sOUT", 16))
            P.step("sync", lambda e: e.dma_start(out=dbg["dbg_st"][:, 8 * NCK1:12 * NCK1],
                                                 in_=st_sum[:]), inc=("sOUT", 16))

        nw = [("sMM", st.get("nbf_mm", 0)), ("sVE", ve_r2)]
        for cc in range(nck):
            ve_nbf = P.step("vector", lambda v, cc=cc: v.tensor_scalar(
                out=n_bf[:, cc * D:(cc + 1) * D], in0=xn_tm[:, cc * D:(cc + 1) * D],
                scalar1=st_m[:, 3 * NCK1 + cc:3 * NCK1 + cc + 1],
                scalar2=st_m[:, 5 * NCK1 + cc:5 * NCK1 + cc + 1],
                op0=ALU.subtract, op1=ALU.mult), waits=(nw if cc == 0 else ()),
                inc=(("sVE", 1) if cc == nck - 1 else None))

        hw = [("sMM", st["h_rd"].get(g_idx - 2, 0)),
              ("sST", st.get("h_st", {}).get(g_idx - 2, 0))]
        mm_last = 0
        for cc in range(nck):
            def em_tr(t, cc=cc):
                ins = None
                for j in range(DC):
                    ins = t.transpose(out=ps_t[:, j * 128:(j + 1) * 128],
                                      in_=n_bf[:, cc * D + j * 128:cc * D + (j + 1) * 128],
                                      identity=c_id_bf[:])
                return ins
            mm_tr = P.step("tensor", em_tr,
                           waits=[("sVE", ve_nbf), ("sGP", gp_id),
                                  ("sVE", st.get("pst_rd", 0))] + (hw if cc == 0 else []),
                           inc=("sMM", 1))
            own = OWN_C0 <= (c_start + cc) < OWN_C0 + 32
            def em_evac(v, cc=cc, own=own, hfm=hfm):
                ins = None
                for j in range(DC):
                    ins = v.tensor_scalar(
                        out=hfm[:, j * T1 + cc * 128: j * T1 + (cc + 1) * 128],
                        in0=ps_t[:, j * 128:(j + 1) * 128],
                        scalar1=bias_col("ln_g", j), scalar2=bias_col("ln_b", j),
                        op0=ALU.mult, op1=ALU.add)
                if own:
                    for j in range(DC):
                        ins = v.tensor_scalar(
                            out=nrm_fm[:, j * T1 + cc * 128: j * T1 + (cc + 1) * 128],
                            in0=ps_t[:, j * 128:(j + 1) * 128],
                            scalar1=bias_col("fln_w", j), scalar2=bias_col("fln_b", j),
                            op0=ALU.mult, op1=ALU.add)
                return ins
            ve_ev = P.step("vector", em_evac,
                           waits=[("sMM", mm_tr), ("sDM", SETUP_DM), ("sWT", PH1_WT),
                                  ("sST", st.get("nrm_st", 0) if own and cc == 0 else 0)],
                           inc=("sVE", 1))
            st["pst_rd"] = ve_ev
            st["h_ready"][(g_idx, cc)] = ve_ev
        st["nbf_mm"] = mm_tr
        ve_hdone = st["h_ready"][(g_idx, nck - 1)]

        if c_start < 34:
            nst = min(nck, 34 - c_start)
            st_h = P.step("sync", lambda e, tok0=tok0, nst=nst, hfm=hfm: e.dma_start(
                out=h_scr[:, :, tok0:tok0 + nst * 128].rearrange("j p t -> p j t"),
                in_=hfm.rearrange("p (j t) -> p j t", j=DC)[:, :, 0:nst * 128]),
                waits=[("sVE", ve_hdone)], inc=("sST", 16))
            st.setdefault("h_st", {})[g_idx] = st_h
        own_lo = max(c_start, OWN_C0)
        own_hi = min(c_start + nck, OWN_C0 + 32)
        if own_lo < own_hi:
            cc0, cc1 = own_lo - c_start, own_hi - c_start
            ot = (own_lo - OWN_C0) * 128
            ncc = cc1 - cc0
            st_x = P.step("sync", lambda e, cc0=cc0, ncc=ncc, ot=ot: e.dma_start(
                out=xn_scr[ot:ot + ncc * 128, :].rearrange("(c p) d -> p c d", p=128),
                in_=xn_tm.rearrange("p (c d) -> p c d", c=NCK1)[:, cc0:cc0 + ncc, :]),
                waits=[("sVE", ve_nbf)], inc=("sST", 16))
            st_x = P.step("sync", lambda e, cc0=cc0, ncc=ncc, ot=ot: e.dma_start(
                out=n_tm_scr[ot:ot + ncc * 128, :].rearrange("(c p) d -> p c d", p=128),
                in_=n_bf.rearrange("p (c d) -> p c d", c=NCK1)[:, cc0:cc0 + ncc, :]),
                inc=("sST", 16))
            st["xn_st"] = st_x
            st_n = P.step("sync", lambda e, cc0=cc0, ncc=ncc, ot=ot: e.dma_start(
                out=nrm_scr[:, :, ot:ot + ncc * 128].rearrange("j p t -> p j t"),
                in_=nrm_fm.rearrange("p (j t) -> p j t", j=DC)[:, :, cc0 * 128:(cc0 + ncc) * 128]),
                waits=[("sVE", ve_hdone)], inc=("sST", 16))
            st["nrm_st"] = st_n

        kv_cc = [cc for cc in range(nck) if (c_start + cc) in set(KV_CHUNKS)]
        for cc in kv_cc:
            def em_kproj(t, cc=cc, hfm=hfm):
                ins = None
                for di in range(DC):
                    lhsT = hfm[:, di * T1 + cc * 128: di * T1 + (cc + 1) * 128]
                    for half in range(2):
                        ins = t.matmul(
                            out=ps_p[:, half * 512:(half + 1) * 512], lhsT=lhsT,
                            rhs=w_kl[:, di * D + half * 512: di * D + (half + 1) * 512],
                            start=(di == 0), stop=(di == DC - 1))
                return ins
            mm_k = P.step("tensor", em_kproj,
                          waits=[("sVE", st["h_ready"][(g_idx, cc)]), ("sWT", PH1_WT),
                                 ("sVE", st["proj_rd"])],
                          inc=("sMM", 1))
            P.step("vector", lambda v: guard(v, v.tensor_tensor(
                out=elu_a[:], in0=ps_p[:], in1=c_bkv_bc[:, 0:D], op=ALU.add), "sCV"),
                waits=[("sMM", mm_k)])
            ve_min = P.step("vector", lambda v: v.tensor_scalar(
                out=elu_b[:], in0=elu_a[:], scalar1=0.0, scalar2=None, op0=ALU.min),
                inc=("sVE", 1))
            st["proj_rd"] = ve_min
            se_exp = P.step("scalar", lambda s: s.activation(
                out=elu_b[:], in_=elu_b[:], func=AF.Exp),
                waits=[("sVE", ve_min)], inc=("sSE", 1))

            def em_vproj(t, cc=cc, hfm=hfm):
                ins = None
                for di in range(DC):
                    lhsT = hfm[:, di * T1 + cc * 128: di * T1 + (cc + 1) * 128]
                    for half in range(2):
                        ins = t.matmul(
                            out=ps_p[:, half * 512:(half + 1) * 512], lhsT=lhsT,
                            rhs=w_vl[:, di * D + half * 512: di * D + (half + 1) * 512],
                            start=(di == 0), stop=(di == DC - 1))
                return ins
            mm_v = P.step("tensor", em_vproj, waits=[("sVE", ve_min)], inc=("sMM", 1))

            def em_elu_fin(v, cc=cc):
                guard(v, v.tensor_scalar(out=elu_a[:], in0=elu_a[:], scalar1=0.0,
                                         scalar2=None, op0=ALU.max), "sCV")
                return v.tensor_tensor(out=ke_tm[:, cc * D:(cc + 1) * D],
                                       in0=elu_a[:], in1=elu_b[:], op=ALU.add)
            ve_ke = P.step("vector", em_elu_fin, waits=[("sSE", se_exp)], inc=("sVE", 1))

            ve_v = P.step("vector", lambda v, cc=cc: v.tensor_tensor(
                out=v_tm[:, cc * D:(cc + 1) * D], in0=ps_p[:], in1=c_bkv_bc[:, D:2 * D],
                op=ALU.add), waits=[("sMM", mm_v)], inc=("sVE", 1))
            st["proj_rd"] = ve_v
            st["kev_ready"] = ve_v

        if kv_cc:
            kvadd_t = {}
            for j in range(DC):
                pbuf = ps_kv[j % 2]
                def em_kvmm(t, j=j, kv_cc=tuple(kv_cc), pbuf=pbuf):
                    ins = None
                    for idx, cc in enumerate(kv_cc):
                        lhsT = ke_tm[:, cc * D + j * 128: cc * D + (j + 1) * 128]
                        for half in range(2):
                            ins = t.matmul(
                                out=pbuf[:, half * 512:(half + 1) * 512], lhsT=lhsT,
                                rhs=v_tm[:, cc * D + half * 512: cc * D + (half + 1) * 512],
                                start=(idx == 0), stop=(idx == len(kv_cc) - 1 and half == 1))
                        ins2 = t.matmul(out=ps_ks[:, j:j + 1], lhsT=lhsT,
                                        rhs=c_ones[:, 0:1],
                                        start=(not st["ks_started"]),
                                        stop=(st["kv_j_seen"] == DC * n_kv_total - 1))
                        st["ks_started"] = True
                        st["kv_j_seen"] += 1
                        ins = ins2
                    return ins
                w = [("sVE", st["kev_ready"])]
                if j >= 2:
                    w.append(("sVE", kvadd_t[j - 2]))
                else:
                    w.append(("sVE", st["kv_ve"]))
                mm_kv = P.step("tensor", em_kvmm, waits=w, inc=("sMM", 1))
                def em_kvadd(v, j=j, pbuf=pbuf, first=(g_idx == 0)):
                    if first:
                        return v.tensor_copy(out=kv_sb[:, j * D:(j + 1) * D], in_=pbuf[:])
                    return v.tensor_tensor(out=kv_sb[:, j * D:(j + 1) * D],
                                           in0=kv_sb[:, j * D:(j + 1) * D],
                                           in1=pbuf[:], op=ALU.add)
                ve_kv = P.step("vector", em_kvadd,
                               waits=[("sMM", mm_kv),
                                      ("sVE", st.get("kvadd_prev", {}).get(j, 0))],
                               inc=("sVE", 1))
                st.setdefault("kvadd_prev", {})[j] = ve_kv
                kvadd_t[j] = ve_kv
                if j == DC - 1:
                    st["kv_ve"] = ve_kv
            st["h_rd"][g_idx] = mm_kv
            st["ke_rd"] = ve_kv
        else:
            st["h_rd"][g_idx] = P.val("sMM")

    last_kv_mm = P.val("sMM")
    last_kv_ve = P.val("sVE")
    def em_kvbf(v):
        ins = None
        for j in range(DC):
            ins = v.tensor_copy(out=kv_bf[:, j * D:(j + 1) * D],
                                in_=kv_sb[:, j * D:(j + 1) * D])
        return ins
    ve_kvbf = P.step("vector", em_kvbf, waits=[("sVE", last_kv_ve)], inc=("sVE", 1))
    ks_f = sb("ks_f", [128, DC], F32)
    def em_ksum(v):
        v.tensor_copy(out=ks_f[:], in_=ps_ks[:])
        return v.tensor_copy(out=ksum_bf[:], in_=ps_ks[:])
    ve_ksum = P.step("vector", em_ksum, waits=[("sMM", last_kv_mm)], inc=("sVE", 1))

    if DEV:
        d_t = P.step("sync", lambda e: e.dma_start(out=dbg["dbg_kv"], in_=kv_sb[:]),
                     waits=[("sVE", ve_kvbf)], inc=("sOUT", 16))
        d_t = P.step("sync", lambda e: e.dma_start(out=dbg["dbg_ksum"], in_=ks_f[:]),
                     waits=[("sVE", ve_ksum)], inc=("sOUT", 16))
        d_t = P.step("sync", lambda e: e.dma_start(
            out=dbg["dbg_h"].rearrange("j p t -> (j p) t"),
            in_=h_scr.rearrange("j p t -> (j p) t")), inc=("sOUT", 16))
        d_t = P.step("sync", lambda e: e.dma_start(out=dbg["dbg_xn"], in_=xn_scr),
                     inc=("sOUT", 16))
        d_t = P.step("sync", lambda e: e.dma_start(
            out=dbg["dbg_nrm"].rearrange("j p t -> (j p) t"),
            in_=nrm_scr.rearrange("j p t -> (j p) t")), inc=("sOUT", 16))


    PH1_VE = P.val("sVE"); PH1_SE = P.val("sSE"); PH1_MM = P.val("sMM")
    PH1_ST = P.val("sST"); PH1_OUT = P.val("sOUT")
    BAR1 = [("sVE", PH1_VE), ("sSE", PH1_SE), ("sMM", PH1_MM), ("sST", PH1_ST)] + \
           ([("sOUT", PH1_OUT)] if DEV else [])

    T2 = 512
    B_cat = af(0, 16 * 192)
    lm_cat = af(3072, 8 * 192)
    rm_cat = af(4608, 8 * 192)
    win_f = af(6144, 4096)
    scr_sm = af(10240, 8 * 192)
    nm_t = af(11776, 16)
    rs_t = af(11800, 16)
    rr_t = af(11824, 16)
    w_ql2 = afb(12288, 4096)
    A_bf = afb(16384, 1536)
    A2_bf = afb(17920, 1536)
    win_b = afb(19456, 2048)
    bvl_bc = af(21504, 1024)
    At_sb = afb(22528, 1024)
    hh_bufs = [ab(0, 5120), ab(5120, 5120)]
    q_fm2 = ab(10240, 4096)
    k_fm2 = ab(14336, 5120)
    vS_b = ab(19456, 5120)
    w_kl2 = w_kl
    w_vl2 = w_vl
    ps_s = PS[:, 0:2048]
    def ps_s_head(h8):
        o = (h8 % 2) * 1024 + (h8 // 2) * 256
        return PS[:, o:o + 192]
    ps_s4 = PS[:, 0:2048].rearrange("p (e g w) -> p e g w", e=2, w=256)[:, :, :, 0:192]
    scr4 = None
    At_ps = PS[:, 2048:3072].bitcast(BF16)
    ps_w2 = PS[:, 3072:4096]
    ps_q2 = [PS[:, 0:512], PS[:, 512:1024]]
    ps_k2 = PS[:, 1024:1664]
    ps_v2 = PS[:, 2048:3072]

    d_band = nc.dram_tensor("d_band", [128, 193], F32, kind="Internal").ap()
    d_rb = nc.dram_tensor("d_rb", [1, 65], F32, kind="Internal").ap()

    wt2 = P.step("gpsimd", em_wload(w_ql2, "wq_loc"), waits=BAR1, inc=("sWT", 16))
    wt2 = P.step("gpsimd", em_wload(w_kl2, "wk_loc"), inc=("sWT", 16))
    wt2 = P.step("gpsimd", em_wload(w_vl2, "wv_loc"), inc=("sWT", 16))
    PH2_WT = wt2
    dm2 = P.step("sync", lambda e: e.dma_start(
        out=bvl_bc[:], in_=ap["bv_loc"].broadcast_to((128, D))), waits=BAR1, inc=("sDM", 16))
    dm2 = P.step("sync", lambda e: e.dma_start(
        out=lm_cat.rearrange("p (t k) -> p t k", t=8),
        in_=ap["lmask"].broadcast_to((1024, 192)).rearrange("(p t) k -> p t k", p=128)),
        inc=("sDM", 16))
    dm2 = P.step("sync", lambda e: e.dma_start(
        out=rm_cat.rearrange("p (t k) -> p t k", t=8),
        in_=ap["rmask"].broadcast_to((1024, 192)).rearrange("(p t) k -> p t k", p=128)),
        inc=("sDM", 16))
    ve_fill = P.step("vector", lambda v: v.memset(scr_sm[:, 0:193], -1e30),
                     waits=BAR1, inc=("sVE", 1))
    for hh_ in range(NH):
        dm2 = P.step("sync", lambda e, hh_=hh_: e.dma_start(
            out=d_rb, in_=ap["rel_bias"][32:97, hh_:hh_ + 1].rearrange("a b -> b a"),
            allow_slow_non_contiguous=True),
            waits=[("sDM", dm2)], inc=("sDM", 16))
        dm2 = P.step("sync", lambda e: e.dma_start(out=d_band, in_=scr_sm[:, 0:193]),
                     waits=[("sVE", ve_fill), ("sDM", dm2)], inc=("sDM", 16))
        dm2 = P.step("sync", lambda e: e.dma_start(
            out=d_band[:, 0:65], in_=d_rb.broadcast_to((128, 65))),
            waits=[("sDM", dm2)], inc=("sDM", 16))
        dview = d_band.rearrange("a b -> (a b)")[0:128 * 192].rearrange("(a b) -> a b", b=192)
        dm2 = P.step("sync", lambda e, hh_=hh_, dview=dview: e.dma_start(
            out=B_cat[:, hh_ * 192:(hh_ + 1) * 192], in_=dview),
            waits=[("sDM", dm2)], inc=("sDM", 16))
    PH2_DM = dm2

    st2 = {"hh_rd": {}, "ps01_rd": 0, "ps23_rd": 0, "ps45_rd": 0, "ps67_rd": 0,
           "qk_ve": 0, "win_st": {}, "at_ve": 0}

    def plan_ldh2(t):
        hh3_ = hh_bufs[t % 2].rearrange("p (j t) -> p j t", j=DC)
        P0_ = 128 + T2 * t
        w_ = list(BAR1) + [("sDM", PH2_DM)]
        if t >= 2:
            w_.append(("sMM", st2["hh_rd"].get(t - 2, 0)))
        if t > 0:
            w_.append(("sXH", XH0 + 16 * t))
        P.step("sync", lambda e, hh3=hh3_, P0=P0_: e.dma_start(
            out=hh3, in_=h_scr[:, :, P0 - 64:P0 + 576].rearrange("j p t -> p j t")),
            waits=w_, inc=("sXH", 16))
    XH0 = P.counts.get("sXH", 0)
    plan_ldh2(0)
    for t in range(8):
        buf = t % 2
        hh = hh_bufs[buf]
        P0 = 128 + T2 * t
        hh3 = hh.rearrange("p (j t) -> p j t", j=DC)
        if t + 1 < 8:
            plan_ldh2(t + 1)
        xw = ("sXH", XH0 + 16 * (t + 1))

        for j in range(DC):
            pq = ps_q2[j % 2]
            def em_qp(tn, j=j, hh=hh, pq=pq):
                ins = None
                for di in range(DC):
                    ins = tn.matmul(out=pq[:],
                                    lhsT=w_ql2[:, di * D + j * 128: di * D + (j + 1) * 128],
                                    rhs=hh[:, di * 640 + 64: di * 640 + 576],
                                    start=(di == 0), stop=(di == DC - 1))
                return ins
            mm_q = P.step("tensor", em_qp,
                          waits=[xw, ("sWT", PH2_WT),
                                 ("sVE", st2["ps01_rd"] if j < 2 else 0),
                                 ("sVE", st2.get(f"qev{j-2}", 0) if j >= 2 else 0)],
                          inc=("sMM", 1))
            ve_q = P.step("vector", lambda v, j=j, pq=pq: v.tensor_scalar(
                out=q_fm2[:, j * T2:(j + 1) * T2], in0=pq[:],
                scalar1=bias_col("bq_loc", j), scalar2=0.125,
                op0=ALU.add, op1=ALU.mult),
                waits=[("sMM", mm_q)],
                inc=("sVE", 1))
            st2[f"qev{j}"] = ve_q
        for j in range(DC):
            def em_kp(tn, j=j, hh=hh):
                ins = None
                for di in range(DC):
                    tn.matmul(out=ps_k2[:, 0:512],
                              lhsT=w_kl2[:, di * D + j * 128: di * D + (j + 1) * 128],
                              rhs=hh[:, di * 640: di * 640 + 512],
                              start=(di == 0), stop=(di == DC - 1))
                for di in range(DC):
                    ins = tn.matmul(out=ps_k2[:, 512:640],
                                    lhsT=w_kl2[:, di * D + j * 128: di * D + (j + 1) * 128],
                                    rhs=hh[:, di * 640 + 512: di * 640 + 640],
                                    start=(di == 0), stop=(di == DC - 1))
                return ins
            mm_k = P.step("tensor", em_kp,
                          waits=[("sVE", st2["ps23_rd"] if j == 0 else
                                  st2.get(f"kev{j-1}", 0))],
                          inc=("sMM", 1))
            ve_k = P.step("vector", lambda v, j=j: v.tensor_scalar(
                out=k_fm2[:, j * 640:(j + 1) * 640], in0=ps_k2[:],
                scalar1=bias_col("bk_loc", j), scalar2=None, op0=ALU.add),
                waits=[("sMM", mm_k)], inc=("sVE", 1))
            st2[f"kev{j}"] = ve_k
        for m in range(5):
            rows = 128 if m < 4 else 64
            def em_vp(tn, m=m, hh=hh, rows=rows):
                ins = None
                for di in range(DC):
                    lhsT = hh[:, di * 640 + 32 + 128 * m: di * 640 + 32 + 128 * m + rows]
                    for half in range(2):
                        ins = tn.matmul(
                            out=ps_v2[0:rows, half * 512:(half + 1) * 512], lhsT=lhsT,
                            rhs=w_vl2[:, di * D + half * 512: di * D + (half + 1) * 512],
                            start=(di == 0), stop=(di == DC - 1))
                return ins
            mm_v = P.step("tensor", em_vp,
                          waits=[("sVE", st2["ps45_rd"] if m == 0 else
                                  st2.get(f"vev{m-1}", 0)),
                                 ("sDM", PH2_DM if m == 0 else 0)],
                          inc=("sMM", 1))
            ve_v = P.step("vector", lambda v, m=m, rows=rows: v.tensor_tensor(
                out=vS_b[0:rows, m * D:(m + 1) * D], in0=ps_v2[0:rows, :],
                in1=bvl_bc[0:rows, :], op=ALU.add),
                waits=[("sMM", mm_v)], inc=("sVE", 1))
            st2[f"vev{m}"] = ve_v
        qkv_ve = ve_v
        st2["hh_rd"][t] = mm_v

        pend = None
        def plan_trav(pd):
            m_, hhh_, vsc2_ = pd
            def em_tr2(tn, hhh=hhh_):
                ins = None
                for h8 in range(8):
                    h = 8 * hhh + h8
                    tn.transpose(out=At_ps[:, h8 * 256: h8 * 256 + 128],
                                 in_=A2_bf[:, h * 192: h * 192 + 128],
                                 identity=c_id_bf[:])
                    ins = tn.transpose(out=At_ps[0:64, h8 * 256 + 128: h8 * 256 + 256],
                                       in_=A2_bf[:, h * 192 + 128: h * 192 + 192],
                                       identity=c_id_bf[:])
                return ins
            mm_tr2 = P.step("tensor", em_tr2,
                            waits=[("sVE", vsc2_), ("sVE", st2["at_ve"]),
                                   ("sGP", gp_id)],
                            inc=("sMM", 1))
            ve_at = P.step("vector", lambda v: v.tensor_copy(out=At_sb[:], in_=At_ps[:]),
                           waits=[("sMM", mm_tr2)], inc=("sVE", 1))
            st2["at_ve"] = ve_at
            def em_av(tn, m=m_, hhh=hhh_):
                ins = None
                for h8 in range(8):
                    h = 8 * hhh + h8
                    ob = (h % 2) * 64
                    oc = (h // 2) * 128
                    tn.matmul(out=ps_w2[ob:ob + 64, oc:oc + 128],
                              lhsT=vS_b[:, m * D + h * 64: m * D + (h + 1) * 64],
                              rhs=At_sb[:, h8 * 256: h8 * 256 + 128],
                              start=True, stop=False)
                    ins = tn.matmul(out=ps_w2[ob:ob + 64, oc:oc + 128],
                                    lhsT=vS_b[0:64, (m + 1) * D + h * 64: (m + 1) * D + (h + 1) * 64],
                                    rhs=At_sb[0:64, h8 * 256 + 128: h8 * 256 + 256],
                                    start=False, stop=True)
                return ins
            mm_av = P.step("tensor", em_av,
                           waits=[("sVE", ve_at),
                                  ("sVE", st2["ps67_rd"] if hhh_ == 0 else 0)],
                           inc=("sMM", 1))
            if hhh_ == 1:
                ve_we = P.step("vector", lambda v, m=m_: v.tensor_copy(
                    out=win_f.rearrange("p (j q) -> p j q", j=DC)[:, :, 128 * m:128 * (m + 1)],
                    in_=ps_w2.rearrange("p (j q) -> p j q", q=128)),
                    waits=[("sMM", mm_av),
                           ("sST", st2.get("winf_st", 0) if m_ == 0 else 0)],
                    inc=("sVE", 1))
                st2["ps67_rd"] = ve_we
        for m in range(4):
            m_global = 4 * t + m
            for hhh in range(2):
                def em_sc(tn, m=m, hhh=hhh):
                    ins = None
                    for h8 in range(8):
                        h = 8 * hhh + h8
                        dj, base = h // 2, (h % 2) * 64
                        ins = tn.matmul(
                            out=ps_s_head(h8),
                            lhsT=q_fm2[base:base + 64, dj * T2 + 128 * m: dj * T2 + 128 * (m + 1)],
                            rhs=k_fm2[base:base + 64, dj * 640 + 128 * m + 32: dj * 640 + 128 * m + 224],
                            start=True, stop=True)
                    return ins
                w_sc = [("sVE", qkv_ve), ("sVE", st2["ps03_free"] if (m, hhh) != (0, 0) else 0)] \
                    if False else [("sVE", qkv_ve), ("sVE", st2.get("badd_ve", 0))]
                mm_sc = P.step("tensor", em_sc, waits=w_sc, inc=("sMM", 1))
                if pend is not None:
                    plan_trav(pend)
                def em_badd(v, m_global=m_global, hhh=hhh):
                    edge = m_global in (0, 31)
                    scrv = scr_sm.rearrange("p (g e w) -> p e g w", e=2, w=192)
                    bv = B_cat[:, hhh * 1536:(hhh + 1) * 1536] \
                        .rearrange("p (g e w) -> p e g w", e=2, w=192)
                    guard(v, v.tensor_tensor(out=scrv[:, 0:1], in0=ps_s4[:, 0:1],
                                             in1=bv[:, 0:1], op=ALU.add), "sCV")
                    ins = v.tensor_tensor(out=scrv[:, 1:2], in0=ps_s4[:, 1:2],
                                          in1=bv[:, 1:2], op=ALU.add)
                    if edge:
                        guard(v, ins, "sCV")
                        ins = v.tensor_tensor(out=scr_sm[:], in0=scr_sm[:],
                                              in1=lm_cat[:] if m_global == 0 else rm_cat[:],
                                              op=ALU.add)
                    return ins
                badd = P.step("vector", em_badd,
                              waits=[("sMM", mm_sc), ("sSE", st2.get("exp_se", 0))],
                              inc=("sVE", 1))
                st2["badd_ve"] = badd
                nmv = P.step("vector", lambda v, hhh=hhh: v.reduce_max(
                    out=nm_t[:, hhh * 8:(hhh + 1) * 8],
                    in_=scr_sm.rearrange("p (h w) -> p h w", w=192), axis=AX.X,
                    negate=True), inc=("sVE", 1))
                def em_exp(s, hhh=hhh):
                    ins = None
                    for h8 in range(8):
                        h = 8 * hhh + h8
                        ins = s.activation(out=A_bf[:, h * 192:(h + 1) * 192],
                                           in_=scr_sm[:, h8 * 192:(h8 + 1) * 192],
                                           func=AF.Exp, bias=nm_t[:, h:h + 1],
                                           accum_out=rs_t[:, h:h + 1])
                    return ins
                se_exp2 = P.step("scalar", em_exp, waits=[("sVE", nmv)], inc=("sSE", 1))
                st2["exp_se"] = se_exp2
                def em_scale(v, hhh=hhh):
                    guard(v, v.reciprocal(out=rr_t[:, hhh * 8:(hhh + 1) * 8],
                                          in_=rs_t[:, hhh * 8:(hhh + 1) * 8]), "sCV")
                    ins = None
                    for h8 in range(8):
                        h = 8 * hhh + h8
                        ins = v.tensor_scalar(out=A2_bf[:, h * 192:(h + 1) * 192],
                                              in0=A_bf[:, h * 192:(h + 1) * 192],
                                              scalar1=rr_t[:, h:h + 1], scalar2=None,
                                              op0=ALU.mult)
                    return ins
                ve_sc2 = P.step("vector", em_scale, waits=[("sSE", se_exp2)], inc=("sVE", 1))
                pend = (m, hhh, ve_sc2)
        plan_trav(pend)
        st2["ps01_rd"] = badd
        st2["ps23_rd"] = badd
        st2["ps45_rd"] = st2["at_ve"]
        st2["qk_ve"] = badd

        ve_wb = P.step("vector", lambda v: v.tensor_copy(out=win_b[:], in_=win_f[:]),
                       waits=[("sVE", st2["ps67_rd"]), ("sST", st2["win_st"].get(t - 1, 0))],
                       inc=("sVE", 1))
        t0 = T2 * t
        sw1 = P.step("sync", lambda e, t0=t0: e.dma_start(
            out=winf_scr[:, :, t0:t0 + T2].rearrange("j p t -> p j t"),
            in_=win_f.rearrange("p (j t) -> p j t", j=DC)),
            waits=[("sVE", ve_wb)], inc=("sST", 16))
        sw2 = P.step("sync", lambda e, t0=t0: e.dma_start(
            out=win_scr[:, :, t0:t0 + T2].rearrange("j p t -> p j t"),
            in_=win_b.rearrange("p (j t) -> p j t", j=DC)),
            inc=("sST", 16))
        st2["win_st"][t] = sw2
        st2["winf_st"] = sw1

    if DEV:
        d_t = P.step("sync", lambda e: e.dma_start(
            out=dbg["dbg_win"].rearrange("j p t -> (j p) t"),
            in_=winf_scr.rearrange("j p t -> (j p) t")),
            waits=[("sST", P.val("sST"))], inc=("sOUT", 16))


    PH2_VE = P.val("sVE"); PH2_SE = P.val("sSE"); PH2_MM = P.val("sMM")
    PH2_ST = P.val("sST"); PH2_OUT = P.val("sOUT")
    BAR2 = [("sVE", PH2_VE), ("sSE", PH2_SE), ("sMM", PH2_MM), ("sST", PH2_ST)]

    T3 = 256
    NT3 = SH // T3
    lin_f = af(0, 2048)
    winf3 = af(2048, 2048)
    xn3 = af(4096, 2048)
    s_tm = af(6144, 2048)
    w_ql3 = afb(8192, 4096)
    w_g2 = afb(12288, 4096)
    w_o = afb(16384, 4096)
    elu3a = af(20480, 1024)
    elu3b = af(21504, 1024)
    T_scr = af(22528, 256)
    D_scr = af(22784, 256)
    P_scr = af(23040, 256)
    dr_bc = af(23296, 256)
    dn_raw = af(23552, 256)
    dn_rcp = af(23808, 128)
    bgh = af(23936, 16)
    w_g1 = ab(0, 24576)
    h3_bufs = [ab(24576, 2048), ab(36864, 2048), ab(38912, 2048)]
    q_e3 = ab(26624, 2048)
    win_b3 = ab(28672, 2048)
    lin_b3 = ab(30720, 2048)
    rg3 = ab(32768, 2048)
    mix3 = ab(34816, 2048)
    ps3_q = [PS[:, 0:256], PS[:, 512:768]]
    ps3_n = [PS[:, 1024:1280], PS[:, 1536:1792]]
    ps3_dn = PS[:, 2048:2304]
    ps3_g1 = [PS[:, 2560:2816], PS[:, 2304:2560]]
    ps3_g2 = [PS[:, 1024:1280], PS[:, 1536:1792]]
    ps3_s = PS[:, 3072:4096]

    d_dn = nc.dram_tensor("d_dn", [1, 256], F32, kind="Internal").ap()
    if DEV:
        lin_scr = nc.dram_tensor("lin_scr", [DC, 128, SH], F32, kind="Internal").ap()

    wt3 = P.step("gpsimd", em_wload(w_ql3, "wq_lin"), waits=BAR2, inc=("sWT", 16))
    wt3 = P.step("gpsimd", em_wload(w_g2, "wg2"), inc=("sWT", 16))
    wt3 = P.step("gpsimd", em_wload(w_o, "wo"), inc=("sWT", 16))
    wt3 = P.step("gpsimd", em_wload(w_g1, "wg1"), inc=("sWT", 16))
    PH3_WT = wt3
    ve_bgh = P.step("vector", lambda v: v.tensor_scalar(
        out=bgh[:, 0:DC], in0=c_bias[:, BIDX["bg2"] * DC:(BIDX["bg2"] + 1) * DC],
        scalar1=0.5, scalar2=None, op0=ALU.mult), waits=BAR2, inc=("sVE", 1))

    st3 = {"h_rd": 0, "win_rd": 0, "xn_rd": 0, "dn_rd": 0, "s_st": 0, "lin_rd": 0,
           "h_rd_hist": {}}

    XH3 = P.counts.get("sXH", 0)
    def plan_ldh3(t):
        h3_ = h3_bufs[t % 3]
        P0_ = 128 + T3 * t
        w_ = list(BAR2) + ([("sMM", st3["h_rd_hist"].get(t - 3, 0))] if t >= 3 else [])
        if t > 0:
            w_.append(("sXH", XH3 + 16 * t))
        P.step("sync", lambda e, P0=P0_, h3=h3_: e.dma_start(
            out=h3.rearrange("p (j t) -> p j t", j=DC),
            in_=h_scr[:, :, P0:P0 + T3].rearrange("j p t -> p j t")),
            waits=w_, inc=("sXH", 16))
    plan_ldh3(0)
    for t in range(NT3):
        P0 = 128 + T3 * t
        t0 = T3 * t
        w0 = list(BAR2)
        h3 = h3_bufs[t % 3]
        if t + 1 < NT3:
            plan_ldh3(t + 1)
        xwh = XH3 + 16 * (t + 1)
        ld_wb = P.step("sync", lambda e, t0=t0: e.dma_start(
            out=win_b3.rearrange("p (j t) -> p j t", j=DC),
            in_=win_scr[:, :, t0:t0 + T3].rearrange("j p t -> p j t")),
            waits=([("sMM", st3["h_rd"])] if t > 0 else []), inc=("sX", 16))
        ld_wf = P.step("sync", lambda e, t0=t0: e.dma_start(
            out=winf3.rearrange("p (j t) -> p j t", j=DC),
            in_=winf_scr[:, :, t0:t0 + T3].rearrange("j p t -> p j t")),
            waits=([("sVE", st3["win_rd"])] if t > 0 else []), inc=("sX", 16))
        ld_xn = P.step("sync", lambda e, t0=t0: e.dma_start(
            out=xn3.rearrange("p (c d) -> p c d", c=2),
            in_=xn_scr[t0:t0 + T3, :].rearrange("(c p) d -> p c d", p=128)),
            waits=([("sVE", st3["xn_rd"])] if t > 0 else []), inc=("sX", 16))
        xw3 = P.counts["sX"]

        for j in range(DC):
            pq = ps3_q[j % 2]
            def em_q3(tn, j=j, pq=pq, h3=h3):
                ins = None
                for di in range(DC):
                    ins = tn.matmul(out=pq[:],
                                    lhsT=w_ql3[:, di * D + j * 128: di * D + (j + 1) * 128],
                                    rhs=h3[:, di * T3:(di + 1) * T3],
                                    start=(di == 0), stop=(di == DC - 1))
                return ins
            mm_q = P.step("tensor", em_q3,
                          waits=[("sXH", xwh), ("sWT", PH3_WT),
                                 ("sVE", st3.get(f"q3ev{j-2}", st3.get("ps01b_rd", 0)))],
                          inc=("sMM", 1))
            def em_emin(v, j=j, pq=pq):
                return v.tensor_scalar(out=elu3a[:, 0:T3], in0=pq[:],
                                       scalar1=bias_col("bq_lin", j), scalar2=0.0,
                                       op0=ALU.add, op1=ALU.min)
            ve_mn = P.step("vector", em_emin, waits=[("sMM", mm_q)], inc=("sVE", 1))
            se_ex = P.step("scalar", lambda s: s.activation(
                out=elu3a[:, 0:T3], in_=elu3a[:, 0:T3], func=AF.Exp),
                waits=[("sVE", ve_mn)], inc=("sSE", 1))
            def em_emax(v, j=j, pq=pq):
                guard(v, v.tensor_scalar(out=elu3b[:, 0:T3], in0=pq[:],
                                         scalar1=bias_col("bq_lin", j), scalar2=0.0,
                                         op0=ALU.add, op1=ALU.max), "sCV")
                return v.tensor_tensor(out=q_e3[:, j * T3:(j + 1) * T3],
                                       in0=elu3b[:, 0:T3], in1=elu3a[:, 0:T3], op=ALU.add)
            ve_qe = P.step("vector", em_emax, waits=[("sSE", se_ex)], inc=("sVE", 1))
            st3[f"q3ev{j}"] = ve_qe
        qe_done = ve_qe

        def em_dn(tn):
            ins = None
            for dc in range(DC):
                ins = tn.matmul(out=ps3_dn[0:1, :],
                                lhsT=ksum_bf[:, dc:dc + 1],
                                rhs=q_e3[:, dc * T3:(dc + 1) * T3],
                                start=(dc == 0), stop=(dc == DC - 1))
            return ins
        mm_dn = P.step("tensor", em_dn,
                       waits=[("sVE", qe_done), ("sVE", st3["dn_rd"])], inc=("sMM", 1))
        def em_dnr(v):
            guard(v, v.tensor_scalar(out=dn_raw[0:1, 0:T3], in0=ps3_dn[0:1, :],
                                     scalar1=1e-6, scalar2=None, op0=ALU.add), "sCV")
            return v.reciprocal(out=dn_rcp[0:1, 0:T3//2] if False else dn_raw[1:2, 0:T3] if False else dn_rcp[0:1, 0:128],
                                in_=dn_raw[0:1, 0:128])
        def em_dnr(v):
            guard(v, v.tensor_scalar(out=dn_raw[0:1, 0:T3], in0=ps3_dn[0:1, :],
                                     scalar1=1e-6, scalar2=None, op0=ALU.add), "sCV")
            return v.reciprocal(out=dr_bc[0:1, 0:T3], in_=dn_raw[0:1, 0:T3])
        ve_dn = P.step("vector", em_dnr, waits=[("sMM", mm_dn)], inc=("sVE", 1))
        st3["dn_rd"] = ve_dn
        dm_dn = P.step("sync", lambda e: e.dma_start(out=d_dn, in_=dr_bc[0:1, 0:T3]),
                       waits=[("sVE", ve_dn)], inc=("sDM", 16))
        dm_db = P.step("sync", lambda e: e.dma_start(
            out=dr_bc[:], in_=d_dn.broadcast_to((128, T3))),
            waits=[("sDM", dm_dn)], inc=("sDM", 16))

        for ec in range(DC):
            pn = ps3_n[ec % 2]
            def em_n3(tn, ec=ec, pn=pn):
                ins = None
                for dc in range(DC):
                    ins = tn.matmul(out=pn[:],
                                    lhsT=kv_bf[:, dc * D + ec * 128: dc * D + (ec + 1) * 128],
                                    rhs=q_e3[:, dc * T3:(dc + 1) * T3],
                                    start=(dc == 0), stop=(dc == DC - 1))
                return ins
            mm_n = P.step("tensor", em_n3,
                          waits=[("sVE", qe_done),
                                 ("sVE", st3.get(f"l3ev{ec-2}", st3.get("ps23b_rd", 0))),
                                 ("sSE", st3.get("g2T_se", 0))],
                          inc=("sMM", 1))
            def em_lin(v, ec=ec, pn=pn):
                guard(v, v.tensor_tensor(out=lin_f[:, ec * T3:(ec + 1) * T3],
                                         in0=pn[:], in1=dr_bc[:], op=ALU.mult), "sCV")
                return v.tensor_copy(out=lin_b3[:, ec * T3:(ec + 1) * T3],
                                     in_=lin_f[:, ec * T3:(ec + 1) * T3])
            ve_l = P.step("vector", em_lin,
                          waits=[("sMM", mm_n), ("sDM", dm_db),
                                 ("sVE", st3["lin_rd"] if ec == 0 else 0)],
                          inc=("sVE", 1))
            st3[f"l3ev{ec}"] = ve_l
        lin_done = ve_l

        for j in range(DC):
            pg = ps3_g1[j % 2]
            def em_g1(tn, j=j, pg=pg, h3=h3):
                ins = None
                srcs = [lin_b3, win_b3, h3]
                for rc in range(24):
                    ins = tn.matmul(out=pg[:],
                                    lhsT=w_g1[:, rc * D + j * 128: rc * D + (j + 1) * 128],
                                    rhs=srcs[rc // 8][:, (rc % 8) * T3:(rc % 8 + 1) * T3],
                                    start=(rc == 0), stop=(rc == 23))
                return ins
            mm_g1 = P.step("tensor", em_g1,
                           waits=[("sVE", lin_done), ("sX", xw3),
                                  ("sVE", st3.get(f"g1ev{j-2}", st3.get("ps45b_rd", 0)))],
                           inc=("sMM", 1))
            ve_g1 = P.step("vector", lambda v, j=j, pg=pg: v.tensor_scalar(
                out=rg3[:, j * T3:(j + 1) * T3], in0=pg[:],
                scalar1=bias_col("bg1", j), scalar2=0.0, op0=ALU.add, op1=ALU.max),
                waits=[("sMM", mm_g1)], inc=("sVE", 1))
            st3[f"g1ev{j}"] = ve_g1
        st3["h_rd"] = mm_g1
        st3["h_rd_hist"][t] = mm_g1

        for j in range(DC):
            pg2 = ps3_g2[j % 2]
            def em_g2(tn, j=j, pg2=pg2):
                ins = None
                for di in range(DC):
                    ins = tn.matmul(out=pg2[:],
                                    lhsT=w_g2[:, di * D + j * 128: di * D + (j + 1) * 128],
                                    rhs=rg3[:, di * T3:(di + 1) * T3],
                                    start=(di == 0), stop=(di == DC - 1))
                return ins
            mm_g2 = P.step("tensor", em_g2,
                           waits=[("sVE", st3[f"g1ev{DC-1}"]),
                                  ("sVE", st3.get(f"mxev{j-2}", 0)),
                                  ("sVE", st3.get(f"l3ev{j}", 0) if t == 0 else 0)],
                           inc=("sMM", 1))
            se_T = P.step("scalar", lambda s, j=j, pg2=pg2: s.activation(
                out=T_scr[:, 0:T3], in_=pg2[:], func=AF.Tanh,
                bias=bgh[:, j:j + 1], scale=0.5),
                waits=[("sMM", mm_g2), ("sVE", ve_bgh),
                       ("sVE", st3.get(f"mxev{j-1}", 0))],
                inc=("sSE", 1))
            st3["g2T_se"] = se_T
            def em_mix(v, j=j):
                guard(v, v.tensor_tensor(out=D_scr[:, 0:T3],
                                         in0=lin_f[:, j * T3:(j + 1) * T3],
                                         in1=winf3[:, j * T3:(j + 1) * T3],
                                         op=ALU.subtract), "sCV")
                guard(v, v.tensor_tensor(out=P_scr[:, 0:T3], in0=T_scr[:, 0:T3],
                                         in1=D_scr[:, 0:T3], op=ALU.mult), "sCV")
                guard(v, v.tensor_tensor(out=D_scr[:, 0:T3], in0=D_scr[:, 0:T3],
                                         in1=P_scr[:, 0:T3], op=ALU.add), "sCV")
                return v.scalar_tensor_tensor(out=mix3[:, j * T3:(j + 1) * T3],
                                              in0=D_scr[:, 0:T3], scalar=0.5,
                                              in1=winf3[:, j * T3:(j + 1) * T3],
                                              op0=ALU.mult, op1=ALU.add)
            ve_mx = P.step("vector", em_mix, waits=[("sSE", se_T)], inc=("sVE", 1))
            st3[f"mxev{j}"] = ve_mx
        st3["win_rd"] = ve_mx

        for tc in range(2):
            def em_s(tn, tc=tc):
                ins = None
                for di in range(DC):
                    for half in range(2):
                        ins = tn.matmul(
                            out=ps3_s[:, half * 512:(half + 1) * 512],
                            lhsT=mix3[:, di * T3 + tc * 128: di * T3 + tc * 128 + 128],
                            rhs=w_o[:, di * D + half * 512: di * D + (half + 1) * 512],
                            start=(di == 0), stop=(di == DC - 1))
                return ins
            mm_s = P.step("tensor", em_s,
                          waits=[("sVE", st3[f"mxev{DC-1}"]),
                                 ("sVE", st3.get("s_ev", 0))],
                          inc=("sMM", 1))
            def em_sev(v, tc=tc):
                guard(v, v.tensor_tensor(out=s_tm[:, tc * D:(tc + 1) * D],
                                         in0=ps3_s[:], in1=c_bo_bc[:], op=ALU.add), "sCV")
                return v.tensor_tensor(out=s_tm[:, tc * D:(tc + 1) * D],
                                       in0=s_tm[:, tc * D:(tc + 1) * D],
                                       in1=xn3[:, tc * D:(tc + 1) * D], op=ALU.add)
            ve_s = P.step("vector", em_sev,
                          waits=[("sMM", mm_s), ("sST", st3["s_st"] if tc == 0 else 0)],
                          inc=("sVE", 1))
            st3["s_ev"] = ve_s
        st3["xn_rd"] = ve_s
        st3["ps01b_rd"] = st3[f"q3ev{DC-1}"]
        st3["ps23b_rd"] = st3[f"mxev{DC-1}"]
        st3["ps45b_rd"] = st3[f"g1ev{DC-1}"]
        st3["lin_rd"] = st3[f"mxev{DC-1}"]

        st_s = P.step("sync", lambda e, t0=t0: e.dma_start(
            out=s_scr[t0:t0 + T3, :].rearrange("(c p) d -> p c d", p=128),
            in_=s_tm.rearrange("p (c d) -> p c d", c=2)),
            waits=[("sVE", ve_s)], inc=("sST", 16))
        st3["s_st"] = st_s
        if DEV:
            st_l = P.step("sync", lambda e, t0=t0: e.dma_start(
                out=lin_scr[:, :, t0:t0 + T3].rearrange("j p t -> p j t"),
                in_=lin_f.rearrange("p (j t) -> p j t", j=DC)),
                waits=[("sVE", lin_done)], inc=("sST", 16))

    if DEV:
        d_t = P.step("sync", lambda e: e.dma_start(
            out=dbg["dbg_lin"].rearrange("j p t -> (j p) t"),
            in_=lin_scr.rearrange("j p t -> (j p) t")),
            waits=[("sST", P.val("sST"))], inc=("sOUT", 16))
        d_t = P.step("sync", lambda e: e.dma_start(out=dbg["dbg_s"], in_=s_scr),
                     waits=[("sST", P.val("sST"))], inc=("sOUT", 16))


    PH3_VE = P.val("sVE"); PH3_SE = P.val("sSE"); PH3_MM = P.val("sMM")
    PH3_ST = P.val("sST")
    BAR3 = [("sVE", PH3_VE), ("sSE", PH3_SE), ("sMM", PH3_MM), ("sST", PH3_ST)]

    T4 = 1024
    NP4 = SH // T4
    CAP = 256
    NSLOT = NE * CAP
    IBLK = 256
    NIB = DI // IBLK
    SENT = 10000000

    sc1 = af(8192, 1024)
    sc2 = af(9216, 1024)
    sc3 = af(10240, 1024)
    silu4 = af(11264, 1024)
    sc_t4 = [silu4[:, 0:256], silu4[:, 512:768]]
    sc_h4 = [silu4[:, 256:512], silu4[:, 768:1024]]
    rc_tm = af(12288, 2048)
    rc_fm = af(14336, 2048)
    sd_stage = af(12288, 4096)
    smb = 16384
    lg4 = af(smb, 64)
    r_mx = af(smb + 64, 8); r_w = af(smb + 72, 8); r_eq = af(smb + 80, 8)
    r_t = af(smb + 88, 8); r_u = af(smb + 96, 8); r_v = af(smb + 104, 8)
    r_oh = af(smb + 112, 8)
    w_all4 = af(smb + 128, 64)
    oh_f = af(smb + 192, 64)
    oh_b = afb(smb + 256, 32)
    pscan_sb = af(smb + 288, 64)
    cs_row = af(smb + 352, 64)
    base_bc = af(smb + 416, 64)
    g_t1 = af(smb + 480, 64)
    g_t2 = af(smb + 544, 64)
    e2c = af(smb + 608, 64)
    gidx_f = af(smb + 672, 8)
    w_tok = af(smb + 680, 8)
    gidx_i = af(smb + 688, 8).bitcast(I32)
    tid_i = af(smb + 696, 8).bitcast(I32)
    wslot_sb = af(smb + 704, 16)
    tos_sb = af(smb + 720, 16).bitcast(I32)
    cs_ex = af(smb + 736, 64)
    cs_a = af(smb + 800, 64)
    cs_b = af(smb + 864, 64)
    srow = af(17408, 1024)
    x_c = af(18432, 1024)
    s_c = af(19456, 1024)
    xn_c = af(20480, 1024)
    acc_tc = [af(21504, 1024), af(22528, 1024)]
    shB = af(23552, 512)
    nrm4 = ab(0, 8192)
    xg_bufs = [ab(28736, 2048), ab(30784, 2048)]
    xg_fm = ab(10240, 2048)
    hm4 = ab(12288, 4096)
    hm_sh = ab(8192, 8192)
    g_sb4 = [ab(16384, 2048), ab(18432, 2048)]
    u_sb4 = [ab(20480, 2048), ab(22528, 2048)]
    d_sb4 = [ab(24576, 2048), ab(26624, 2048)]
    rw_b = ab(28672, 64)
    lg_ps = PS[:, 2048:2056]
    pp_ps = PS[:, 3584:3648]
    cs_ps = PS[:, 3648:3712]
    ps_yg = [PS[:, 2048:2304], PS[:, 2560:2816]]
    ps_yu = [PS[:, 3072:3328], PS[:, 3584:3840]]
    tr_ps2 = [PS[:, 2304:2560].bitcast(BF16), PS[:, 2816:3072].bitcast(BF16)]
    dn_tm = [PS[:, 0:1024], PS[:, 1024:2048]]
    sh_g = [PS[:, 0:1024], PS[:, 2048:3072]]
    sh_u = [PS[:, 1024:2048], PS[:, 3072:4096]]
    trS_ps = PS[:, 0:1024]

    tos_scr = nc.dram_tensor("tos_scr", [NSLOT, 1], I32, kind="Internal").ap()
    wslot_scr = nc.dram_tensor("wslot_scr", [NSLOT, 1], F32, kind="Internal").ap()
    rc_scr = nc.dram_tensor("rc_scr", [NSLOT, D], F32, kind="Internal").ap()
    if DEV:
        moe2_scr = nc.dram_tensor("moe2_scr", [SH, D], F32, kind="Internal").ap()

    wt4 = P.step("gpsimd", lambda g: g.dma_start(
        out=rw_b.rearrange("p (i n) -> p i n", i=DC),
        in_=ap["router_w"].rearrange("(i p) n -> p i n", p=128)),
        waits=BAR3, inc=("sWT", 16))
    PH4_WT0 = wt4
    def em_ltmat(g):
        guard(g, g.iota(
            out=c_lt[:], pattern=[[1, 128]], base=0, channel_multiplier=-1,
            allow_small_or_imprecise_dtypes=True), "sCG")
        return g.tensor_single_scalar(
            out=c_lt[:], in_=c_lt[:], scalar=0.0, op=ALU.is_ge)
    gp_lt = P.step("gpsimd", em_ltmat, waits=BAR3, inc=("sGP", 1))
    def em_e2tile(g):
        guard(g, g.iota(
            out=e2c[:, 0:8], pattern=[[2, 8]], base=0, channel_multiplier=0,
            allow_small_or_imprecise_dtypes=True), "sCG")
        ins = None
        for t_ in range(1, 8):
            ins = g.tensor_copy(out=e2c[:, t_ * 8:(t_ + 1) * 8], in_=e2c[:, 0:8])
        return ins
    gp_e2 = P.step("gpsimd", em_e2tile, inc=("sGP", 1))
    ve_sro = P.step("vector", lambda v: v.memset(srow[0:1, :].bitcast(I32), SENT),
                    waits=BAR3, inc=("sVE", 1))
    ve_srz = P.step("vector", lambda v: v.memset(srow[1:2, :] if False else sc1[0:1, :], 0.0),
                    inc=("sVE", 1))
    dm_wz = P.step("sync", lambda e: e.dma_start(
        out=wslot_scr.rearrange("a o -> o a")[:, 0:1024], in_=sc1[0:1, :]),
        waits=[("sVE", ve_srz)] + BAR3, inc=("sDM", 16))
    dm_wz = P.step("sync", lambda e: e.dma_start(
        out=wslot_scr.rearrange("a o -> o a")[:, 1024:2048], in_=sc1[0:1, :]),
        inc=("sDM", 16))
    PH4_DM0 = dm_wz

    st4 = {"nrm_rd": 0, "lg_rd": 0, "acc_last": 0, "gu_rd": {}, "d_rd": {},
           "xg_rd_par": {}, "stA": {},
           "y_st": 0, "fin_ve": 0, "fin_se": 0, "idx_rd": 0, "tosld": 0,
           "rc_st": 0, "xg_rd": 0, "hm_rd": 0, "sd_ve": 0, "shg_rd": {}}

    for p4 in range(NP4):
        t0 = T4 * p4
        ld_n = P.step("sync", lambda e, t0=t0: e.dma_start(
            out=nrm4.rearrange("p (j t) -> p j t", j=DC),
            in_=nrm_scr[:, :, t0:t0 + T4].rearrange("j p t -> p j t")),
            waits=BAR3 + ([("sMM", st4["nrm_rd"])] if p4 > 0 else []), inc=("sX", 16))
        xw4 = P.counts["sX"]

        for tc in range(8):
            def em_rt(tn, tc=tc):
                ins = None
                for di in range(DC):
                    ins = tn.matmul(out=lg_ps[:],
                                    lhsT=nrm4[:, di * T4 + tc * 128: di * T4 + tc * 128 + 128],
                                    rhs=rw_b[:, di * 8:(di + 1) * 8],
                                    start=(di == 0), stop=(di == DC - 1))
                return ins
            mm_rt = P.step("tensor", em_rt,
                           waits=[("sX", xw4), ("sWT", PH4_WT0), ("sVE", st4["lg_rd"]),
                                  ("sVE", st4["acc_last"])],
                           inc=("sMM", 1))
            def em_route(v, tc=tc):
                guard(v, v.tensor_copy(out=lg4[:, tc * 8:(tc + 1) * 8], in_=lg_ps[:]), "sCV")
                guard(v, v.reduce_max(out=r_mx[:, 0:1], in_=lg4[:, tc * 8:(tc + 1) * 8],
                                      axis=AX.X), "sCV")
                guard(v, v.tensor_scalar(out=r_eq[:, 0:8], in0=lg4[:, tc * 8:(tc + 1) * 8],
                                         scalar1=r_mx[:, 0:1], scalar2=None,
                                         op0=ALU.is_equal), "sCV")
                guard(v, v.tensor_copy(out=r_t[:, 0:1], in_=r_eq[:, 0:1]), "sCV")
                guard(v, v.tensor_tensor(out=r_t[:, 1:8], in0=r_eq[:, 1:8],
                                         in1=r_eq[:, 0:7], op=ALU.add), "sCV")
                guard(v, v.tensor_copy(out=r_u[:, 0:2], in_=r_t[:, 0:2]), "sCV")
                guard(v, v.tensor_tensor(out=r_u[:, 2:8], in0=r_t[:, 2:8],
                                         in1=r_t[:, 0:6], op=ALU.add), "sCV")
                guard(v, v.tensor_copy(out=r_v[:, 0:4], in_=r_u[:, 0:4]), "sCV")
                guard(v, v.tensor_tensor(out=r_v[:, 4:8], in0=r_u[:, 4:8],
                                         in1=r_u[:, 0:4], op=ALU.add), "sCV")
                guard(v, v.tensor_scalar(out=r_v[:, 0:8], in0=r_v[:, 0:8],
                                         scalar1=1.0, scalar2=None, op0=ALU.is_equal), "sCV")
                guard(v, v.tensor_tensor(out=r_oh[:, 0:8], in0=r_v[:, 0:8],
                                         in1=r_eq[:, 0:8], op=ALU.mult), "sCV")
                guard(v, v.tensor_copy(out=oh_f[:, tc * 8:(tc + 1) * 8],
                                       in_=r_oh[:, 0:8]), "sCV")
                return v.tensor_copy(out=oh_b[:, tc * 8:(tc + 1) * 8], in_=r_oh[:, 0:8])
            ve_rt = P.step("vector", em_route, waits=[("sMM", mm_rt)], inc=("sVE", 1))
            st4["lg_rd"] = ve_rt
            se_sg = P.step("scalar", lambda s: s.activation(
                out=r_w[:, 0:1], in_=r_mx[:, 0:1], func=AF.Tanh, scale=0.5),
                waits=[("sVE", ve_rt)], inc=("sSE", 1))
            def em_wsc(v, tc=tc):
                guard(v, v.tensor_scalar(out=r_w[:, 1:2], in0=r_w[:, 0:1],
                                         scalar1=0.5, scalar2=0.5,
                                         op0=ALU.mult, op1=ALU.add), "sCV")
                return v.tensor_scalar(out=w_all4[:, tc * 8:(tc + 1) * 8],
                                       in0=oh_f[:, tc * 8:(tc + 1) * 8],
                                       scalar1=r_w[:, 1:2], scalar2=None, op0=ALU.mult)
            ve_ws = P.step("vector", em_wsc, waits=[("sSE", se_sg)], inc=("sVE", 1))

        def em_pp(tn):
            tn.matmul(out=pp_ps[:], lhsT=c_lt[:], rhs=oh_b[:], start=True, stop=True)
            return tn.matmul(out=cs_ps[0:1, :], lhsT=c_ones[:, 0:1], rhs=oh_b[:],
                             start=True, stop=True)
        mm_pp = P.step("tensor", em_pp,
                       waits=[("sVE", ve_ws), ("sGP", gp_lt)],
                       inc=("sMM", 1))
        def em_gidx(v):
            guard(v, v.tensor_copy(out=pscan_sb[:], in_=pp_ps[:]), "sCV")
            guard(v, v.tensor_copy(out=cs_row[0:1, :], in_=cs_ps[0:1, :]), "sCV")
            guard(v, v.tensor_copy(out=cs_a[0:1, 0:8], in_=cs_row[0:1, 0:8]), "sCV")
            guard(v, v.tensor_tensor(out=cs_a[0:1, 8:64], in0=cs_row[0:1, 8:64],
                                     in1=cs_row[0:1, 0:56], op=ALU.add), "sCV")
            guard(v, v.tensor_copy(out=cs_b[0:1, 0:16], in_=cs_a[0:1, 0:16]), "sCV")
            guard(v, v.tensor_tensor(out=cs_b[0:1, 16:64], in0=cs_a[0:1, 16:64],
                                     in1=cs_a[0:1, 0:48], op=ALU.add), "sCV")
            guard(v, v.tensor_copy(out=cs_row[0:1, 0:32], in_=cs_b[0:1, 0:32]), "sCV")
            guard(v, v.tensor_tensor(out=cs_row[0:1, 32:64], in0=cs_b[0:1, 32:64],
                                     in1=cs_b[0:1, 0:32], op=ALU.add), "sCV")
            return v.tensor_tensor(out=cs_ex[0:1, 0:64], in0=cs_row[0:1, 0:64],
                                   in1=cs_ps[0:1, :], op=ALU.subtract)
        ve_gx0 = P.step("vector", em_gidx, waits=[("sMM", mm_pp)], inc=("sVE", 1))
        dm_ts = P.step("sync", lambda e: e.dma_start(
            out=tos_scr.rearrange("a o -> o a")[:, 0:1024],
            in_=srow[0:1, :].bitcast(I32)),
            waits=[("sVE", ve_sro), ("sST", st4["tosld"])], inc=("sDM", 16))
        dm_ts = P.step("sync", lambda e: e.dma_start(
            out=tos_scr.rearrange("a o -> o a")[:, 1024:2048],
            in_=srow[0:1, :].bitcast(I32)), inc=("sDM", 16))
        dm_bb = P.step("sync", lambda e: e.dma_start(out=d_dn[0:1, 0:64],
                                                     in_=cs_ex[0:1, 0:64]),
                       waits=[("sVE", ve_gx0)], inc=("sDM", 16))
        dm_bb = P.step("sync", lambda e: e.dma_start(
            out=base_bc[:], in_=d_dn[0:1, 0:64].broadcast_to((128, 64))),
            waits=[("sDM", dm_bb)], inc=("sDM", 16))
        def em_gidx2(v):
            guard(v, v.scalar_tensor_tensor(out=g_t1[:], in0=pscan_sb[:], scalar=-1.0,
                                            in1=base_bc[:], op0=ALU.add, op1=ALU.add), "sCV")
            guard(v, v.tensor_single_scalar(out=g_t2[:], in_=g_t1[:], scalar=128.0,
                                            op=ALU.is_ge), "sCV")
            guard(v, v.scalar_tensor_tensor(out=g_t1[:], in0=g_t2[:], scalar=-128.0,
                                            in1=g_t1[:], op0=ALU.mult, op1=ALU.add), "sCV")
            guard(v, v.scalar_tensor_tensor(out=g_t2[:], in0=g_t1[:], scalar=16.0,
                                            in1=g_t2[:], op0=ALU.mult, op1=ALU.add), "sCV")
            guard(v, v.tensor_tensor(out=g_t2[:], in0=g_t2[:], in1=e2c[:],
                                     op=ALU.add), "sCV")
            guard(v, v.tensor_tensor(out=g_t2[:], in0=g_t2[:], in1=oh_f[:],
                                     op=ALU.mult), "sCV")
            guard(v, v.reduce_sum(out=gidx_f[:], in_=g_t2[:]
                                  .rearrange("p (t e) -> p t e", e=8), axis=AX.X), "sCV")
            guard(v, v.tensor_copy(out=gidx_i[:], in_=gidx_f[:]), "sCV")
            return v.reduce_sum(out=w_tok[:], in_=w_all4[:]
                                .rearrange("p (t e) -> p t e", e=8), axis=AX.X)
        if DEV and p4 == 0:
            pass
        ve_gx = P.step("vector", em_gidx2,
                       waits=[("sDM", dm_bb), ("sGP", gp_e2),
                              ("sWT", st4.get("ga_last", 0)),
                              ("sWT", st4.get("scat_last", 0))], inc=("sVE", 1))
        if DEV and p4 == 0:
            P.step("sync", lambda e: e.dma_start(out=dbg["dbg_route"][:, 0:8], in_=gidx_f[:]),
                   waits=[("sVE", ve_gx)], inc=("sOUT", 16))
            P.step("sync", lambda e: e.dma_start(out=dbg["dbg_route"][:, 8:16], in_=w_tok[:]),
                   inc=("sOUT", 16))
            P.step("sync", lambda e: e.dma_start(out=dbg["dbg_route"][:, 16:24],
                                                 in_=lg4[:, 0:8]), inc=("sOUT", 16))
            P.step("sync", lambda e: e.dma_start(out=dbg["dbg_route"][:, 24:32],
                                                 in_=oh_f[:, 0:8]), inc=("sOUT", 16))
        gp_tid = P.step("gpsimd", lambda g, t0=t0: g.iota(
            out=tid_i[:], pattern=[[128, 8]], base=t0, channel_multiplier=1),
            waits=[("sWT", st4.get("scat_last", 0))], inc=("sGP", 1))
        sc_t = dm_bb
        for tc in range(8):
            sc_t = P.step("gpsimd", lambda g, tc=tc: g.indirect_dma_start(
                out=tos_scr, out_offset=IndirectOffsetOnAxis(ap=gidx_i[:, tc:tc + 1], axis=0),
                in_=tid_i[:, tc:tc + 1], in_offset=None),
                waits=([("sVE", ve_gx), ("sGP", gp_tid), ("sDM", sc_t)] if tc == 0 else []),
                inc=("sWT", 16))
            sc_t = P.step("gpsimd", lambda g, tc=tc: g.indirect_dma_start(
                out=wslot_scr, out_offset=IndirectOffsetOnAxis(ap=gidx_i[:, tc:tc + 1], axis=0),
                in_=w_tok[:, tc:tc + 1], in_offset=None),
                waits=([("sDM", PH4_DM0)] if tc == 0 and p4 == 0 else []),
                inc=("sWT", 16))
        ld_tw = P.step("sync", lambda e: e.dma_start(
            out=tos_sb[:].rearrange("p (c o) -> p c o", o=1),
            in_=tos_scr.rearrange("(p c) o -> p c o", c=16)),
            waits=[("sWT", sc_t)], inc=("sST", 16))
        ld_tw = P.step("sync", lambda e: e.dma_start(
            out=wslot_sb[:].rearrange("p (c o) -> p c o", o=1),
            in_=wslot_scr.rearrange("(p c) o -> p c o", c=16)),
            inc=("sST", 16))
        st4["tosld"] = ld_tw

        def stage_A(e_):
            xge = xg_bufs[e_ % 2]
            ve_z = P.step("vector", lambda v, xge=xge: v.memset(xge[:], 0.0),
                          waits=[("sMM", st4["xg_rd_par"].get(e_ % 2, 0)),
                                 ("sMM", st4["xg_rd"] if e_ == 0 else 0)],
                          inc=("sVE", 1))
            g_g = P.step("gpsimd", lambda g, e_=e_, xge=xge: g.indirect_dma_start(
                out=xge[:, 0:D], out_offset=None, in_=n_tm_scr,
                in_offset=IndirectOffsetOnAxis(ap=tos_sb[:, 2 * e_:2 * e_ + 1], axis=0)),
                waits=[("sST", st4["tosld"]), ("sVE", ve_z)], inc=("sWT", 16))
            g_g = P.step("gpsimd", lambda g, e_=e_, xge=xge: g.indirect_dma_start(
                out=xge[:, D:2 * D], out_offset=None, in_=n_tm_scr,
                in_offset=IndirectOffsetOnAxis(ap=tos_sb[:, 2 * e_ + 1:2 * e_ + 2], axis=0)),
                inc=("sWT", 16))
            def em_xs(v, e_=e_, xge=xge):
                ins = None
                for cc in range(2):
                    guard(v, v.tensor_tensor(out=sc2[:], in0=xge[:, cc * D:(cc + 1) * D],
                                             in1=c_flnw_bc[:], op=ALU.mult), "sCV")
                    guard(v, v.tensor_tensor(out=sc2[:], in0=sc2[:], in1=c_flnb_bc[:],
                                             op=ALU.add), "sCV")
                    ins = v.tensor_scalar(out=xge[:, cc * D:(cc + 1) * D], in0=sc2[:],
                                          scalar1=wslot_sb[:, 2 * e_ + cc:2 * e_ + cc + 1],
                                          scalar2=None, op0=ALU.mult)
                    if cc == 0:
                        guard(v, ins, "sCV")
                return ins
            ve_xs = P.step("vector", em_xs,
                           waits=[("sWT", g_g), ("sVE", st4["fin_ve"]) if e_ == 0 else
                                  ("sVE", 0)],
                           inc=("sVE", 1))
            st4["stA"][e_] = (xge, ve_xs)
        stage_A(0)
        for e_ in range(NE):
            xge, ve_xs = st4["stA"][e_]
            if DEV and p4 == 0 and e_ == 0:
                P.step("sync", lambda e, xge=xge: e.dma_start(out=dbg["dbg_xg"], in_=xge[:]),
                       waits=[("sVE", ve_xs)], inc=("sOUT", 16))
            def em_xt(tn, cc):
                def f(tn_, cc=cc, xge=xge):
                    ins = None
                    for dj in range(DC):
                        tp = tr_ps2[dj // 4]
                        ins = tn_.transpose(
                            out=tp[:, (dj % 4) * 128:(dj % 4 + 1) * 128],
                            in_=xge[:, cc * D + dj * 128: cc * D + (dj + 1) * 128],
                            identity=c_id_bf[:])
                    return ins
                return f
            ve_xf = 0
            for cc in range(2):
                mm_xt = P.step("tensor", em_xt(None, cc),
                               waits=[("sVE", ve_xs), ("sGP", gp_id),
                                      ("sVE", st4.get("trp_rd", 0)),
                                      ("sVE", st4.get("hm_prev", 0)),
                                      ("sVE", st4.get("sd_ve", 0) if e_ == 0 else 0)],
                               inc=("sMM", 1))
                def em_xf(v, cc=cc):
                    xv = xg_fm.rearrange("p (dj s) -> p dj s", s=CAP)
                    guard(v, v.tensor_copy(
                        out=xv[:, 0:4, cc * 128:(cc + 1) * 128],
                        in_=tr_ps2[0].rearrange("p (dj s) -> p dj s", s=128)), "sCV")
                    return v.tensor_copy(
                        out=xv[:, 4:8, cc * 128:(cc + 1) * 128],
                        in_=tr_ps2[1].rearrange("p (dj s) -> p dj s", s=128))
                ve_xf = P.step("vector", em_xf,
                    waits=[("sMM", mm_xt), ("sMM", st4.get("yglast", 0))],
                    inc=("sVE", 1))
                st4["trp_rd"] = ve_xf
            st4["xg_rd_par"][e_ % 2] = mm_xt

            dn_t = {}
            hm_t = {}
            ldd_t = {}
            def plan_dn(j, e_=e_):
                ibd, i2 = j // 2, j % 2
                dpar = ibd % 2
                def em_dnj(tn, j=j, i2=i2, dpar=dpar):
                    ins = None
                    for c in range(2):
                        for hlf in range(2):
                            ins = tn.matmul(
                                out=dn_tm[c][:, hlf * 512:(hlf + 1) * 512],
                                lhsT=hm4[:, j * CAP + c * 128: j * CAP + c * 128 + 128],
                                rhs=d_sb4[dpar][:, i2 * D + hlf * 512: i2 * D + (hlf + 1) * 512],
                                start=(j == 0), stop=(j == 15))
                    return ins
                w = [("sVE", hm_t[j]), ("sWS", ldd_t[ibd])]
                if j == 0:
                    w += [("sVE", st4.get("rcf_rd", 0)),
                          ("sVE", st4.get("sd_ve", 0) if e_ == 0 else 0),
                          ("sVE", st4.get("fintr_rd", 0) if e_ == 0 else 0)]
                dn_t[j] = P.step("tensor", em_dnj, waits=w, inc=("sMM", 1))
                st4["d_rd"][dpar] = dn_t[j]
            for ib in range(NIB):
                par = ib % 2
                def em_ldg(g, e_=e_, ib=ib, par=par):
                    srcw = gwb[e_].rearrange("(i p) n -> p i n", p=128)[
                        :, :, ib * IBLK:(ib + 1) * IBLK]
                    return g.dma_start(out=g_sb4[par].rearrange("p (i n) -> p i n", i=DC),
                                       in_=srcw)
                def em_ldu(g, e_=e_, ib=ib, par=par):
                    srcw = uwb[e_].rearrange("(i p) n -> p i n", p=128)[
                        :, :, ib * IBLK:(ib + 1) * IBLK]
                    return g.dma_start(out=u_sb4[par].rearrange("p (i n) -> p i n", i=DC),
                                       in_=srcw)
                wld = [("sMM", st4["gu_rd"].get(par, 0)), ("sPC", N_PC)]
                if par not in st4["gu_rd"]:
                    wld = wld + BAR3
                ldg = P.step("gpsimd", em_ldg, waits=wld, inc=("sWS", 16))
                ldu = P.step("gpsimd", em_ldu, inc=("sWS", 16))
                def em_ldd(g, e_=e_, ib=ib):
                    srcw = dwb[e_].rearrange("(i p) n -> p i n", p=128)[
                        :, 2 * ib:2 * ib + 2, :]
                    return g.dma_start(out=d_sb4[ib % 2].rearrange("p (i n) -> p i n", i=2),
                                       in_=srcw)
                wldd = [("sMM", st4["d_rd"].get(ib % 2, 0))]
                if ib % 2 not in st4["d_rd"]:
                    wldd = wldd + BAR3
                ldd_t[ib] = P.step("gpsimd", em_ldd, waits=wldd, inc=("sWS", 16))
                for ic in range(IBLK // 128):
                    icg = ib * (IBLK // 128) + ic
                    ipar = icg % 2
                    def em_yg(tn, ic=ic, par=par, ipar=ipar):
                        ins = None
                        for di in range(DC):
                            ins = tn.matmul(
                                out=ps_yg[ipar][:],
                                lhsT=g_sb4[par][:, di * IBLK + ic * 128: di * IBLK + ic * 128 + 128],
                                rhs=xg_fm[:, di * CAP:(di + 1) * CAP],
                                start=(di == 0), stop=(di == DC - 1))
                        return ins
                    def em_yu(tn, ic=ic, par=par, ipar=ipar):
                        ins = None
                        for di in range(DC):
                            ins = tn.matmul(
                                out=ps_yu[ipar][:],
                                lhsT=u_sb4[par][:, di * IBLK + ic * 128: di * IBLK + ic * 128 + 128],
                                rhs=xg_fm[:, di * CAP:(di + 1) * CAP],
                                start=(di == 0), stop=(di == DC - 1))
                        return ins
                    mm_yg = P.step("tensor", em_yg,
                                   waits=[("sWS", ldu), ("sVE", ve_xf),
                                          ("sSE", st4.get(f"ss{ipar}", 0)),
                                          ("sVE", st4.get(f"sv{ipar}", 0))],
                                   inc=("sMM", 1))
                    mm_yu = P.step("tensor", em_yu, inc=("sMM", 1))
                    st4["gu_rd"][par] = mm_yu
                    se_t4 = P.step("scalar", lambda s, ipar=ipar: s.activation(
                        out=sc_t4[ipar], in_=ps_yg[ipar][:], func=AF.Tanh, scale=0.5),
                        waits=[("sMM", mm_yg)], inc=("sSE", 1))
                    st4[f"ss{ipar}"] = se_t4
                    def em_hm(v, icg=icg, ipar=ipar):
                        guard(v, v.scalar_tensor_tensor(
                            out=sc_h4[ipar], in0=sc_t4[ipar], scalar=1.0,
                            in1=ps_yg[ipar][:], op0=ALU.add, op1=ALU.mult), "sCV")
                        return v.scalar_tensor_tensor(
                            out=hm4[:, icg * CAP:(icg + 1) * CAP], in0=sc_h4[ipar],
                            scalar=0.5, in1=ps_yu[ipar][:], op0=ALU.mult, op1=ALU.mult)
                    ve_hm = P.step("vector", em_hm,
                                   waits=[("sSE", se_t4), ("sMM", mm_yu)], inc=("sVE", 1))
                    st4[f"sv{ipar}"] = ve_hm
                    hm_t[icg] = ve_hm
                    if icg == 6 and e_ + 1 < NE:
                        stage_A(e_ + 1)
                    if icg >= 2:
                        plan_dn(icg - 2)
            plan_dn(14)
            plan_dn(15)
            st4["yglast"] = mm_yu
            hm_done = ve_hm
            st4["hm_prev"] = ve_hm
            mm_d = dn_t[15]
            st4["hm_rd"] = mm_d
            if DEV and p4 == 0 and e_ == 0:
                P.step("sync", lambda e: e.dma_start(out=dbg["dbg_hm"], in_=hm4[:]),
                       waits=[("sVE", hm_done)], inc=("sOUT", 16))
            ve_rcf = P.step("vector", lambda v: v.tensor_copy(
                out=rc_tm.rearrange("p (c w) -> p c w", c=2),
                in_=PS[:, 0:2048].rearrange("p (c w) -> p c w", c=2)),
                waits=[("sMM", mm_d), ("sST", st4["rc_st"])], inc=("sVE", 1))
            st4["rcf_rd"] = ve_rcf
            st_rc = P.step("sync", lambda e, e_=e_: e.dma_start(
                out=rc_scr.rearrange("(p c) w -> p c w", c=16)[:, 2 * e_:2 * e_ + 2, :],
                in_=rc_tm.rearrange("p (c w) -> p c w", c=2)),
                waits=[("sVE", ve_rcf)], inc=("sST", 16))
            st4["rc_st"] = st_rc
        st4["scat_last"] = sc_t
        rc_all = st_rc
        if DEV and p4 == 0:
            P.step("sync", lambda e: e.dma_start(out=dbg["dbg_rc"], in_=rc_scr),
                   waits=[("sST", rc_all)], inc=("sOUT", 16))

        for th in range(2):
            for ib in range(NIB):
                par = ib % 2
                def em_lds(g, ib=ib, par=par):
                    srcw = sgb.rearrange("(i p) n -> p i n", p=128)[
                        :, :, ib * IBLK:(ib + 1) * IBLK]
                    return g.dma_start(out=g_sb4[par].rearrange("p (i n) -> p i n", i=DC),
                                       in_=srcw)
                def em_ldsu(g, ib=ib, par=par):
                    srcw = sub_.rearrange("(i p) n -> p i n", p=128)[
                        :, :, ib * IBLK:(ib + 1) * IBLK]
                    return g.dma_start(out=u_sb4[par].rearrange("p (i n) -> p i n", i=DC),
                                       in_=srcw)
                ldg = P.step("gpsimd", em_lds,
                             waits=[("sMM", st4["gu_rd"].get(par, 0))], inc=("sWS", 16))
                ldu = P.step("gpsimd", em_ldsu, inc=("sWS", 16))
                for ic in range(IBLK // 128):
                    icg = ib * (IBLK // 128) + ic
                    ipar = icg % 2
                    def em_syg(tn, ic=ic, par=par, ipar=ipar, th=th):
                        ins = None
                        for di in range(DC):
                            lhsT = g_sb4[par][:, di * IBLK + ic * 128: di * IBLK + ic * 128 + 128]
                            ins = tn.matmul(
                                out=sh_g[ipar][:, 0:512], lhsT=lhsT,
                                rhs=nrm4[:, di * T4 + th * 512: di * T4 + th * 512 + 512],
                                start=(di == 0), stop=(di == DC - 1))
                        return ins
                    def em_syu(tn, ic=ic, par=par, ipar=ipar, th=th):
                        ins = None
                        for di in range(DC):
                            lhsT = u_sb4[par][:, di * IBLK + ic * 128: di * IBLK + ic * 128 + 128]
                            ins = tn.matmul(
                                out=sh_u[ipar][:, 0:512], lhsT=lhsT,
                                rhs=nrm4[:, di * T4 + th * 512: di * T4 + th * 512 + 512],
                                start=(di == 0), stop=(di == DC - 1))
                        return ins
                    mm_yg = P.step("tensor", em_syg,
                                   waits=[("sWS", ldu),
                                          ("sSE", st4.get(f"ss{ipar}", 0)),
                                          ("sVE", st4.get(f"sv{ipar}", 0)),
                                          ("sVE", st4.get("rcf_rd", 0) if ib == 0 and ic == 0 and th == 0 else 0),
                                          ("sVE", st4.get("fintr_rd", 0) if ib == 0 and ic == 0 else 0),
                                          ("sVE", st4.get("sd_ve", 0) if ib == 0 and ic == 0 else 0)],
                                   inc=("sMM", 1))
                    mm_yu = P.step("tensor", em_syu, inc=("sMM", 1))
                    st4["gu_rd"][par] = mm_yu
                    se_t4 = P.step("scalar", lambda s, ipar=ipar: s.activation(
                        out=silu4[:, ipar * 512:ipar * 512 + 512],
                        in_=sh_g[ipar][:, 0:512], func=AF.Tanh, scale=0.5),
                        waits=[("sMM", mm_yg), ("sSE", st4.get("shse", 0))],
                        inc=("sSE", 1))
                    st4[f"ss{ipar}"] = se_t4
                    st4["shse"] = se_t4
                    def em_shm(v, icg=icg, ipar=ipar):
                        guard(v, v.scalar_tensor_tensor(
                            out=shB[:], in0=silu4[:, ipar * 512:ipar * 512 + 512],
                            scalar=1.0, in1=sh_g[ipar][:, 0:512],
                            op0=ALU.add, op1=ALU.mult), "sCV")
                        return v.scalar_tensor_tensor(
                            out=hm_sh[:, icg * 512:(icg + 1) * 512], in0=shB[:],
                            scalar=0.5, in1=sh_u[ipar][:, 0:512],
                            op0=ALU.mult, op1=ALU.mult)
                    ve_hm = P.step("vector", em_shm,
                                   waits=[("sSE", se_t4), ("sMM", mm_yu)], inc=("sVE", 1))
                    st4[f"sv{ipar}"] = ve_hm
            shm_done = ve_hm
            for ib in range(NIB):
                par = ib % 2
                def em_ldsd(g, ib=ib, par=par):
                    srcw = sdb.rearrange("(i p) n -> p i n", p=128)[
                        :, 2 * ib:2 * ib + 2, :]
                    return g.dma_start(out=d_sb4[par].rearrange("p (i n) -> p i n", i=2),
                                       in_=srcw)
                ldd = P.step("gpsimd", em_ldsd,
                             waits=[("sMM", st4["d_rd"].get(par, 0))], inc=("sWS", 16))
                def em_sdn(tn, ib=ib, par=par):
                    ins = None
                    for i2 in range(2):
                        icg = 2 * ib + i2
                        for hc in range(DC):
                            ins = tn.matmul(
                                out=PS[:, hc * 512:(hc + 1) * 512],
                                lhsT=d_sb4[par][:, i2 * D + hc * 128: i2 * D + (hc + 1) * 128],
                                rhs=hm_sh[:, icg * 512:(icg + 1) * 512],
                                start=(icg == 0), stop=(icg == 15))
                    return ins
                w_d = [("sWS", ldd)]
                if ib == 0:
                    w_d += [("sVE", shm_done), ("sSE", st4.get("ss0", 0)),
                            ("sSE", st4.get("ss1", 0)), ("sVE", st4.get("sd_ve", 0)),
                            ("sSE", st4.get("fin_se", 0)), ("sVE", st4["fin_ve"])]
                mm_d = P.step("tensor", em_sdn, waits=w_d, inc=("sMM", 1))
                st4["d_rd"][par] = mm_d
            ve_sd = P.step("vector", lambda v: v.tensor_copy(out=sd_stage[:], in_=PS[:]),
                           waits=[("sMM", mm_d), ("sST", rc_all)], inc=("sVE", 1))
            st4["sd_ve"] = ve_sd
            for tcl in range(4):
                tc = th * 4 + tcl
                cpar = tc % 2
                at = acc_tc[cpar]
                def em_str(tn, tcl=tcl):
                    ins = None
                    for hc in range(DC):
                        ins = tn.transpose(
                            out=trS_ps[:, hc * 128:(hc + 1) * 128],
                            in_=sd_stage[:, hc * 512 + tcl * 128: hc * 512 + tcl * 128 + 128],
                            identity=c_id_f[:])
                    return ins
                mm_st = P.step("tensor", em_str,
                               waits=[("sVE", ve_sd), ("sVE", st4.get("fintr_rd", 0))],
                               inc=("sMM", 1))
                ve_at = P.step("vector", lambda v, at=at: v.tensor_copy(
                    out=at[:], in_=trS_ps[:]),
                    waits=[("sMM", mm_st), ("sWT", st4.get("ga_done", 0))],
                    inc=("sVE", 1))
                st4["fintr_rd"] = ve_at
                ga = P.step("gpsimd", lambda g, tc=tc, at=at: g.indirect_dma_start(
                    out=at[:], out_offset=None, in_=rc_scr,
                    in_offset=IndirectOffsetOnAxis(ap=gidx_i[:, tc:tc + 1], axis=0),
                    compute_op=ALU.add),
                    waits=[("sVE", ve_at), ("sST", rc_all)], inc=("sWT", 16))
                st4["ga_done"] = 0
                ld_x4 = P.step("sync", lambda e, t0=t0, tc=tc: e.dma_start(
                    out=x_c[:], in_=xp[128 + t0 + tc * 128: 128 + t0 + (tc + 1) * 128, :]),
                    waits=[("sVE", st4["fin_ve"])], inc=("sX", 16))
                ld_s4 = P.step("sync", lambda e, t0=t0, tc=tc: e.dma_start(
                    out=s_c[:], in_=s_scr[t0 + tc * 128:t0 + (tc + 1) * 128, :]),
                    inc=("sX", 16))
                ld_xn4 = P.step("sync", lambda e, t0=t0, tc=tc: e.dma_start(
                    out=xn_c[:], in_=xn_scr[t0 + tc * 128:t0 + (tc + 1) * 128, :]),
                    inc=("sX", 16))
                xw5 = P.counts["sX"]
                if DEV:
                    P.step("sync", lambda e, t0=t0, tc=tc, at=at: e.dma_start(
                        out=moe2_scr[t0 + tc * 128:t0 + (tc + 1) * 128, :], in_=at[:]),
                        waits=[("sWT", P.counts["sWT"])], inc=("sST", 16))
                def em_fu(v, at=at):
                    return v.tensor_tensor(out=sc1[:], in0=at[:], in1=xn_c[:], op=ALU.add)
                ve_fu = P.step("vector", em_fu,
                               waits=[("sWT", P.counts["sWT"]), ("sX", xw5),
                                      ("sSE", st4["fin_se"])],
                               inc=("sVE", 1))
                st4["ga_done"] = P.counts["sWT"]
                se_ft = P.step("scalar", lambda s: s.activation(
                    out=sc2[:], in_=sc1[:], func=AF.Tanh, scale=0.5),
                    waits=[("sVE", ve_fu)], inc=("sSE", 1))
                st4["fin_se"] = se_ft
                def em_fy(v):
                    guard(v, v.scalar_tensor_tensor(out=sc3[:], in0=sc2[:], scalar=1.0,
                                                    in1=sc1[:], op0=ALU.add,
                                                    op1=ALU.mult), "sCV")
                    guard(v, v.scalar_tensor_tensor(out=sc3[:], in0=sc3[:], scalar=0.5,
                                                    in1=s_c[:], op0=ALU.mult,
                                                    op1=ALU.mult), "sCV")
                    return v.tensor_tensor(out=sc3[:], in0=sc3[:], in1=x_c[:], op=ALU.add)
                ve_fy = P.step("vector", em_fy,
                               waits=[("sSE", se_ft), ("sOUT", st4["y_st"])],
                               inc=("sVE", 1))
                st4["fin_ve"] = ve_fy
                st_y = P.step("sync", lambda e, t0=t0, tc=tc: e.dma_start(
                    out=y[t0 + tc * 128:t0 + (tc + 1) * 128, :], in_=sc3[:]),
                    waits=[("sVE", ve_fy)], inc=("sOUT", 16))
                st4["y_st"] = st_y
        st4["nrm_rd"] = mm_d
        st4["xg_rd"] = mm_d
        st4["ga_last"] = st4["ga_done"]
        st4["acc_last"] = st4["fin_ve"]
    P.step("sync", lambda e: e.wait_ge(sems["sOUT"], P.val("sOUT")))

    with es:
        block = es.enter_context(nc.Block())
        final_out = P.val("sOUT")

        @block.sync
        def _(e):
            P.emit_for("sync", e, sems)

        @block.vector
        def _(e):
            P.emit_for("vector", e, sems)

        @block.scalar
        def _(e):
            P.emit_for("scalar", e, sems)

        @block.tensor
        def _(e):
            P.emit_for("tensor", e, sems)

        @block.gpsimd
        def _(e):
            P.emit_for("gpsimd", e, sems)

    return nc, dbg


def make_inputs(x_full, weights, core):
    b, hf = core // 2, core % 2
    xb = np.asarray(x_full[b])
    own = xb[hf * SH:(hf + 1) * SH]
    partner = xb[(1 - hf) * SH:(2 - hf) * SH]
    z96 = np.zeros((96, D), np.float32)
    lhalo = np.zeros((32, D), np.float32) if hf == 0 else xb[SH - 32:SH]
    rhalo = xb[SH:SH + 32] if hf == 0 else np.zeros((32, D), np.float32)
    xp = np.concatenate([z96, lhalo, own, rhalo, z96, partner], axis=0)
    assert xp.shape == (NPT, D)
    lmask = np.zeros((1, 192), np.float32)
    rmask = np.zeros((1, 192), np.float32)
    if hf == 0:
        lmask[0, :32] = -1e30
    else:
        rmask[0, 160:] = -1e30
    m = {"xp": np.ascontiguousarray(xp), "lmask": lmask, "rmask": rmask}
    m.update(weights)
    return m


def kernel(**inputs):
    x = inputs["L0_attn_x"]
    w = {
        "wq_lin": inputs["L0_attn_wq_lin"], "bq_lin": inputs["L0_attn_bq_lin"].reshape(1, D),
        "wk_lin": inputs["L0_attn_wk_lin"], "bk_lin": inputs["L0_attn_bk_lin"].reshape(1, D),
        "wv_lin": inputs["L0_attn_wv_lin"], "bv_lin": inputs["L0_attn_bv_lin"].reshape(1, D),
        "wq_loc": inputs["L0_attn_wq_loc"], "bq_loc": inputs["L0_attn_bq_loc"].reshape(1, D),
        "wk_loc": inputs["L0_attn_wk_loc"], "bk_loc": inputs["L0_attn_bk_loc"].reshape(1, D),
        "wv_loc": inputs["L0_attn_wv_loc"], "bv_loc": inputs["L0_attn_bv_loc"].reshape(1, D),
        "wo": inputs["L0_attn_wo"], "bo": inputs["L0_attn_bo"].reshape(1, D),
        "wg1": inputs["L0_attn_wg1"], "bg1": inputs["L0_attn_bg1"].reshape(1, D),
        "wg2": inputs["L0_attn_wg2"], "bg2": inputs["L0_attn_bg2"].reshape(1, D),
        "rel_bias": inputs["L0_attn_rel_bias"],
        "ln_g": inputs["L0_attn_ln_g"].reshape(1, D), "ln_b": inputs["L0_attn_ln_b"].reshape(1, D),
        "fln_w": inputs["L0_ffn_ln_w"].reshape(1, D), "fln_b": inputs["L0_ffn_ln_b"].reshape(1, D),
        "router_w": inputs["L0_ffn_router_w"],
        "gate_w": inputs["L0_ffn_gate_w"], "up_w": inputs["L0_ffn_up_w"],
        "down_w": inputs["L0_ffn_down_w"],
        "sh_gate": inputs["L0_ffn_sh_gate"], "sh_up": inputs["L0_ffn_sh_up"],
        "sh_down": inputs["L0_ffn_sh_down"],
    }
    w = {k: np.ascontiguousarray(v, dtype=np.float32) for k, v in w.items()}
    nc, _ = build_program()
    in_maps = [make_inputs(x, w, c) for c in range(8)]
    res = run_bass_kernel_spmd(nc, in_maps, core_ids=list(range(8)))
    out = np.empty((4, S, D), np.float32)
    for c in range(8):
        b, hf = c // 2, c % 2
        out[b, hf * SH:(hf + 1) * SH] = res.results[c]["y"]
    return out


# version 27
# speedup vs baseline: 1.0213x; 1.0213x over previous
import contextlib
import os
from concourse.bass_core_rust import IndirectOffsetOnAxis
import numpy as np
import concourse.bass as bass
import concourse.mybir as mybir
from concourse.bass_utils import run_bass_kernel_spmd

F32 = mybir.dt.float32
BF16 = mybir.dt.bfloat16
AF = mybir.ActivationFunctionType
ALU = mybir.AluOpType
AX = mybir.AxisListType
I32 = mybir.dt.int32

D = 1024
DC = 8
NH = 16
DH = 64
S = 8192
SH = 4096
NPT = 8448
NCH = 66
OWN_C0 = 1
KV_CHUNKS = [c for c in range(NCH) if (1 <= c <= 32) or (34 <= c <= 65)]
T1 = 512
NCK1 = 4
NE = 8
DI = 2048
LN_EPS = 1e-5

DEV = int(os.environ.get("KDEV", "0"))


class Prog:
    def __init__(self):
        self.steps = []
        self.counts = {}

    def val(self, name):
        return self.counts.get(name, 0)

    def step(self, engine, emit, waits=(), inc=None):
        val = None
        if inc is not None:
            name, by = inc
            self.counts[name] = self.counts.get(name, 0) + by
            val = self.counts[name]
        ws = tuple((sn, sv) for (sn, sv) in waits if sv and sv > 0)
        self.steps.append((engine, ws, emit, inc))
        return val

    def emit_for(self, engine_name, eng, sems):
        for engine, waits, emit, inc in self.steps:
            if engine != engine_name:
                continue
            for (sn, sv) in waits:
                eng.wait_ge(sems[sn], sv)
            ins = emit(eng)
            if inc is not None:
                name, by = inc
                assert ins is not None, f"step with inc {name} returned None"
                ins.then_inc(sems[name], by)


def build_program():
    nc = bass.Bass("TRN2", target_bir_lowering=False, debug=False)

    ap = {}
    def din(name, shape, dt=F32):
        ap[name] = nc.dram_tensor(name, list(shape), dt, kind="ExternalInput").ap()
        return ap[name]

    xp = din("xp", [NPT, D])
    for w in ["wq_lin", "wk_lin", "wv_lin", "wq_loc", "wk_loc", "wv_loc", "wo", "wg2"]:
        din(w, [D, D])
    din("wg1", [3 * D, D])
    for b in ["bq_lin", "bk_lin", "bv_lin", "bq_loc", "bk_loc", "bv_loc",
              "bo", "bg1", "bg2", "ln_g", "ln_b", "fln_w", "fln_b"]:
        din(b, [1, D])
    din("rel_bias", [129, NH])
    din("router_w", [D, NE])
    din("gate_w", [NE, D, DI])
    din("up_w", [NE, D, DI])
    din("down_w", [NE, DI, D])
    din("sh_gate", [D, DI])
    din("sh_up", [D, DI])
    din("sh_down", [DI, D])
    din("lmask", [1, 192])
    din("rmask", [1, 192])

    y = nc.dram_tensor("y", [SH, D], F32, kind="ExternalOutput").ap()

    h_scr = nc.dram_tensor("h_scr", [DC, 128, 34 * 128], BF16, kind="Internal").ap()
    xn_scr = nc.dram_tensor("xn_scr", [SH, D], F32, kind="Internal").ap()
    nrm_scr = nc.dram_tensor("nrm_scr", [DC, 128, SH], BF16, kind="Internal").ap()
    win_scr = nc.dram_tensor("win_scr", [DC, 128, SH], BF16, kind="Internal").ap()
    winf_scr = nc.dram_tensor("winf_scr", [DC, 128, SH], F32, kind="Internal").ap()
    s_scr = nc.dram_tensor("s_scr", [SH, D], F32, kind="Internal").ap()
    n_tm_scr = nc.dram_tensor("n_tm_scr", [SH, D], BF16, kind="Internal").ap()

    dbg = {}
    if DEV:
        def dout(name, shape, dt=F32):
            dbg[name] = nc.dram_tensor(name, list(shape), dt, kind="ExternalOutput").ap()
        dout("dbg_kv", [128, DC * D])
        dout("dbg_st", [128, 12 * NCK1])
        dout("dbg_ksum", [128, DC])
        dout("dbg_h", [DC, 128, 34 * 128], BF16)
        dout("dbg_xn", [SH, D])
        dout("dbg_nrm", [DC, 128, SH], BF16)
        dout("dbg_win", [DC, 128, SH])
        dout("dbg_lin", [DC, 128, SH])
        dout("dbg_s", [SH, D])
        dout("dbg_moe", [DC, 128, SH])
        dout("dbg_moe2", [SH, D])
        dout("dbg_route", [128, 32])
        dout("dbg_rc", [NE * 256, D])
        dout("dbg_xg", [128, 2 * D], BF16)
        dout("dbg_hm", [128, 16 * 256], BF16)
        dout("dbg_rcfm", [128, 2048])

    es = contextlib.ExitStack()
    def sb(name, shape, dt):
        return es.enter_context(nc.sbuf_tensor(name, list(shape), dt))
    def psum(name, shape, dt=F32):
        return es.enter_context(nc.psum_tensor(name, list(shape), dt))

    c_id_bf = sb("c_id_bf", [128, 128], BF16)
    c_id_f = sb("c_id_f", [128, 128], F32)
    c_bias = sb("c_bias", [128, 13 * DC], F32)
    c_ones = sb("c_ones", [128, 8], BF16)
    c_bkv_bc = sb("c_bkv_bc", [128, 2 * D], F32)
    c_bo_bc = sb("c_bo_bc", [128, D], F32)
    c_lt = sb("c_lt", [128, 128], BF16)
    c_flnw_bc = sb("c_flnw_bc", [128, D], BF16)
    c_flnb_bc = sb("c_flnb_bc", [128, D], BF16)

    BIAS_ORDER = ["bq_lin", "bk_lin", "bv_lin", "bq_loc", "bk_loc", "bv_loc",
                  "bo", "bg1", "bg2", "ln_g", "ln_b", "fln_w", "fln_b"]
    BIDX = {n: i for i, n in enumerate(BIAS_ORDER)}
    def bias_col(name, j):
        o = BIDX[name] * DC + j
        return c_bias[:, o:o + 1]

    NF_AF = 24064
    NF_AB = 40960
    ARF = sb("ARF", [128, NF_AF], F32).ap()
    ARB = sb("ARB", [128, NF_AB], BF16).ap()

    def af(off, n):
        assert off + n <= NF_AF
        return ARF[:, off:off + n]
    def afb(off, n):
        assert off + n <= NF_AF and off % 8 == 0
        return ARF[:, off:off + n].bitcast(BF16)
    def ab(off, n):
        assert off + n <= NF_AB
        return ARB[:, off:off + n]

    x_tm = [af(0, 4096), af(4096, 4096)]
    xn_tm = af(8192, 4096)
    kv_sb = af(12288, 8192)
    elu_a = af(20480, 1024)
    elu_b = af(21504, 1024)
    sq_scr = af(22528, 1024)
    st_sum = af(23552, 16)
    st_m = af(23568, 32)
    n_bf = ab(0, 4096)
    h_fm = [ab(4096, 4096), ab(8192, 4096)]
    nrm_fm = ab(12288, 4096)
    ke_tm = ab(16384, 4096)
    v_tm = ab(20480, 4096)
    w_kl = ab(24576, 8192)
    w_vl = ab(32768, 8192)
    kv_bf = sb("kv_bf", [128, DC * D], BF16)
    ksum_bf = sb("ksum_bf", [128, DC], BF16)

    PS = psum("PS", [128, 4096], F32).ap()
    ps_p = PS[:, 0:1024]
    ps_kv = [PS[:, 1024:2048], PS[:, 2048:3072]]
    ps_ks = PS[:, 3072:3072 + DC]
    ps_t = PS[:, 3584:4096].bitcast(BF16)

    sems = {}
    for s_ in ["sX", "sXH", "sWT", "sWS", "sST", "sMM", "sVE", "sSE", "sGP", "sDM",
               "sOUT", "sPC", "sCV", "sCS", "sCG"]:
        sems[s_] = es.enter_context(nc.semaphore(s_))

    chain_cnt = {"sCV": 0, "sCS": 0, "sCG": 0}
    def guard(eng, ins, cs):
        chain_cnt[cs] += 1
        ins.then_inc(sems[cs], 1)
        eng.wait_ge(sems[cs], chain_cnt[cs])
        return ins

    P = Prog()

    def em_idmat(g):
        guard(g, g.iota(out=c_id_f[:], pattern=[[1, 128]], base=0,
                        channel_multiplier=-1,
                        allow_small_or_imprecise_dtypes=True), "sCG")
        guard(g, g.tensor_single_scalar(out=c_id_f[:], in_=c_id_f[:],
                                        scalar=0.0, op=ALU.is_equal), "sCG")
        return g.tensor_copy(out=c_id_bf[:], in_=c_id_f[:])
    gp_id = P.step("gpsimd", em_idmat, inc=("sGP", 1))
    ve_ones = P.step("vector", lambda v: v.memset(c_ones[:], 1.0), inc=("sVE", 1))

    def em_bias(name):
        def f(e):
            src = ap[name].rearrange("o (j p) -> (o p) j", p=128)
            return e.dma_start(out=c_bias[:, BIDX[name] * DC:(BIDX[name] + 1) * DC],
                               in_=src, allow_slow_non_contiguous=True)
        return f
    for bn in BIAS_ORDER:
        dm_t = P.step("sync", em_bias(bn), inc=("sDM", 16))
    dm_t = P.step("sync", lambda e: e.dma_start(
        out=c_bkv_bc[:, 0:D], in_=ap["bk_lin"].broadcast_to((128, D))), inc=("sDM", 16))
    dm_t = P.step("sync", lambda e: e.dma_start(
        out=c_bkv_bc[:, D:2 * D], in_=ap["bv_lin"].broadcast_to((128, D))), inc=("sDM", 16))
    dm_t = P.step("sync", lambda e: e.dma_start(
        out=c_bo_bc[:], in_=ap["bo"].broadcast_to((128, D))), inc=("sDM", 16))
    dm_t2 = P.step("gpsimd", lambda g: g.dma_start(
        out=c_flnw_bc[:], in_=ap["fln_w"].broadcast_to((128, D))), inc=("sWT", 16))
    dm_t2 = P.step("gpsimd", lambda g: g.dma_start(
        out=c_flnb_bc[:], in_=ap["fln_b"].broadcast_to((128, D))), inc=("sWT", 16))
    SETUP_DM = dm_t

    def em_wload(dst, src_name):
        def f(g):
            src = ap[src_name].rearrange("(i p) n -> p i n", p=128)
            nI = src.shape[1]
            return g.dma_start(out=dst[:].rearrange("p (i n) -> p i n", i=nI), in_=src)
        return f
    P.step("gpsimd", em_wload(w_kl, "wk_lin"), inc=("sWT", 16))
    PH1_WT = P.step("gpsimd", em_wload(w_vl, "wv_lin"), inc=("sWT", 16))

    gwb = nc.dram_tensor("gwb", [NE, D, DI], BF16, kind="Internal").ap()
    uwb = nc.dram_tensor("uwb", [NE, D, DI], BF16, kind="Internal").ap()
    dwb = nc.dram_tensor("dwb", [NE, DI, D], BF16, kind="Internal").ap()
    sgb = nc.dram_tensor("sgb", [D, DI], BF16, kind="Internal").ap()
    sub_ = nc.dram_tensor("sub", [D, DI], BF16, kind="Internal").ap()
    sdb = nc.dram_tensor("sdb", [DI, D], BF16, kind="Internal").ap()
    N_PC = 0
    for e9 in range(NE):
        for dst9, nm9 in [(gwb, "gate_w"), (uwb, "up_w"), (dwb, "down_w")]:
            P.step("gpsimd", lambda g, dst9=dst9, nm9=nm9, e9=e9: g.dma_start(
                out=dst9[e9], in_=ap[nm9][e9]), inc=("sPC", 16))
            N_PC += 16
    for dst9, nm9 in [(sgb, "sh_gate"), (sub_, "sh_up"), (sdb, "sh_down")]:
        P.step("gpsimd", lambda g, dst9=dst9, nm9=nm9: g.dma_start(
            out=dst9, in_=ap[nm9]), inc=("sPC", 16))
        N_PC += 16

    groups = []
    c0 = 0
    while c0 < NCH:
        n = min(NCK1, NCH - c0)
        groups.append((c0, n))
        c0 += n

    st = {"x_rd": {}, "h_rd": {}, "kv_ve": 0, "ks_started": False,
          "xn_st": 0, "proj_rd": 0, "kv_mm": {}, "h_ready": {}, "ke_rd": 0,
          "kv_j_seen": 0, "x_rd_se": 0}
    n_kv_total = len(KV_CHUNKS)
    kv_seen = 0

    for g_idx, (c_start, nck) in enumerate(groups):
        buf = g_idx % 2
        xt = x_tm[buf]
        hfm = h_fm[buf]
        tok0 = c_start * 128

        waits = []
        if g_idx >= 2:
            waits.append(("sVE", st["x_rd"].get(g_idx - 2, 0)))
            waits.append(("sSE", st["x_rd_se"]))
        ld_x = P.step("sync", lambda e, xt=xt, tok0=tok0, nck=nck: e.dma_start(
            out=xt[:, 0:nck * D].rearrange("p (c d) -> p c d", c=nck),
            in_=xp[tok0:tok0 + nck * 128, :].rearrange("(c p) d -> p c d", p=128)),
            waits=waits, inc=("sX", 16))

        for cc in range(nck):
            xa = xt[:, cc * D:(cc + 1) * D]
            ve_red1 = P.step("vector", lambda v, xa=xa, cc=cc: v.reduce_sum(
                out=st_sum[:, cc:cc + 1], in_=xa, axis=AX.X),
                waits=([("sX", 16 * (g_idx + 1))] if cc == 0 else ()),
                inc=(("sVE", 1) if cc == nck - 1 else None))
            se_sq = P.step("scalar", lambda s, xa=xa, cc=cc: s.activation(
                out=sq_scr[:], in_=xa, func=AF.Square,
                accum_out=st_sum[:, NCK1 + cc:NCK1 + cc + 1]),
                waits=([("sX", 16 * (g_idx + 1))] if cc == 0 else
                       [("sSE", P.counts.get("sSE", 0))]),
                inc=("sSE", 1))

        def em_stats(dst_mean, dst_var, tmp, src_off=0):
            def f(v):
                guard(v, v.tensor_scalar(out=st_m[:, dst_mean:dst_mean + NCK1],
                                         in0=st_sum[:, src_off:src_off + NCK1],
                                         scalar1=1.0 / D, scalar2=None, op0=ALU.mult), "sCV")
                guard(v, v.tensor_tensor(out=st_m[:, tmp:tmp + NCK1],
                                         in0=st_m[:, dst_mean:dst_mean + NCK1],
                                         in1=st_m[:, dst_mean:dst_mean + NCK1],
                                         op=ALU.mult), "sCV")
                return guard(v, v.scalar_tensor_tensor(
                    out=st_m[:, dst_var:dst_var + NCK1],
                    in0=st_sum[:, src_off + NCK1:src_off + 2 * NCK1],
                    scalar=1.0 / D, in1=st_m[:, tmp:tmp + NCK1],
                    op0=ALU.mult, op1=ALU.subtract), "sCV")
            return f

        def em_rsqrt(dst, src, tmp):
            def f(v):
                guard(v, v.memset(st_m[:, dst:dst + NCK1], 1.0), "sCV")
                ins = None
                for _ in range(5):
                    guard(v, v.tensor_tensor(out=st_m[:, tmp:tmp + NCK1],
                                             in0=st_m[:, dst:dst + NCK1],
                                             in1=st_m[:, dst:dst + NCK1], op=ALU.mult), "sCV")
                    guard(v, v.scalar_tensor_tensor(out=st_m[:, tmp:tmp + NCK1],
                                                    in0=st_m[:, src:src + NCK1], scalar=LN_EPS,
                                                    in1=st_m[:, tmp:tmp + NCK1],
                                                    op0=ALU.add, op1=ALU.mult), "sCV")
                    guard(v, v.tensor_scalar(out=st_m[:, tmp:tmp + NCK1],
                                             in0=st_m[:, tmp:tmp + NCK1],
                                             scalar1=-0.5, scalar2=1.5,
                                             op0=ALU.mult, op1=ALU.add), "sCV")
                    ins = v.tensor_tensor(out=st_m[:, dst:dst + NCK1],
                                          in0=st_m[:, dst:dst + NCK1],
                                          in1=st_m[:, tmp:tmp + NCK1], op=ALU.mult)
                    if _ < 4:
                        guard(v, ins, "sCV")
                return ins
            return f

        P.step("vector", em_stats(0, 1 * NCK1, 6 * NCK1, 0),
               waits=[("sSE", se_sq), ("sVE", ve_red1)])
        ve_r1 = P.step("vector", em_rsqrt(2 * NCK1, 1 * NCK1, 7 * NCK1), inc=("sVE", 1))

        xw = [("sST", st["xn_st"]), ("sSE", st.get("xn_se", 0)), ("sVE", ve_r1)]
        for cc in range(nck):
            ve_xn = P.step("vector", lambda v, cc=cc, xt=xt: v.tensor_scalar(
                out=xn_tm[:, cc * D:(cc + 1) * D], in0=xt[:, cc * D:(cc + 1) * D],
                scalar1=st_m[:, cc:cc + 1], scalar2=st_m[:, 2 * NCK1 + cc:2 * NCK1 + cc + 1],
                op0=ALU.subtract, op1=ALU.mult), waits=(xw if cc == 0 else ()),
                inc=(("sVE", 1) if cc == nck - 1 else None))
        st["x_rd"][g_idx] = ve_xn
        st["x_rd_se"] = se_sq

        for cc in range(nck):
            xa = xn_tm[:, cc * D:(cc + 1) * D]
            ve_red2 = P.step("vector", lambda v, xa=xa, cc=cc: v.reduce_sum(
                out=st_sum[:, 2 * NCK1 + cc:2 * NCK1 + cc + 1], in_=xa, axis=AX.X),
                waits=([("sVE", ve_xn)] if cc == 0 else ()),
                inc=(("sVE", 1) if cc == nck - 1 else None))
            se_sq2 = P.step("scalar", lambda s, xa=xa, cc=cc: s.activation(
                out=sq_scr[:], in_=xa, func=AF.Square,
                accum_out=st_sum[:, 3 * NCK1 + cc:3 * NCK1 + cc + 1]),
                waits=([("sVE", ve_xn)] if cc == 0 else
                       [("sSE", P.counts.get("sSE", 0))]),
                inc=("sSE", 1))
        st["xn_se"] = se_sq2
        P.step("vector", em_stats(3 * NCK1, 4 * NCK1, 6 * NCK1, 2 * NCK1),
               waits=[("sSE", se_sq2), ("sVE", ve_red2)])
        ve_r2 = P.step("vector", em_rsqrt(5 * NCK1, 4 * NCK1, 7 * NCK1), inc=("sVE", 1))
        if DEV and g_idx == 0:
            P.step("sync", lambda e: e.dma_start(out=dbg["dbg_st"][:, 0:8 * NCK1],
                                                 in_=st_m[:]),
                   waits=[("sVE", ve_r2)], inc=("sOUT", 16))
            P.step("sync", lambda e: e.dma_start(out=dbg["dbg_st"][:, 8 * NCK1:12 * NCK1],
                                                 in_=st_sum[:]), inc=("sOUT", 16))

        nw = [("sMM", st.get("nbf_mm", 0)), ("sVE", ve_r2)]
        for cc in range(nck):
            ve_nbf = P.step("vector", lambda v, cc=cc: v.tensor_scalar(
                out=n_bf[:, cc * D:(cc + 1) * D], in0=xn_tm[:, cc * D:(cc + 1) * D],
                scalar1=st_m[:, 3 * NCK1 + cc:3 * NCK1 + cc + 1],
                scalar2=st_m[:, 5 * NCK1 + cc:5 * NCK1 + cc + 1],
                op0=ALU.subtract, op1=ALU.mult), waits=(nw if cc == 0 else ()),
                inc=(("sVE", 1) if cc == nck - 1 else None))

        hw = [("sMM", st["h_rd"].get(g_idx - 2, 0)),
              ("sST", st.get("h_st", {}).get(g_idx - 2, 0))]
        mm_last = 0
        for cc in range(nck):
            def em_tr(t, cc=cc):
                ins = None
                for j in range(DC):
                    ins = t.transpose(out=ps_t[:, j * 128:(j + 1) * 128],
                                      in_=n_bf[:, cc * D + j * 128:cc * D + (j + 1) * 128],
                                      identity=c_id_bf[:])
                return ins
            mm_tr = P.step("tensor", em_tr,
                           waits=[("sVE", ve_nbf), ("sGP", gp_id),
                                  ("sVE", st.get("pst_rd", 0))] + (hw if cc == 0 else []),
                           inc=("sMM", 1))
            own = OWN_C0 <= (c_start + cc) < OWN_C0 + 32
            def em_evac(v, cc=cc, own=own, hfm=hfm):
                ins = None
                for j in range(DC):
                    ins = v.tensor_scalar(
                        out=hfm[:, j * T1 + cc * 128: j * T1 + (cc + 1) * 128],
                        in0=ps_t[:, j * 128:(j + 1) * 128],
                        scalar1=bias_col("ln_g", j), scalar2=bias_col("ln_b", j),
                        op0=ALU.mult, op1=ALU.add)
                if own:
                    for j in range(DC):
                        ins = v.tensor_scalar(
                            out=nrm_fm[:, j * T1 + cc * 128: j * T1 + (cc + 1) * 128],
                            in0=ps_t[:, j * 128:(j + 1) * 128],
                            scalar1=bias_col("fln_w", j), scalar2=bias_col("fln_b", j),
                            op0=ALU.mult, op1=ALU.add)
                return ins
            ve_ev = P.step("vector", em_evac,
                           waits=[("sMM", mm_tr), ("sDM", SETUP_DM), ("sWT", PH1_WT),
                                  ("sST", st.get("nrm_st", 0) if own and cc == 0 else 0)],
                           inc=("sVE", 1))
            st["pst_rd"] = ve_ev
            st["h_ready"][(g_idx, cc)] = ve_ev
        st["nbf_mm"] = mm_tr
        ve_hdone = st["h_ready"][(g_idx, nck - 1)]

        if c_start < 34:
            nst = min(nck, 34 - c_start)
            st_h = P.step("sync", lambda e, tok0=tok0, nst=nst, hfm=hfm: e.dma_start(
                out=h_scr[:, :, tok0:tok0 + nst * 128].rearrange("j p t -> p j t"),
                in_=hfm.rearrange("p (j t) -> p j t", j=DC)[:, :, 0:nst * 128]),
                waits=[("sVE", ve_hdone)], inc=("sST", 16))
            st.setdefault("h_st", {})[g_idx] = st_h
        own_lo = max(c_start, OWN_C0)
        own_hi = min(c_start + nck, OWN_C0 + 32)
        if own_lo < own_hi:
            cc0, cc1 = own_lo - c_start, own_hi - c_start
            ot = (own_lo - OWN_C0) * 128
            ncc = cc1 - cc0
            st_x = P.step("sync", lambda e, cc0=cc0, ncc=ncc, ot=ot: e.dma_start(
                out=xn_scr[ot:ot + ncc * 128, :].rearrange("(c p) d -> p c d", p=128),
                in_=xn_tm.rearrange("p (c d) -> p c d", c=NCK1)[:, cc0:cc0 + ncc, :]),
                waits=[("sVE", ve_nbf)], inc=("sST", 16))
            st_x = P.step("sync", lambda e, cc0=cc0, ncc=ncc, ot=ot: e.dma_start(
                out=n_tm_scr[ot:ot + ncc * 128, :].rearrange("(c p) d -> p c d", p=128),
                in_=n_bf.rearrange("p (c d) -> p c d", c=NCK1)[:, cc0:cc0 + ncc, :]),
                inc=("sST", 16))
            st["xn_st"] = st_x
            st_n = P.step("sync", lambda e, cc0=cc0, ncc=ncc, ot=ot: e.dma_start(
                out=nrm_scr[:, :, ot:ot + ncc * 128].rearrange("j p t -> p j t"),
                in_=nrm_fm.rearrange("p (j t) -> p j t", j=DC)[:, :, cc0 * 128:(cc0 + ncc) * 128]),
                waits=[("sVE", ve_hdone)], inc=("sST", 16))
            st["nrm_st"] = st_n

        kv_cc = [cc for cc in range(nck) if (c_start + cc) in set(KV_CHUNKS)]
        for cc in kv_cc:
            def em_kproj(t, cc=cc, hfm=hfm):
                ins = None
                for di in range(DC):
                    lhsT = hfm[:, di * T1 + cc * 128: di * T1 + (cc + 1) * 128]
                    for half in range(2):
                        ins = t.matmul(
                            out=ps_p[:, half * 512:(half + 1) * 512], lhsT=lhsT,
                            rhs=w_kl[:, di * D + half * 512: di * D + (half + 1) * 512],
                            start=(di == 0), stop=(di == DC - 1))
                return ins
            mm_k = P.step("tensor", em_kproj,
                          waits=[("sVE", st["h_ready"][(g_idx, cc)]), ("sWT", PH1_WT),
                                 ("sVE", st["proj_rd"])],
                          inc=("sMM", 1))
            P.step("vector", lambda v: guard(v, v.tensor_tensor(
                out=elu_a[:], in0=ps_p[:], in1=c_bkv_bc[:, 0:D], op=ALU.add), "sCV"),
                waits=[("sMM", mm_k)])
            ve_min = P.step("vector", lambda v: v.tensor_scalar(
                out=elu_b[:], in0=elu_a[:], scalar1=0.0, scalar2=None, op0=ALU.min),
                inc=("sVE", 1))
            st["proj_rd"] = ve_min
            se_exp = P.step("scalar", lambda s: s.activation(
                out=elu_b[:], in_=elu_b[:], func=AF.Exp),
                waits=[("sVE", ve_min)], inc=("sSE", 1))

            def em_vproj(t, cc=cc, hfm=hfm):
                ins = None
                for di in range(DC):
                    lhsT = hfm[:, di * T1 + cc * 128: di * T1 + (cc + 1) * 128]
                    for half in range(2):
                        ins = t.matmul(
                            out=ps_p[:, half * 512:(half + 1) * 512], lhsT=lhsT,
                            rhs=w_vl[:, di * D + half * 512: di * D + (half + 1) * 512],
                            start=(di == 0), stop=(di == DC - 1))
                return ins
            mm_v = P.step("tensor", em_vproj, waits=[("sVE", ve_min)], inc=("sMM", 1))

            def em_elu_fin(v, cc=cc):
                guard(v, v.tensor_scalar(out=elu_a[:], in0=elu_a[:], scalar1=0.0,
                                         scalar2=None, op0=ALU.max), "sCV")
                return v.tensor_tensor(out=ke_tm[:, cc * D:(cc + 1) * D],
                                       in0=elu_a[:], in1=elu_b[:], op=ALU.add)
            ve_ke = P.step("vector", em_elu_fin, waits=[("sSE", se_exp)], inc=("sVE", 1))

            ve_v = P.step("vector", lambda v, cc=cc: v.tensor_tensor(
                out=v_tm[:, cc * D:(cc + 1) * D], in0=ps_p[:], in1=c_bkv_bc[:, D:2 * D],
                op=ALU.add), waits=[("sMM", mm_v)], inc=("sVE", 1))
            st["proj_rd"] = ve_v
            st["kev_ready"] = ve_v

        if kv_cc:
            kvadd_t = {}
            for j in range(DC):
                pbuf = ps_kv[j % 2]
                def em_kvmm(t, j=j, kv_cc=tuple(kv_cc), pbuf=pbuf):
                    ins = None
                    for idx, cc in enumerate(kv_cc):
                        lhsT = ke_tm[:, cc * D + j * 128: cc * D + (j + 1) * 128]
                        for half in range(2):
                            ins = t.matmul(
                                out=pbuf[:, half * 512:(half + 1) * 512], lhsT=lhsT,
                                rhs=v_tm[:, cc * D + half * 512: cc * D + (half + 1) * 512],
                                start=(idx == 0), stop=(idx == len(kv_cc) - 1 and half == 1))
                        ins2 = t.matmul(out=ps_ks[:, j:j + 1], lhsT=lhsT,
                                        rhs=c_ones[:, 0:1],
                                        start=(not st["ks_started"]),
                                        stop=(st["kv_j_seen"] == DC * n_kv_total - 1))
                        st["ks_started"] = True
                        st["kv_j_seen"] += 1
                        ins = ins2
                    return ins
                w = [("sVE", st["kev_ready"])]
                if j >= 2:
                    w.append(("sVE", kvadd_t[j - 2]))
                else:
                    w.append(("sVE", st["kv_ve"]))
                mm_kv = P.step("tensor", em_kvmm, waits=w, inc=("sMM", 1))
                def em_kvadd(v, j=j, pbuf=pbuf, first=(g_idx == 0)):
                    if first:
                        return v.tensor_copy(out=kv_sb[:, j * D:(j + 1) * D], in_=pbuf[:])
                    return v.tensor_tensor(out=kv_sb[:, j * D:(j + 1) * D],
                                           in0=kv_sb[:, j * D:(j + 1) * D],
                                           in1=pbuf[:], op=ALU.add)
                ve_kv = P.step("vector", em_kvadd,
                               waits=[("sMM", mm_kv),
                                      ("sVE", st.get("kvadd_prev", {}).get(j, 0))],
                               inc=("sVE", 1))
                st.setdefault("kvadd_prev", {})[j] = ve_kv
                kvadd_t[j] = ve_kv
                if j == DC - 1:
                    st["kv_ve"] = ve_kv
            st["h_rd"][g_idx] = mm_kv
            st["ke_rd"] = ve_kv
        else:
            st["h_rd"][g_idx] = P.val("sMM")

    last_kv_mm = P.val("sMM")
    last_kv_ve = P.val("sVE")
    def em_kvbf(v):
        ins = None
        for j in range(DC):
            ins = v.tensor_copy(out=kv_bf[:, j * D:(j + 1) * D],
                                in_=kv_sb[:, j * D:(j + 1) * D])
        return ins
    ve_kvbf = P.step("vector", em_kvbf, waits=[("sVE", last_kv_ve)], inc=("sVE", 1))
    ks_f = sb("ks_f", [128, DC], F32)
    def em_ksum(v):
        v.tensor_copy(out=ks_f[:], in_=ps_ks[:])
        return v.tensor_copy(out=ksum_bf[:], in_=ps_ks[:])
    ve_ksum = P.step("vector", em_ksum, waits=[("sMM", last_kv_mm)], inc=("sVE", 1))

    if DEV:
        d_t = P.step("sync", lambda e: e.dma_start(out=dbg["dbg_kv"], in_=kv_sb[:]),
                     waits=[("sVE", ve_kvbf)], inc=("sOUT", 16))
        d_t = P.step("sync", lambda e: e.dma_start(out=dbg["dbg_ksum"], in_=ks_f[:]),
                     waits=[("sVE", ve_ksum)], inc=("sOUT", 16))
        d_t = P.step("sync", lambda e: e.dma_start(
            out=dbg["dbg_h"].rearrange("j p t -> (j p) t"),
            in_=h_scr.rearrange("j p t -> (j p) t")), inc=("sOUT", 16))
        d_t = P.step("sync", lambda e: e.dma_start(out=dbg["dbg_xn"], in_=xn_scr),
                     inc=("sOUT", 16))
        d_t = P.step("sync", lambda e: e.dma_start(
            out=dbg["dbg_nrm"].rearrange("j p t -> (j p) t"),
            in_=nrm_scr.rearrange("j p t -> (j p) t")), inc=("sOUT", 16))


    PH1_VE = P.val("sVE"); PH1_SE = P.val("sSE"); PH1_MM = P.val("sMM")
    PH1_ST = P.val("sST"); PH1_OUT = P.val("sOUT")
    BAR1 = [("sVE", PH1_VE), ("sSE", PH1_SE), ("sMM", PH1_MM), ("sST", PH1_ST)] + \
           ([("sOUT", PH1_OUT)] if DEV else [])

    T2 = 512
    B_cat = af(0, 16 * 192)
    lm_cat = af(3072, 8 * 192)
    rm_cat = af(4608, 8 * 192)
    win_f = af(6144, 4096)
    scr_sm = af(10240, 8 * 192)
    nm_t = af(11776, 16)
    rs_t = af(11800, 16)
    rr_t = af(11824, 16)
    w_ql2 = afb(12288, 4096)
    A_bf = afb(16384, 1536)
    A2_bf = afb(17920, 1536)
    win_b = afb(19456, 2048)
    bvl_bc = af(21504, 1024)
    At_sb = afb(22528, 1024)
    hh_bufs = [ab(0, 5120), ab(5120, 5120)]
    q_fm2 = ab(10240, 4096)
    k_fm2 = ab(14336, 5120)
    vS_b = ab(19456, 5120)
    w_kl2 = w_kl
    w_vl2 = w_vl
    ps_s = PS[:, 0:2048]
    def ps_s_head(h8):
        o = (h8 % 2) * 1024 + (h8 // 2) * 256
        return PS[:, o:o + 192]
    ps_s4 = PS[:, 0:2048].rearrange("p (e g w) -> p e g w", e=2, w=256)[:, :, :, 0:192]
    scr4 = None
    At_ps = PS[:, 2048:3072].bitcast(BF16)
    ps_w2 = PS[:, 3072:4096]
    ps_q2 = [PS[:, 0:512], PS[:, 512:1024]]
    ps_k2 = PS[:, 1024:1664]
    ps_v2 = PS[:, 2048:3072]

    d_band = nc.dram_tensor("d_band", [128, 193], F32, kind="Internal").ap()
    d_rb = nc.dram_tensor("d_rb", [1, 65], F32, kind="Internal").ap()

    wt2 = P.step("gpsimd", em_wload(w_ql2, "wq_loc"), waits=BAR1, inc=("sWT", 16))
    wt2 = P.step("gpsimd", em_wload(w_kl2, "wk_loc"), inc=("sWT", 16))
    wt2 = P.step("gpsimd", em_wload(w_vl2, "wv_loc"), inc=("sWT", 16))
    PH2_WT = wt2
    dm2 = P.step("sync", lambda e: e.dma_start(
        out=bvl_bc[:], in_=ap["bv_loc"].broadcast_to((128, D))), waits=BAR1, inc=("sDM", 16))
    dm2 = P.step("sync", lambda e: e.dma_start(
        out=lm_cat.rearrange("p (t k) -> p t k", t=8),
        in_=ap["lmask"].broadcast_to((1024, 192)).rearrange("(p t) k -> p t k", p=128)),
        inc=("sDM", 16))
    dm2 = P.step("sync", lambda e: e.dma_start(
        out=rm_cat.rearrange("p (t k) -> p t k", t=8),
        in_=ap["rmask"].broadcast_to((1024, 192)).rearrange("(p t) k -> p t k", p=128)),
        inc=("sDM", 16))
    ve_fill = P.step("vector", lambda v: v.memset(scr_sm[:, 0:193], -1e30),
                     waits=BAR1, inc=("sVE", 1))
    for hh_ in range(NH):
        dm2 = P.step("sync", lambda e, hh_=hh_: e.dma_start(
            out=d_rb, in_=ap["rel_bias"][32:97, hh_:hh_ + 1].rearrange("a b -> b a"),
            allow_slow_non_contiguous=True),
            waits=[("sDM", dm2)], inc=("sDM", 16))
        dm2 = P.step("sync", lambda e: e.dma_start(out=d_band, in_=scr_sm[:, 0:193]),
                     waits=[("sVE", ve_fill), ("sDM", dm2)], inc=("sDM", 16))
        dm2 = P.step("sync", lambda e: e.dma_start(
            out=d_band[:, 0:65], in_=d_rb.broadcast_to((128, 65))),
            waits=[("sDM", dm2)], inc=("sDM", 16))
        dview = d_band.rearrange("a b -> (a b)")[0:128 * 192].rearrange("(a b) -> a b", b=192)
        dm2 = P.step("sync", lambda e, hh_=hh_, dview=dview: e.dma_start(
            out=B_cat[:, hh_ * 192:(hh_ + 1) * 192], in_=dview),
            waits=[("sDM", dm2)], inc=("sDM", 16))
    PH2_DM = dm2

    st2 = {"hh_rd": {}, "ps01_rd": 0, "ps23_rd": 0, "ps45_rd": 0, "ps67_rd": 0,
           "qk_ve": 0, "win_st": {}, "at_ve": 0}

    def plan_ldh2(t):
        hh3_ = hh_bufs[t % 2].rearrange("p (j t) -> p j t", j=DC)
        P0_ = 128 + T2 * t
        w_ = list(BAR1) + [("sDM", PH2_DM)]
        if t >= 2:
            w_.append(("sMM", st2["hh_rd"].get(t - 2, 0)))
        if t > 0:
            w_.append(("sXH", XH0 + 16 * t))
        P.step("sync", lambda e, hh3=hh3_, P0=P0_: e.dma_start(
            out=hh3, in_=h_scr[:, :, P0 - 64:P0 + 576].rearrange("j p t -> p j t")),
            waits=w_, inc=("sXH", 16))
    XH0 = P.counts.get("sXH", 0)
    plan_ldh2(0)
    for t in range(8):
        buf = t % 2
        hh = hh_bufs[buf]
        P0 = 128 + T2 * t
        hh3 = hh.rearrange("p (j t) -> p j t", j=DC)
        if t + 1 < 8:
            plan_ldh2(t + 1)
        xw = ("sXH", XH0 + 16 * (t + 1))

        for j in range(DC):
            pq = ps_q2[j % 2]
            def em_qp(tn, j=j, hh=hh, pq=pq):
                ins = None
                for di in range(DC):
                    ins = tn.matmul(out=pq[:],
                                    lhsT=w_ql2[:, di * D + j * 128: di * D + (j + 1) * 128],
                                    rhs=hh[:, di * 640 + 64: di * 640 + 576],
                                    start=(di == 0), stop=(di == DC - 1))
                return ins
            mm_q = P.step("tensor", em_qp,
                          waits=[xw, ("sWT", PH2_WT),
                                 ("sVE", st2["ps01_rd"] if j < 2 else 0),
                                 ("sVE", st2.get(f"qev{j-2}", 0) if j >= 2 else 0)],
                          inc=("sMM", 1))
            ve_q = P.step("vector", lambda v, j=j, pq=pq: v.tensor_scalar(
                out=q_fm2[:, j * T2:(j + 1) * T2], in0=pq[:],
                scalar1=bias_col("bq_loc", j), scalar2=0.125,
                op0=ALU.add, op1=ALU.mult),
                waits=[("sMM", mm_q)],
                inc=("sVE", 1))
            st2[f"qev{j}"] = ve_q
        for j in range(DC):
            def em_kp(tn, j=j, hh=hh):
                ins = None
                for di in range(DC):
                    tn.matmul(out=ps_k2[:, 0:512],
                              lhsT=w_kl2[:, di * D + j * 128: di * D + (j + 1) * 128],
                              rhs=hh[:, di * 640: di * 640 + 512],
                              start=(di == 0), stop=(di == DC - 1))
                for di in range(DC):
                    ins = tn.matmul(out=ps_k2[:, 512:640],
                                    lhsT=w_kl2[:, di * D + j * 128: di * D + (j + 1) * 128],
                                    rhs=hh[:, di * 640 + 512: di * 640 + 640],
                                    start=(di == 0), stop=(di == DC - 1))
                return ins
            mm_k = P.step("tensor", em_kp,
                          waits=[("sVE", st2["ps23_rd"] if j == 0 else
                                  st2.get(f"kev{j-1}", 0))],
                          inc=("sMM", 1))
            ve_k = P.step("vector", lambda v, j=j: v.tensor_scalar(
                out=k_fm2[:, j * 640:(j + 1) * 640], in0=ps_k2[:],
                scalar1=bias_col("bk_loc", j), scalar2=None, op0=ALU.add),
                waits=[("sMM", mm_k)], inc=("sVE", 1))
            st2[f"kev{j}"] = ve_k
        for m in range(5):
            rows = 128 if m < 4 else 64
            def em_vp(tn, m=m, hh=hh, rows=rows):
                ins = None
                for di in range(DC):
                    lhsT = hh[:, di * 640 + 32 + 128 * m: di * 640 + 32 + 128 * m + rows]
                    for half in range(2):
                        ins = tn.matmul(
                            out=ps_v2[0:rows, half * 512:(half + 1) * 512], lhsT=lhsT,
                            rhs=w_vl2[:, di * D + half * 512: di * D + (half + 1) * 512],
                            start=(di == 0), stop=(di == DC - 1))
                return ins
            mm_v = P.step("tensor", em_vp,
                          waits=[("sVE", st2["ps45_rd"] if m == 0 else
                                  st2.get(f"vev{m-1}", 0)),
                                 ("sDM", PH2_DM if m == 0 else 0)],
                          inc=("sMM", 1))
            ve_v = P.step("vector", lambda v, m=m, rows=rows: v.tensor_tensor(
                out=vS_b[0:rows, m * D:(m + 1) * D], in0=ps_v2[0:rows, :],
                in1=bvl_bc[0:rows, :], op=ALU.add),
                waits=[("sMM", mm_v)], inc=("sVE", 1))
            st2[f"vev{m}"] = ve_v
        qkv_ve = ve_v
        st2["hh_rd"][t] = mm_v

        pend = None
        def plan_trav(pd):
            m_, hhh_, vsc2_ = pd
            def em_tr2(tn, hhh=hhh_):
                ins = None
                for h8 in range(8):
                    h = 8 * hhh + h8
                    tn.transpose(out=At_ps[:, h8 * 256: h8 * 256 + 128],
                                 in_=A2_bf[:, h * 192: h * 192 + 128],
                                 identity=c_id_bf[:])
                    ins = tn.transpose(out=At_ps[0:64, h8 * 256 + 128: h8 * 256 + 256],
                                       in_=A2_bf[:, h * 192 + 128: h * 192 + 192],
                                       identity=c_id_bf[:])
                return ins
            mm_tr2 = P.step("tensor", em_tr2,
                            waits=[("sVE", vsc2_), ("sVE", st2["at_ve"]),
                                   ("sGP", gp_id)],
                            inc=("sMM", 1))
            ve_at = P.step("vector", lambda v: v.tensor_copy(out=At_sb[:], in_=At_ps[:]),
                           waits=[("sMM", mm_tr2)], inc=("sVE", 1))
            st2["at_ve"] = ve_at
            def em_av(tn, m=m_, hhh=hhh_):
                ins = None
                for h8 in range(8):
                    h = 8 * hhh + h8
                    ob = (h % 2) * 64
                    oc = (h // 2) * 128
                    tn.matmul(out=ps_w2[ob:ob + 64, oc:oc + 128],
                              lhsT=vS_b[:, m * D + h * 64: m * D + (h + 1) * 64],
                              rhs=At_sb[:, h8 * 256: h8 * 256 + 128],
                              start=True, stop=False)
                    ins = tn.matmul(out=ps_w2[ob:ob + 64, oc:oc + 128],
                                    lhsT=vS_b[0:64, (m + 1) * D + h * 64: (m + 1) * D + (h + 1) * 64],
                                    rhs=At_sb[0:64, h8 * 256 + 128: h8 * 256 + 256],
                                    start=False, stop=True)
                return ins
            mm_av = P.step("tensor", em_av,
                           waits=[("sVE", ve_at),
                                  ("sVE", st2["ps67_rd"] if hhh_ == 0 else 0)],
                           inc=("sMM", 1))
            if hhh_ == 1:
                ve_we = P.step("vector", lambda v, m=m_: v.tensor_copy(
                    out=win_f.rearrange("p (j q) -> p j q", j=DC)[:, :, 128 * m:128 * (m + 1)],
                    in_=ps_w2.rearrange("p (j q) -> p j q", q=128)),
                    waits=[("sMM", mm_av),
                           ("sST", st2.get("winf_st", 0) if m_ == 0 else 0)],
                    inc=("sVE", 1))
                st2["ps67_rd"] = ve_we
        for m in range(4):
            m_global = 4 * t + m
            for hhh in range(2):
                def em_sc(tn, m=m, hhh=hhh):
                    ins = None
                    for h8 in range(8):
                        h = 8 * hhh + h8
                        dj, base = h // 2, (h % 2) * 64
                        ins = tn.matmul(
                            out=ps_s_head(h8),
                            lhsT=q_fm2[base:base + 64, dj * T2 + 128 * m: dj * T2 + 128 * (m + 1)],
                            rhs=k_fm2[base:base + 64, dj * 640 + 128 * m + 32: dj * 640 + 128 * m + 224],
                            start=True, stop=True)
                    return ins
                w_sc = [("sVE", qkv_ve), ("sVE", st2["ps03_free"] if (m, hhh) != (0, 0) else 0)] \
                    if False else [("sVE", qkv_ve), ("sVE", st2.get("badd_ve", 0))]
                mm_sc = P.step("tensor", em_sc, waits=w_sc, inc=("sMM", 1))
                if pend is not None:
                    plan_trav(pend)
                def em_badd(v, m_global=m_global, hhh=hhh):
                    edge = m_global in (0, 31)
                    scrv = scr_sm.rearrange("p (g e w) -> p e g w", e=2, w=192)
                    bv = B_cat[:, hhh * 1536:(hhh + 1) * 1536] \
                        .rearrange("p (g e w) -> p e g w", e=2, w=192)
                    guard(v, v.tensor_tensor(out=scrv[:, 0:1], in0=ps_s4[:, 0:1],
                                             in1=bv[:, 0:1], op=ALU.add), "sCV")
                    ins = v.tensor_tensor(out=scrv[:, 1:2], in0=ps_s4[:, 1:2],
                                          in1=bv[:, 1:2], op=ALU.add)
                    if edge:
                        guard(v, ins, "sCV")
                        ins = v.tensor_tensor(out=scr_sm[:], in0=scr_sm[:],
                                              in1=lm_cat[:] if m_global == 0 else rm_cat[:],
                                              op=ALU.add)
                    return ins
                badd = P.step("vector", em_badd,
                              waits=[("sMM", mm_sc), ("sSE", st2.get("exp_se", 0))],
                              inc=("sVE", 1))
                st2["badd_ve"] = badd
                nmv = P.step("vector", lambda v, hhh=hhh: v.reduce_max(
                    out=nm_t[:, hhh * 8:(hhh + 1) * 8],
                    in_=scr_sm.rearrange("p (h w) -> p h w", w=192), axis=AX.X,
                    negate=True), inc=("sVE", 1))
                def em_exp(s, hhh=hhh):
                    ins = None
                    for h8 in range(8):
                        h = 8 * hhh + h8
                        ins = s.activation(out=A_bf[:, h * 192:(h + 1) * 192],
                                           in_=scr_sm[:, h8 * 192:(h8 + 1) * 192],
                                           func=AF.Exp, bias=nm_t[:, h:h + 1],
                                           accum_out=rs_t[:, h:h + 1])
                    return ins
                se_exp2 = P.step("scalar", em_exp, waits=[("sVE", nmv)], inc=("sSE", 1))
                st2["exp_se"] = se_exp2
                def em_scale(v, hhh=hhh):
                    guard(v, v.reciprocal(out=rr_t[:, hhh * 8:(hhh + 1) * 8],
                                          in_=rs_t[:, hhh * 8:(hhh + 1) * 8]), "sCV")
                    ins = None
                    for h8 in range(8):
                        h = 8 * hhh + h8
                        ins = v.tensor_scalar(out=A2_bf[:, h * 192:(h + 1) * 192],
                                              in0=A_bf[:, h * 192:(h + 1) * 192],
                                              scalar1=rr_t[:, h:h + 1], scalar2=None,
                                              op0=ALU.mult)
                    return ins
                ve_sc2 = P.step("vector", em_scale, waits=[("sSE", se_exp2)], inc=("sVE", 1))
                pend = (m, hhh, ve_sc2)
        plan_trav(pend)
        st2["ps01_rd"] = badd
        st2["ps23_rd"] = badd
        st2["ps45_rd"] = st2["at_ve"]
        st2["qk_ve"] = badd

        ve_wb = P.step("vector", lambda v: v.tensor_copy(out=win_b[:], in_=win_f[:]),
                       waits=[("sVE", st2["ps67_rd"]), ("sST", st2["win_st"].get(t - 1, 0))],
                       inc=("sVE", 1))
        t0 = T2 * t
        sw1 = P.step("sync", lambda e, t0=t0: e.dma_start(
            out=winf_scr[:, :, t0:t0 + T2].rearrange("j p t -> p j t"),
            in_=win_f.rearrange("p (j t) -> p j t", j=DC)),
            waits=[("sVE", ve_wb)], inc=("sST", 16))
        sw2 = P.step("sync", lambda e, t0=t0: e.dma_start(
            out=win_scr[:, :, t0:t0 + T2].rearrange("j p t -> p j t"),
            in_=win_b.rearrange("p (j t) -> p j t", j=DC)),
            inc=("sST", 16))
        st2["win_st"][t] = sw2
        st2["winf_st"] = sw1

    if DEV:
        d_t = P.step("sync", lambda e: e.dma_start(
            out=dbg["dbg_win"].rearrange("j p t -> (j p) t"),
            in_=winf_scr.rearrange("j p t -> (j p) t")),
            waits=[("sST", P.val("sST"))], inc=("sOUT", 16))


    PH2_VE = P.val("sVE"); PH2_SE = P.val("sSE"); PH2_MM = P.val("sMM")
    PH2_ST = P.val("sST"); PH2_OUT = P.val("sOUT")
    BAR2 = [("sVE", PH2_VE), ("sSE", PH2_SE), ("sMM", PH2_MM), ("sST", PH2_ST)]

    T3 = 256
    NT3 = SH // T3
    lin_f = af(0, 2048)
    winf3 = af(2048, 2048)
    xn3 = af(4096, 2048)
    s_tm = af(6144, 2048)
    w_ql3 = afb(8192, 4096)
    w_g2 = afb(12288, 4096)
    w_o = afb(16384, 4096)
    elu3a = af(20480, 1024)
    elu3b = af(21504, 1024)
    T_scr = af(22528, 256)
    D_scr = af(22784, 256)
    P_scr = af(23040, 256)
    dr_bc = af(23296, 256)
    dn_raw = af(23552, 256)
    dn_rcp = af(23808, 128)
    bgh = af(23936, 16)
    w_g1 = ab(0, 24576)
    h3_bufs = [ab(24576, 2048), ab(36864, 2048), ab(38912, 2048)]
    q_e3 = ab(26624, 2048)
    win_b3 = ab(28672, 2048)
    lin_b3 = ab(30720, 2048)
    rg3 = ab(32768, 2048)
    mix3 = ab(34816, 2048)
    ps3_q = [PS[:, 0:256], PS[:, 512:768]]
    ps3_n = [PS[:, 1024:1280], PS[:, 1536:1792]]
    ps3_dn = PS[:, 2048:2304]
    ps3_g1 = [PS[:, 2560:2816], PS[:, 2304:2560]]
    ps3_g2 = [PS[:, 1024:1280], PS[:, 1536:1792]]
    ps3_s = PS[:, 3072:4096]

    d_dn = nc.dram_tensor("d_dn", [1, 256], F32, kind="Internal").ap()
    if DEV:
        lin_scr = nc.dram_tensor("lin_scr", [DC, 128, SH], F32, kind="Internal").ap()

    wt3 = P.step("gpsimd", em_wload(w_ql3, "wq_lin"), waits=BAR2, inc=("sWT", 16))
    wt3 = P.step("gpsimd", em_wload(w_g2, "wg2"), inc=("sWT", 16))
    wt3 = P.step("gpsimd", em_wload(w_o, "wo"), inc=("sWT", 16))
    wt3 = P.step("gpsimd", em_wload(w_g1, "wg1"), inc=("sWT", 16))
    PH3_WT = wt3
    ve_bgh = P.step("vector", lambda v: v.tensor_scalar(
        out=bgh[:, 0:DC], in0=c_bias[:, BIDX["bg2"] * DC:(BIDX["bg2"] + 1) * DC],
        scalar1=0.5, scalar2=None, op0=ALU.mult), waits=BAR2, inc=("sVE", 1))

    st3 = {"h_rd": 0, "win_rd": 0, "xn_rd": 0, "dn_rd": 0, "s_st": 0, "lin_rd": 0,
           "h_rd_hist": {}}

    XH3 = P.counts.get("sXH", 0)
    def plan_ldh3(t):
        h3_ = h3_bufs[t % 3]
        P0_ = 128 + T3 * t
        w_ = list(BAR2) + ([("sMM", st3["h_rd_hist"].get(t - 3, 0))] if t >= 3 else [])
        if t > 0:
            w_.append(("sXH", XH3 + 16 * t))
        P.step("sync", lambda e, P0=P0_, h3=h3_: e.dma_start(
            out=h3.rearrange("p (j t) -> p j t", j=DC),
            in_=h_scr[:, :, P0:P0 + T3].rearrange("j p t -> p j t")),
            waits=w_, inc=("sXH", 16))
    plan_ldh3(0)
    for t in range(NT3):
        P0 = 128 + T3 * t
        t0 = T3 * t
        w0 = list(BAR2)
        h3 = h3_bufs[t % 3]
        if t + 1 < NT3:
            plan_ldh3(t + 1)
        xwh = XH3 + 16 * (t + 1)
        ld_wb = P.step("sync", lambda e, t0=t0: e.dma_start(
            out=win_b3.rearrange("p (j t) -> p j t", j=DC),
            in_=win_scr[:, :, t0:t0 + T3].rearrange("j p t -> p j t")),
            waits=([("sMM", st3["h_rd"])] if t > 0 else []), inc=("sX", 16))
        ld_wf = P.step("sync", lambda e, t0=t0: e.dma_start(
            out=winf3.rearrange("p (j t) -> p j t", j=DC),
            in_=winf_scr[:, :, t0:t0 + T3].rearrange("j p t -> p j t")),
            waits=([("sVE", st3["win_rd"])] if t > 0 else []), inc=("sX", 16))
        ld_xn = P.step("sync", lambda e, t0=t0: e.dma_start(
            out=xn3.rearrange("p (c d) -> p c d", c=2),
            in_=xn_scr[t0:t0 + T3, :].rearrange("(c p) d -> p c d", p=128)),
            waits=([("sVE", st3["xn_rd"])] if t > 0 else []), inc=("sX", 16))
        xw3 = P.counts["sX"]

        for j in range(DC):
            pq = ps3_q[j % 2]
            def em_q3(tn, j=j, pq=pq, h3=h3):
                ins = None
                for di in range(DC):
                    ins = tn.matmul(out=pq[:],
                                    lhsT=w_ql3[:, di * D + j * 128: di * D + (j + 1) * 128],
                                    rhs=h3[:, di * T3:(di + 1) * T3],
                                    start=(di == 0), stop=(di == DC - 1))
                return ins
            mm_q = P.step("tensor", em_q3,
                          waits=[("sXH", xwh), ("sWT", PH3_WT),
                                 ("sVE", st3.get(f"q3ev{j-2}", st3.get("ps01b_rd", 0)))],
                          inc=("sMM", 1))
            def em_emin(v, j=j, pq=pq):
                return v.tensor_scalar(out=elu3a[:, 0:T3], in0=pq[:],
                                       scalar1=bias_col("bq_lin", j), scalar2=0.0,
                                       op0=ALU.add, op1=ALU.min)
            ve_mn = P.step("vector", em_emin, waits=[("sMM", mm_q)], inc=("sVE", 1))
            se_ex = P.step("scalar", lambda s: s.activation(
                out=elu3a[:, 0:T3], in_=elu3a[:, 0:T3], func=AF.Exp),
                waits=[("sVE", ve_mn)], inc=("sSE", 1))
            def em_emax(v, j=j, pq=pq):
                guard(v, v.tensor_scalar(out=elu3b[:, 0:T3], in0=pq[:],
                                         scalar1=bias_col("bq_lin", j), scalar2=0.0,
                                         op0=ALU.add, op1=ALU.max), "sCV")
                return v.tensor_tensor(out=q_e3[:, j * T3:(j + 1) * T3],
                                       in0=elu3b[:, 0:T3], in1=elu3a[:, 0:T3], op=ALU.add)
            ve_qe = P.step("vector", em_emax, waits=[("sSE", se_ex)], inc=("sVE", 1))
            st3[f"q3ev{j}"] = ve_qe
        qe_done = ve_qe

        def em_dn(tn):
            ins = None
            for dc in range(DC):
                ins = tn.matmul(out=ps3_dn[0:1, :],
                                lhsT=ksum_bf[:, dc:dc + 1],
                                rhs=q_e3[:, dc * T3:(dc + 1) * T3],
                                start=(dc == 0), stop=(dc == DC - 1))
            return ins
        mm_dn = P.step("tensor", em_dn,
                       waits=[("sVE", qe_done), ("sVE", st3["dn_rd"])], inc=("sMM", 1))
        def em_dnr(v):
            guard(v, v.tensor_scalar(out=dn_raw[0:1, 0:T3], in0=ps3_dn[0:1, :],
                                     scalar1=1e-6, scalar2=None, op0=ALU.add), "sCV")
            return v.reciprocal(out=dn_rcp[0:1, 0:T3//2] if False else dn_raw[1:2, 0:T3] if False else dn_rcp[0:1, 0:128],
                                in_=dn_raw[0:1, 0:128])
        def em_dnr(v):
            guard(v, v.tensor_scalar(out=dn_raw[0:1, 0:T3], in0=ps3_dn[0:1, :],
                                     scalar1=1e-6, scalar2=None, op0=ALU.add), "sCV")
            return v.reciprocal(out=dr_bc[0:1, 0:T3], in_=dn_raw[0:1, 0:T3])
        ve_dn = P.step("vector", em_dnr, waits=[("sMM", mm_dn)], inc=("sVE", 1))
        st3["dn_rd"] = ve_dn
        dm_dn = P.step("sync", lambda e: e.dma_start(out=d_dn, in_=dr_bc[0:1, 0:T3]),
                       waits=[("sVE", ve_dn)], inc=("sDM", 16))
        dm_db = P.step("sync", lambda e: e.dma_start(
            out=dr_bc[:], in_=d_dn.broadcast_to((128, T3))),
            waits=[("sDM", dm_dn)], inc=("sDM", 16))

        for ec in range(DC):
            pn = ps3_n[ec % 2]
            def em_n3(tn, ec=ec, pn=pn):
                ins = None
                for dc in range(DC):
                    ins = tn.matmul(out=pn[:],
                                    lhsT=kv_bf[:, dc * D + ec * 128: dc * D + (ec + 1) * 128],
                                    rhs=q_e3[:, dc * T3:(dc + 1) * T3],
                                    start=(dc == 0), stop=(dc == DC - 1))
                return ins
            mm_n = P.step("tensor", em_n3,
                          waits=[("sVE", qe_done),
                                 ("sVE", st3.get(f"l3ev{ec-2}", st3.get("ps23b_rd", 0))),
                                 ("sSE", st3.get("g2T_se", 0))],
                          inc=("sMM", 1))
            def em_lin(v, ec=ec, pn=pn):
                guard(v, v.tensor_tensor(out=lin_f[:, ec * T3:(ec + 1) * T3],
                                         in0=pn[:], in1=dr_bc[:], op=ALU.mult), "sCV")
                return v.tensor_copy(out=lin_b3[:, ec * T3:(ec + 1) * T3],
                                     in_=lin_f[:, ec * T3:(ec + 1) * T3])
            ve_l = P.step("vector", em_lin,
                          waits=[("sMM", mm_n), ("sDM", dm_db),
                                 ("sVE", st3["lin_rd"] if ec == 0 else 0)],
                          inc=("sVE", 1))
            st3[f"l3ev{ec}"] = ve_l
        lin_done = ve_l

        for j in range(DC):
            pg = ps3_g1[j % 2]
            def em_g1(tn, j=j, pg=pg, h3=h3):
                ins = None
                srcs = [lin_b3, win_b3, h3]
                for rc in range(24):
                    ins = tn.matmul(out=pg[:],
                                    lhsT=w_g1[:, rc * D + j * 128: rc * D + (j + 1) * 128],
                                    rhs=srcs[rc // 8][:, (rc % 8) * T3:(rc % 8 + 1) * T3],
                                    start=(rc == 0), stop=(rc == 23))
                return ins
            mm_g1 = P.step("tensor", em_g1,
                           waits=[("sVE", lin_done), ("sX", xw3),
                                  ("sVE", st3.get(f"g1ev{j-2}", st3.get("ps45b_rd", 0)))],
                           inc=("sMM", 1))
            ve_g1 = P.step("vector", lambda v, j=j, pg=pg: v.tensor_scalar(
                out=rg3[:, j * T3:(j + 1) * T3], in0=pg[:],
                scalar1=bias_col("bg1", j), scalar2=0.0, op0=ALU.add, op1=ALU.max),
                waits=[("sMM", mm_g1)], inc=("sVE", 1))
            st3[f"g1ev{j}"] = ve_g1
        st3["h_rd"] = mm_g1
        st3["h_rd_hist"][t] = mm_g1

        for j in range(DC):
            pg2 = ps3_g2[j % 2]
            def em_g2(tn, j=j, pg2=pg2):
                ins = None
                for di in range(DC):
                    ins = tn.matmul(out=pg2[:],
                                    lhsT=w_g2[:, di * D + j * 128: di * D + (j + 1) * 128],
                                    rhs=rg3[:, di * T3:(di + 1) * T3],
                                    start=(di == 0), stop=(di == DC - 1))
                return ins
            mm_g2 = P.step("tensor", em_g2,
                           waits=[("sVE", st3[f"g1ev{DC-1}"]),
                                  ("sVE", st3.get(f"mxev{j-2}", 0)),
                                  ("sVE", st3.get(f"l3ev{j}", 0) if t == 0 else 0)],
                           inc=("sMM", 1))
            se_T = P.step("scalar", lambda s, j=j, pg2=pg2: s.activation(
                out=T_scr[:, 0:T3], in_=pg2[:], func=AF.Tanh,
                bias=bgh[:, j:j + 1], scale=0.5),
                waits=[("sMM", mm_g2), ("sVE", ve_bgh),
                       ("sVE", st3.get(f"mxev{j-1}", 0))],
                inc=("sSE", 1))
            st3["g2T_se"] = se_T
            def em_mix(v, j=j):
                guard(v, v.tensor_tensor(out=D_scr[:, 0:T3],
                                         in0=lin_f[:, j * T3:(j + 1) * T3],
                                         in1=winf3[:, j * T3:(j + 1) * T3],
                                         op=ALU.subtract), "sCV")
                guard(v, v.tensor_tensor(out=P_scr[:, 0:T3], in0=T_scr[:, 0:T3],
                                         in1=D_scr[:, 0:T3], op=ALU.mult), "sCV")
                guard(v, v.tensor_tensor(out=D_scr[:, 0:T3], in0=D_scr[:, 0:T3],
                                         in1=P_scr[:, 0:T3], op=ALU.add), "sCV")
                return v.scalar_tensor_tensor(out=mix3[:, j * T3:(j + 1) * T3],
                                              in0=D_scr[:, 0:T3], scalar=0.5,
                                              in1=winf3[:, j * T3:(j + 1) * T3],
                                              op0=ALU.mult, op1=ALU.add)
            ve_mx = P.step("vector", em_mix, waits=[("sSE", se_T)], inc=("sVE", 1))
            st3[f"mxev{j}"] = ve_mx
        st3["win_rd"] = ve_mx

        for tc in range(2):
            def em_s(tn, tc=tc):
                ins = None
                for di in range(DC):
                    for half in range(2):
                        ins = tn.matmul(
                            out=ps3_s[:, half * 512:(half + 1) * 512],
                            lhsT=mix3[:, di * T3 + tc * 128: di * T3 + tc * 128 + 128],
                            rhs=w_o[:, di * D + half * 512: di * D + (half + 1) * 512],
                            start=(di == 0), stop=(di == DC - 1))
                return ins
            mm_s = P.step("tensor", em_s,
                          waits=[("sVE", st3[f"mxev{DC-1}"]),
                                 ("sVE", st3.get("s_ev", 0))],
                          inc=("sMM", 1))
            def em_sev(v, tc=tc):
                guard(v, v.tensor_tensor(out=s_tm[:, tc * D:(tc + 1) * D],
                                         in0=ps3_s[:], in1=c_bo_bc[:], op=ALU.add), "sCV")
                return v.tensor_tensor(out=s_tm[:, tc * D:(tc + 1) * D],
                                       in0=s_tm[:, tc * D:(tc + 1) * D],
                                       in1=xn3[:, tc * D:(tc + 1) * D], op=ALU.add)
            ve_s = P.step("vector", em_sev,
                          waits=[("sMM", mm_s), ("sST", st3["s_st"] if tc == 0 else 0)],
                          inc=("sVE", 1))
            st3["s_ev"] = ve_s
        st3["xn_rd"] = ve_s
        st3["ps01b_rd"] = st3[f"q3ev{DC-1}"]
        st3["ps23b_rd"] = st3[f"mxev{DC-1}"]
        st3["ps45b_rd"] = st3[f"g1ev{DC-1}"]
        st3["lin_rd"] = st3[f"mxev{DC-1}"]

        st_s = P.step("sync", lambda e, t0=t0: e.dma_start(
            out=s_scr[t0:t0 + T3, :].rearrange("(c p) d -> p c d", p=128),
            in_=s_tm.rearrange("p (c d) -> p c d", c=2)),
            waits=[("sVE", ve_s)], inc=("sST", 16))
        st3["s_st"] = st_s
        if DEV:
            st_l = P.step("sync", lambda e, t0=t0: e.dma_start(
                out=lin_scr[:, :, t0:t0 + T3].rearrange("j p t -> p j t"),
                in_=lin_f.rearrange("p (j t) -> p j t", j=DC)),
                waits=[("sVE", lin_done)], inc=("sST", 16))

    if DEV:
        d_t = P.step("sync", lambda e: e.dma_start(
            out=dbg["dbg_lin"].rearrange("j p t -> (j p) t"),
            in_=lin_scr.rearrange("j p t -> (j p) t")),
            waits=[("sST", P.val("sST"))], inc=("sOUT", 16))
        d_t = P.step("sync", lambda e: e.dma_start(out=dbg["dbg_s"], in_=s_scr),
                     waits=[("sST", P.val("sST"))], inc=("sOUT", 16))


    PH3_VE = P.val("sVE"); PH3_SE = P.val("sSE"); PH3_MM = P.val("sMM")
    PH3_ST = P.val("sST")
    BAR3 = [("sVE", PH3_VE), ("sSE", PH3_SE), ("sMM", PH3_MM), ("sST", PH3_ST)]

    T4 = 1024
    NP4 = SH // T4
    CAP = 256
    NSLOT = NE * CAP
    IBLK = 256
    NIB = DI // IBLK
    SENT = 10000000

    sc1 = af(8192, 1024)
    sc2 = af(9216, 1024)
    sc3 = af(10240, 1024)
    silu4 = af(11264, 1024)
    sc_t4 = [silu4[:, 0:256], silu4[:, 512:768]]
    sc_h4 = [silu4[:, 256:512], silu4[:, 768:1024]]
    rc_tm = af(12288, 2048)
    rc_fm = af(14336, 2048)
    sd_stage = af(12288, 4096)
    smb = 16384
    lg4 = af(smb, 64)
    r_mx = af(smb + 64, 8); r_w = af(smb + 72, 8); r_eq = af(smb + 80, 8)
    r_t = af(smb + 88, 8); r_u = af(smb + 96, 8); r_v = af(smb + 104, 8)
    r_oh = af(smb + 112, 8)
    w_all4 = af(smb + 128, 64)
    oh_f = af(smb + 192, 64)
    oh_b = afb(smb + 256, 32)
    pscan_sb = af(smb + 288, 64)
    cs_row = af(smb + 352, 64)
    base_bc = af(smb + 416, 64)
    g_t1 = af(smb + 480, 64)
    g_t2 = af(smb + 544, 64)
    e2c = af(smb + 608, 64)
    gidx_f = af(smb + 672, 8)
    w_tok = af(smb + 680, 8)
    gidx_i = af(smb + 688, 8).bitcast(I32)
    tid_i = af(smb + 696, 8).bitcast(I32)
    wslot_sb = af(smb + 704, 16)
    tos_sb = af(smb + 720, 16).bitcast(I32)
    cs_ex = af(smb + 736, 64)
    cs_a = af(smb + 800, 64)
    cs_b = af(smb + 864, 64)
    srow = af(17408, 1024)
    x_c = af(18432, 1024)
    s_c = af(19456, 1024)
    xn_c = af(20480, 1024)
    acc_tc = [af(21504, 1024), af(22528, 1024)]
    shB = af(23552, 512)
    nrm4 = ab(0, 8192)
    xg_bufs = [ab(28736, 2048), ab(30784, 2048)]
    xg_fm = ab(10240, 2048)
    hm4 = ab(12288, 4096)
    hm_sh = ab(8192, 8192)
    g_sb4 = [ab(16384, 2048), ab(18432, 2048)]
    u_sb4 = [ab(20480, 2048), ab(22528, 2048)]
    d_sb4 = [ab(24576, 2048), ab(26624, 2048)]
    rw_b = ab(28672, 64)
    lg_ps = PS[:, 2048:2056]
    pp_ps = PS[:, 3584:3648]
    cs_ps = PS[:, 3648:3712]
    ps_yg = [PS[:, 2048:2304], PS[:, 2560:2816]]
    ps_yu = [PS[:, 3072:3328], PS[:, 3584:3840]]
    tr_ps2 = [PS[:, 2304:2560].bitcast(BF16), PS[:, 2816:3072].bitcast(BF16)]
    dn_tm = [PS[:, 0:1024], PS[:, 1024:2048]]
    sh_g = [PS[:, 0:1024], PS[:, 2048:3072]]
    sh_u = [PS[:, 1024:2048], PS[:, 3072:4096]]
    trS_ps = PS[:, 0:1024]

    tos_scr = nc.dram_tensor("tos_scr", [NSLOT, 1], I32, kind="Internal").ap()
    wslot_scr = nc.dram_tensor("wslot_scr", [NSLOT, 1], F32, kind="Internal").ap()
    rc_scr = nc.dram_tensor("rc_scr", [NSLOT, D], F32, kind="Internal").ap()
    if DEV:
        moe2_scr = nc.dram_tensor("moe2_scr", [SH, D], F32, kind="Internal").ap()

    wt4 = P.step("gpsimd", lambda g: g.dma_start(
        out=rw_b.rearrange("p (i n) -> p i n", i=DC),
        in_=ap["router_w"].rearrange("(i p) n -> p i n", p=128)),
        waits=BAR3, inc=("sWT", 16))
    PH4_WT0 = wt4
    def em_ltmat(g):
        guard(g, g.iota(
            out=c_lt[:], pattern=[[1, 128]], base=0, channel_multiplier=-1,
            allow_small_or_imprecise_dtypes=True), "sCG")
        return g.tensor_single_scalar(
            out=c_lt[:], in_=c_lt[:], scalar=0.0, op=ALU.is_ge)
    gp_lt = P.step("gpsimd", em_ltmat, waits=BAR3, inc=("sGP", 1))
    def em_e2tile(g):
        guard(g, g.iota(
            out=e2c[:, 0:8], pattern=[[2, 8]], base=0, channel_multiplier=0,
            allow_small_or_imprecise_dtypes=True), "sCG")
        ins = None
        for t_ in range(1, 8):
            ins = g.tensor_copy(out=e2c[:, t_ * 8:(t_ + 1) * 8], in_=e2c[:, 0:8])
        return ins
    gp_e2 = P.step("gpsimd", em_e2tile, inc=("sGP", 1))
    ve_sro = P.step("vector", lambda v: v.memset(srow[0:1, :].bitcast(I32), SENT),
                    waits=BAR3, inc=("sVE", 1))
    ve_srz = P.step("vector", lambda v: v.memset(srow[1:2, :] if False else sc1[0:1, :], 0.0),
                    inc=("sVE", 1))
    dm_wz = P.step("sync", lambda e: e.dma_start(
        out=wslot_scr.rearrange("a o -> o a")[:, 0:1024], in_=sc1[0:1, :]),
        waits=[("sVE", ve_srz)] + BAR3, inc=("sDM", 16))
    dm_wz = P.step("sync", lambda e: e.dma_start(
        out=wslot_scr.rearrange("a o -> o a")[:, 1024:2048], in_=sc1[0:1, :]),
        inc=("sDM", 16))
    PH4_DM0 = dm_wz

    st4 = {"nrm_rd": 0, "lg_rd": 0, "acc_last": 0, "gu_rd": {}, "d_rd": {},
           "xg_rd_par": {}, "stA": {},
           "y_st": 0, "fin_ve": 0, "fin_se": 0, "idx_rd": 0, "tosld": 0,
           "rc_st": 0, "xg_rd": 0, "hm_rd": 0, "sd_ve": 0, "shg_rd": {}}

    for p4 in range(NP4):
        t0 = T4 * p4
        ld_n = P.step("sync", lambda e, t0=t0: e.dma_start(
            out=nrm4.rearrange("p (j t) -> p j t", j=DC),
            in_=nrm_scr[:, :, t0:t0 + T4].rearrange("j p t -> p j t")),
            waits=BAR3 + ([("sMM", st4["nrm_rd"])] if p4 > 0 else []), inc=("sX", 16))
        xw4 = P.counts["sX"]

        for tc in range(8):
            def em_rt(tn, tc=tc):
                ins = None
                for di in range(DC):
                    ins = tn.matmul(out=lg_ps[:],
                                    lhsT=nrm4[:, di * T4 + tc * 128: di * T4 + tc * 128 + 128],
                                    rhs=rw_b[:, di * 8:(di + 1) * 8],
                                    start=(di == 0), stop=(di == DC - 1))
                return ins
            mm_rt = P.step("tensor", em_rt,
                           waits=[("sX", xw4), ("sWT", PH4_WT0), ("sVE", st4["lg_rd"]),
                                  ("sVE", st4["acc_last"])],
                           inc=("sMM", 1))
            def em_route(v, tc=tc):
                guard(v, v.tensor_copy(out=lg4[:, tc * 8:(tc + 1) * 8], in_=lg_ps[:]), "sCV")
                guard(v, v.reduce_max(out=r_mx[:, 0:1], in_=lg4[:, tc * 8:(tc + 1) * 8],
                                      axis=AX.X), "sCV")
                guard(v, v.tensor_scalar(out=r_eq[:, 0:8], in0=lg4[:, tc * 8:(tc + 1) * 8],
                                         scalar1=r_mx[:, 0:1], scalar2=None,
                                         op0=ALU.is_equal), "sCV")
                guard(v, v.tensor_copy(out=r_t[:, 0:1], in_=r_eq[:, 0:1]), "sCV")
                guard(v, v.tensor_tensor(out=r_t[:, 1:8], in0=r_eq[:, 1:8],
                                         in1=r_eq[:, 0:7], op=ALU.add), "sCV")
                guard(v, v.tensor_copy(out=r_u[:, 0:2], in_=r_t[:, 0:2]), "sCV")
                guard(v, v.tensor_tensor(out=r_u[:, 2:8], in0=r_t[:, 2:8],
                                         in1=r_t[:, 0:6], op=ALU.add), "sCV")
                guard(v, v.tensor_copy(out=r_v[:, 0:4], in_=r_u[:, 0:4]), "sCV")
                guard(v, v.tensor_tensor(out=r_v[:, 4:8], in0=r_u[:, 4:8],
                                         in1=r_u[:, 0:4], op=ALU.add), "sCV")
                guard(v, v.tensor_scalar(out=r_v[:, 0:8], in0=r_v[:, 0:8],
                                         scalar1=1.0, scalar2=None, op0=ALU.is_equal), "sCV")
                guard(v, v.tensor_tensor(out=r_oh[:, 0:8], in0=r_v[:, 0:8],
                                         in1=r_eq[:, 0:8], op=ALU.mult), "sCV")
                guard(v, v.tensor_copy(out=oh_f[:, tc * 8:(tc + 1) * 8],
                                       in_=r_oh[:, 0:8]), "sCV")
                return v.tensor_copy(out=oh_b[:, tc * 8:(tc + 1) * 8], in_=r_oh[:, 0:8])
            ve_rt = P.step("vector", em_route, waits=[("sMM", mm_rt)], inc=("sVE", 1))
            st4["lg_rd"] = ve_rt
            se_sg = P.step("scalar", lambda s: s.activation(
                out=r_w[:, 0:1], in_=r_mx[:, 0:1], func=AF.Tanh, scale=0.5),
                waits=[("sVE", ve_rt)], inc=("sSE", 1))
            def em_wsc(v, tc=tc):
                guard(v, v.tensor_scalar(out=r_w[:, 1:2], in0=r_w[:, 0:1],
                                         scalar1=0.5, scalar2=0.5,
                                         op0=ALU.mult, op1=ALU.add), "sCV")
                return v.tensor_scalar(out=w_all4[:, tc * 8:(tc + 1) * 8],
                                       in0=oh_f[:, tc * 8:(tc + 1) * 8],
                                       scalar1=r_w[:, 1:2], scalar2=None, op0=ALU.mult)
            ve_ws = P.step("vector", em_wsc, waits=[("sSE", se_sg)], inc=("sVE", 1))

        def em_pp(tn):
            tn.matmul(out=pp_ps[:], lhsT=c_lt[:], rhs=oh_b[:], start=True, stop=True)
            return tn.matmul(out=cs_ps[0:1, :], lhsT=c_ones[:, 0:1], rhs=oh_b[:],
                             start=True, stop=True)
        mm_pp = P.step("tensor", em_pp,
                       waits=[("sVE", ve_ws), ("sGP", gp_lt)],
                       inc=("sMM", 1))
        def em_gidx(v):
            guard(v, v.tensor_copy(out=pscan_sb[:], in_=pp_ps[:]), "sCV")
            guard(v, v.tensor_copy(out=cs_row[0:1, :], in_=cs_ps[0:1, :]), "sCV")
            guard(v, v.tensor_copy(out=cs_a[0:1, 0:8], in_=cs_row[0:1, 0:8]), "sCV")
            guard(v, v.tensor_tensor(out=cs_a[0:1, 8:64], in0=cs_row[0:1, 8:64],
                                     in1=cs_row[0:1, 0:56], op=ALU.add), "sCV")
            guard(v, v.tensor_copy(out=cs_b[0:1, 0:16], in_=cs_a[0:1, 0:16]), "sCV")
            guard(v, v.tensor_tensor(out=cs_b[0:1, 16:64], in0=cs_a[0:1, 16:64],
                                     in1=cs_a[0:1, 0:48], op=ALU.add), "sCV")
            guard(v, v.tensor_copy(out=cs_row[0:1, 0:32], in_=cs_b[0:1, 0:32]), "sCV")
            guard(v, v.tensor_tensor(out=cs_row[0:1, 32:64], in0=cs_b[0:1, 32:64],
                                     in1=cs_b[0:1, 0:32], op=ALU.add), "sCV")
            return v.tensor_tensor(out=cs_ex[0:1, 0:64], in0=cs_row[0:1, 0:64],
                                   in1=cs_ps[0:1, :], op=ALU.subtract)
        ve_gx0 = P.step("vector", em_gidx, waits=[("sMM", mm_pp)], inc=("sVE", 1))
        dm_ts = P.step("sync", lambda e: e.dma_start(
            out=tos_scr.rearrange("a o -> o a")[:, 0:1024],
            in_=srow[0:1, :].bitcast(I32)),
            waits=[("sVE", ve_sro), ("sST", st4["tosld"])], inc=("sDM", 16))
        dm_ts = P.step("sync", lambda e: e.dma_start(
            out=tos_scr.rearrange("a o -> o a")[:, 1024:2048],
            in_=srow[0:1, :].bitcast(I32)), inc=("sDM", 16))
        dm_bb = P.step("sync", lambda e: e.dma_start(out=d_dn[0:1, 0:64],
                                                     in_=cs_ex[0:1, 0:64]),
                       waits=[("sVE", ve_gx0)], inc=("sDM", 16))
        dm_bb = P.step("sync", lambda e: e.dma_start(
            out=base_bc[:], in_=d_dn[0:1, 0:64].broadcast_to((128, 64))),
            waits=[("sDM", dm_bb)], inc=("sDM", 16))
        def em_gidx2(v):
            guard(v, v.scalar_tensor_tensor(out=g_t1[:], in0=pscan_sb[:], scalar=-1.0,
                                            in1=base_bc[:], op0=ALU.add, op1=ALU.add), "sCV")
            guard(v, v.tensor_single_scalar(out=g_t2[:], in_=g_t1[:], scalar=128.0,
                                            op=ALU.is_ge), "sCV")
            guard(v, v.scalar_tensor_tensor(out=g_t1[:], in0=g_t2[:], scalar=-128.0,
                                            in1=g_t1[:], op0=ALU.mult, op1=ALU.add), "sCV")
            guard(v, v.scalar_tensor_tensor(out=g_t2[:], in0=g_t1[:], scalar=16.0,
                                            in1=g_t2[:], op0=ALU.mult, op1=ALU.add), "sCV")
            guard(v, v.tensor_tensor(out=g_t2[:], in0=g_t2[:], in1=e2c[:],
                                     op=ALU.add), "sCV")
            guard(v, v.tensor_tensor(out=g_t2[:], in0=g_t2[:], in1=oh_f[:],
                                     op=ALU.mult), "sCV")
            guard(v, v.reduce_sum(out=gidx_f[:], in_=g_t2[:]
                                  .rearrange("p (t e) -> p t e", e=8), axis=AX.X), "sCV")
            guard(v, v.tensor_copy(out=gidx_i[:], in_=gidx_f[:]), "sCV")
            return v.reduce_sum(out=w_tok[:], in_=w_all4[:]
                                .rearrange("p (t e) -> p t e", e=8), axis=AX.X)
        if DEV and p4 == 0:
            pass
        ve_gx = P.step("vector", em_gidx2,
                       waits=[("sDM", dm_bb), ("sGP", gp_e2),
                              ("sWT", st4.get("ga_last", 0)),
                              ("sWT", st4.get("scat_last", 0))], inc=("sVE", 1))
        if DEV and p4 == 0:
            P.step("sync", lambda e: e.dma_start(out=dbg["dbg_route"][:, 0:8], in_=gidx_f[:]),
                   waits=[("sVE", ve_gx)], inc=("sOUT", 16))
            P.step("sync", lambda e: e.dma_start(out=dbg["dbg_route"][:, 8:16], in_=w_tok[:]),
                   inc=("sOUT", 16))
            P.step("sync", lambda e: e.dma_start(out=dbg["dbg_route"][:, 16:24],
                                                 in_=lg4[:, 0:8]), inc=("sOUT", 16))
            P.step("sync", lambda e: e.dma_start(out=dbg["dbg_route"][:, 24:32],
                                                 in_=oh_f[:, 0:8]), inc=("sOUT", 16))
        gp_tid = P.step("gpsimd", lambda g, t0=t0: g.iota(
            out=tid_i[:], pattern=[[128, 8]], base=t0, channel_multiplier=1),
            waits=[("sWT", st4.get("scat_last", 0))], inc=("sGP", 1))
        sc_t = dm_bb
        for tc in range(8):
            sc_t = P.step("gpsimd", lambda g, tc=tc: g.indirect_dma_start(
                out=tos_scr, out_offset=IndirectOffsetOnAxis(ap=gidx_i[:, tc:tc + 1], axis=0),
                in_=tid_i[:, tc:tc + 1], in_offset=None),
                waits=([("sVE", ve_gx), ("sGP", gp_tid), ("sDM", sc_t)] if tc == 0 else []),
                inc=("sWT", 16))
            sc_t = P.step("gpsimd", lambda g, tc=tc: g.indirect_dma_start(
                out=wslot_scr, out_offset=IndirectOffsetOnAxis(ap=gidx_i[:, tc:tc + 1], axis=0),
                in_=w_tok[:, tc:tc + 1], in_offset=None),
                waits=([("sDM", PH4_DM0)] if tc == 0 and p4 == 0 else []),
                inc=("sWT", 16))
        ld_tw = P.step("sync", lambda e: e.dma_start(
            out=tos_sb[:].rearrange("p (c o) -> p c o", o=1),
            in_=tos_scr.rearrange("(p c) o -> p c o", c=16)),
            waits=[("sWT", sc_t)], inc=("sST", 16))
        ld_tw = P.step("sync", lambda e: e.dma_start(
            out=wslot_sb[:].rearrange("p (c o) -> p c o", o=1),
            in_=wslot_scr.rearrange("(p c) o -> p c o", c=16)),
            inc=("sST", 16))
        st4["tosld"] = ld_tw

        def stage_A(e_):
            xge = xg_bufs[e_ % 2]
            ve_z = P.step("vector", lambda v, xge=xge: v.memset(xge[:], 0.0),
                          waits=[("sMM", st4["xg_rd_par"].get(e_ % 2, 0)),
                                 ("sMM", st4["xg_rd"] if e_ == 0 else 0)],
                          inc=("sVE", 1))
            g_g = P.step("gpsimd", lambda g, e_=e_, xge=xge: g.indirect_dma_start(
                out=xge[:, 0:D], out_offset=None, in_=n_tm_scr,
                in_offset=IndirectOffsetOnAxis(ap=tos_sb[:, 2 * e_:2 * e_ + 1], axis=0)),
                waits=[("sST", st4["tosld"]), ("sVE", ve_z)], inc=("sWT", 16))
            g_g = P.step("gpsimd", lambda g, e_=e_, xge=xge: g.indirect_dma_start(
                out=xge[:, D:2 * D], out_offset=None, in_=n_tm_scr,
                in_offset=IndirectOffsetOnAxis(ap=tos_sb[:, 2 * e_ + 1:2 * e_ + 2], axis=0)),
                inc=("sWT", 16))
            def em_xs(v, e_=e_, xge=xge):
                ins = None
                for cc in range(2):
                    guard(v, v.tensor_tensor(out=sc2[:], in0=xge[:, cc * D:(cc + 1) * D],
                                             in1=c_flnw_bc[:], op=ALU.mult), "sCV")
                    guard(v, v.tensor_tensor(out=sc2[:], in0=sc2[:], in1=c_flnb_bc[:],
                                             op=ALU.add), "sCV")
                    ins = v.tensor_scalar(out=xge[:, cc * D:(cc + 1) * D], in0=sc2[:],
                                          scalar1=wslot_sb[:, 2 * e_ + cc:2 * e_ + cc + 1],
                                          scalar2=None, op0=ALU.mult)
                    if cc == 0:
                        guard(v, ins, "sCV")
                return ins
            ve_xs = P.step("vector", em_xs,
                           waits=[("sWT", g_g), ("sVE", st4["fin_ve"]) if e_ == 0 else
                                  ("sVE", 0)],
                           inc=("sVE", 1))
            st4["stA"][e_] = (xge, ve_xs)
        stage_A(0)
        for e_ in range(NE):
            xge, ve_xs = st4["stA"][e_]
            if DEV and p4 == 0 and e_ == 0:
                P.step("sync", lambda e, xge=xge: e.dma_start(out=dbg["dbg_xg"], in_=xge[:]),
                       waits=[("sVE", ve_xs)], inc=("sOUT", 16))
            def em_xt(tn, cc):
                def f(tn_, cc=cc, xge=xge):
                    ins = None
                    for dj in range(DC):
                        tp = tr_ps2[dj // 4]
                        ins = tn_.transpose(
                            out=tp[:, (dj % 4) * 128:(dj % 4 + 1) * 128],
                            in_=xge[:, cc * D + dj * 128: cc * D + (dj + 1) * 128],
                            identity=c_id_bf[:])
                    return ins
                return f
            ve_xf = 0
            for cc in range(2):
                mm_xt = P.step("tensor", em_xt(None, cc),
                               waits=[("sVE", ve_xs), ("sGP", gp_id),
                                      ("sVE", st4.get("trp_rd", 0)),
                                      ("sVE", st4.get("hm_prev", 0)),
                                      ("sVE", st4.get("sd_ve", 0) if e_ == 0 else 0)],
                               inc=("sMM", 1))
                def em_xf(v, cc=cc):
                    xv = xg_fm.rearrange("p (dj s) -> p dj s", s=CAP)
                    guard(v, v.tensor_copy(
                        out=xv[:, 0:4, cc * 128:(cc + 1) * 128],
                        in_=tr_ps2[0].rearrange("p (dj s) -> p dj s", s=128)), "sCV")
                    return v.tensor_copy(
                        out=xv[:, 4:8, cc * 128:(cc + 1) * 128],
                        in_=tr_ps2[1].rearrange("p (dj s) -> p dj s", s=128))
                ve_xf = P.step("vector", em_xf,
                    waits=[("sMM", mm_xt), ("sMM", st4.get("yglast", 0))],
                    inc=("sVE", 1))
                st4["trp_rd"] = ve_xf
            st4["xg_rd_par"][e_ % 2] = mm_xt

            dn_t = {}
            hm_t = {}
            ldd_t = {}
            def plan_dn(j, e_=e_):
                ibd, i2 = j // 2, j % 2
                dpar = ibd % 2
                def em_dnj(tn, j=j, i2=i2, dpar=dpar):
                    ins = None
                    for c in range(2):
                        for hlf in range(2):
                            ins = tn.matmul(
                                out=dn_tm[c][:, hlf * 512:(hlf + 1) * 512],
                                lhsT=hm4[:, j * CAP + c * 128: j * CAP + c * 128 + 128],
                                rhs=d_sb4[dpar][:, i2 * D + hlf * 512: i2 * D + (hlf + 1) * 512],
                                start=(j == 0), stop=(j == 15))
                    return ins
                w = [("sVE", hm_t[j]), ("sWS", ldd_t[ibd])]
                if j == 0:
                    w += [("sVE", st4.get("rcf_rd", 0)),
                          ("sVE", st4.get("sd_ve", 0) if e_ == 0 else 0),
                          ("sVE", st4.get("fintr_rd", 0) if e_ == 0 else 0)]
                dn_t[j] = P.step("tensor", em_dnj, waits=w, inc=("sMM", 1))
                st4["d_rd"][dpar] = dn_t[j]
            for ib in range(NIB):
                par = ib % 2
                def em_ldg(g, e_=e_, ib=ib, par=par):
                    srcw = gwb[e_].rearrange("(i p) n -> p i n", p=128)[
                        :, :, ib * IBLK:(ib + 1) * IBLK]
                    return g.dma_start(out=g_sb4[par].rearrange("p (i n) -> p i n", i=DC),
                                       in_=srcw)
                def em_ldu(g, e_=e_, ib=ib, par=par):
                    srcw = uwb[e_].rearrange("(i p) n -> p i n", p=128)[
                        :, :, ib * IBLK:(ib + 1) * IBLK]
                    return g.dma_start(out=u_sb4[par].rearrange("p (i n) -> p i n", i=DC),
                                       in_=srcw)
                wld = [("sMM", st4["gu_rd"].get(par, 0)), ("sPC", N_PC)]
                if par not in st4["gu_rd"]:
                    wld = wld + BAR3
                ldg = P.step("gpsimd", em_ldg, waits=wld, inc=("sWS", 16))
                ldu = P.step("gpsimd", em_ldu, inc=("sWS", 16))
                def em_ldd(g, e_=e_, ib=ib):
                    srcw = dwb[e_].rearrange("(i p) n -> p i n", p=128)[
                        :, 2 * ib:2 * ib + 2, :]
                    return g.dma_start(out=d_sb4[ib % 2].rearrange("p (i n) -> p i n", i=2),
                                       in_=srcw)
                wldd = [("sMM", st4["d_rd"].get(ib % 2, 0))]
                if ib % 2 not in st4["d_rd"]:
                    wldd = wldd + BAR3
                ldd_t[ib] = P.step("gpsimd", em_ldd, waits=wldd, inc=("sWS", 16))
                for ic in range(IBLK // 128):
                    icg = ib * (IBLK // 128) + ic
                    ipar = icg % 2
                    def em_yg(tn, ic=ic, par=par, ipar=ipar):
                        ins = None
                        for di in range(DC):
                            ins = tn.matmul(
                                out=ps_yg[ipar][:],
                                lhsT=g_sb4[par][:, di * IBLK + ic * 128: di * IBLK + ic * 128 + 128],
                                rhs=xg_fm[:, di * CAP:(di + 1) * CAP],
                                start=(di == 0), stop=(di == DC - 1))
                        return ins
                    def em_yu(tn, ic=ic, par=par, ipar=ipar):
                        ins = None
                        for di in range(DC):
                            ins = tn.matmul(
                                out=ps_yu[ipar][:],
                                lhsT=u_sb4[par][:, di * IBLK + ic * 128: di * IBLK + ic * 128 + 128],
                                rhs=xg_fm[:, di * CAP:(di + 1) * CAP],
                                start=(di == 0), stop=(di == DC - 1))
                        return ins
                    mm_yg = P.step("tensor", em_yg,
                                   waits=[("sWS", ldu), ("sVE", ve_xf),
                                          ("sSE", st4.get(f"ss{ipar}", 0)),
                                          ("sVE", st4.get(f"sv{ipar}", 0))],
                                   inc=("sMM", 1))
                    mm_yu = P.step("tensor", em_yu, inc=("sMM", 1))
                    st4["gu_rd"][par] = mm_yu
                    se_t4 = P.step("scalar", lambda s, ipar=ipar: s.activation(
                        out=sc_t4[ipar], in_=ps_yg[ipar][:], func=AF.Tanh, scale=0.5),
                        waits=[("sMM", mm_yg)], inc=("sSE", 1))
                    st4[f"ss{ipar}"] = se_t4
                    def em_hm(v, icg=icg, ipar=ipar):
                        guard(v, v.scalar_tensor_tensor(
                            out=sc_h4[ipar], in0=sc_t4[ipar], scalar=1.0,
                            in1=ps_yg[ipar][:], op0=ALU.add, op1=ALU.mult), "sCV")
                        return v.scalar_tensor_tensor(
                            out=hm4[:, icg * CAP:(icg + 1) * CAP], in0=sc_h4[ipar],
                            scalar=0.5, in1=ps_yu[ipar][:], op0=ALU.mult, op1=ALU.mult)
                    ve_hm = P.step("vector", em_hm,
                                   waits=[("sSE", se_t4), ("sMM", mm_yu)], inc=("sVE", 1))
                    st4[f"sv{ipar}"] = ve_hm
                    hm_t[icg] = ve_hm
                    if icg == 6 and e_ + 1 < NE:
                        stage_A(e_ + 1)
                    if icg >= 2:
                        plan_dn(icg - 2)
            plan_dn(14)
            plan_dn(15)
            st4["yglast"] = mm_yu
            hm_done = ve_hm
            st4["hm_prev"] = ve_hm
            mm_d = dn_t[15]
            st4["hm_rd"] = mm_d
            if DEV and p4 == 0 and e_ == 0:
                P.step("sync", lambda e: e.dma_start(out=dbg["dbg_hm"], in_=hm4[:]),
                       waits=[("sVE", hm_done)], inc=("sOUT", 16))
            ve_rcf = P.step("vector", lambda v: v.tensor_copy(
                out=rc_tm.rearrange("p (c w) -> p c w", c=2),
                in_=PS[:, 0:2048].rearrange("p (c w) -> p c w", c=2)),
                waits=[("sMM", mm_d), ("sST", st4["rc_st"])], inc=("sVE", 1))
            st4["rcf_rd"] = ve_rcf
            st_rc = P.step("sync", lambda e, e_=e_: e.dma_start(
                out=rc_scr.rearrange("(p c) w -> p c w", c=16)[:, 2 * e_:2 * e_ + 2, :],
                in_=rc_tm.rearrange("p (c w) -> p c w", c=2)),
                waits=[("sVE", ve_rcf)], inc=("sST", 16))
            st4["rc_st"] = st_rc
        st4["scat_last"] = sc_t
        rc_all = st_rc
        if DEV and p4 == 0:
            P.step("sync", lambda e: e.dma_start(out=dbg["dbg_rc"], in_=rc_scr),
                   waits=[("sST", rc_all)], inc=("sOUT", 16))

        for th in range(2):
            for ib in range(NIB):
                par = ib % 2
                def em_lds(g, ib=ib, par=par):
                    srcw = sgb.rearrange("(i p) n -> p i n", p=128)[
                        :, :, ib * IBLK:(ib + 1) * IBLK]
                    return g.dma_start(out=g_sb4[par].rearrange("p (i n) -> p i n", i=DC),
                                       in_=srcw)
                def em_ldsu(g, ib=ib, par=par):
                    srcw = sub_.rearrange("(i p) n -> p i n", p=128)[
                        :, :, ib * IBLK:(ib + 1) * IBLK]
                    return g.dma_start(out=u_sb4[par].rearrange("p (i n) -> p i n", i=DC),
                                       in_=srcw)
                ldg = P.step("gpsimd", em_lds,
                             waits=[("sMM", st4["gu_rd"].get(par, 0))], inc=("sWS", 16))
                ldu = P.step("gpsimd", em_ldsu, inc=("sWS", 16))
                for ic in range(IBLK // 128):
                    icg = ib * (IBLK // 128) + ic
                    ipar = icg % 2
                    def em_syg(tn, ic=ic, par=par, ipar=ipar, th=th):
                        ins = None
                        for di in range(DC):
                            lhsT = g_sb4[par][:, di * IBLK + ic * 128: di * IBLK + ic * 128 + 128]
                            ins = tn.matmul(
                                out=sh_g[ipar][:, 0:512], lhsT=lhsT,
                                rhs=nrm4[:, di * T4 + th * 512: di * T4 + th * 512 + 512],
                                start=(di == 0), stop=(di == DC - 1))
                        return ins
                    def em_syu(tn, ic=ic, par=par, ipar=ipar, th=th):
                        ins = None
                        for di in range(DC):
                            lhsT = u_sb4[par][:, di * IBLK + ic * 128: di * IBLK + ic * 128 + 128]
                            ins = tn.matmul(
                                out=sh_u[ipar][:, 0:512], lhsT=lhsT,
                                rhs=nrm4[:, di * T4 + th * 512: di * T4 + th * 512 + 512],
                                start=(di == 0), stop=(di == DC - 1))
                        return ins
                    mm_yg = P.step("tensor", em_syg,
                                   waits=[("sWS", ldu),
                                          ("sSE", st4.get(f"ss{ipar}", 0)),
                                          ("sVE", st4.get(f"sv{ipar}", 0)),
                                          ("sVE", st4.get("rcf_rd", 0) if ib == 0 and ic == 0 and th == 0 else 0),
                                          ("sVE", st4.get("fintr_rd", 0) if ib == 0 and ic == 0 else 0),
                                          ("sVE", st4.get("sd_ve", 0) if ib == 0 and ic == 0 else 0)],
                                   inc=("sMM", 1))
                    mm_yu = P.step("tensor", em_syu, inc=("sMM", 1))
                    st4["gu_rd"][par] = mm_yu
                    se_t4 = P.step("scalar", lambda s, ipar=ipar: s.activation(
                        out=silu4[:, ipar * 512:ipar * 512 + 512],
                        in_=sh_g[ipar][:, 0:512], func=AF.Tanh, scale=0.5),
                        waits=[("sMM", mm_yg), ("sSE", st4.get("shse", 0))],
                        inc=("sSE", 1))
                    st4[f"ss{ipar}"] = se_t4
                    st4["shse"] = se_t4
                    def em_shm(v, icg=icg, ipar=ipar):
                        guard(v, v.scalar_tensor_tensor(
                            out=shB[:], in0=silu4[:, ipar * 512:ipar * 512 + 512],
                            scalar=1.0, in1=sh_g[ipar][:, 0:512],
                            op0=ALU.add, op1=ALU.mult), "sCV")
                        return v.scalar_tensor_tensor(
                            out=hm_sh[:, icg * 512:(icg + 1) * 512], in0=shB[:],
                            scalar=0.5, in1=sh_u[ipar][:, 0:512],
                            op0=ALU.mult, op1=ALU.mult)
                    ve_hm = P.step("vector", em_shm,
                                   waits=[("sSE", se_t4), ("sMM", mm_yu)], inc=("sVE", 1))
                    st4[f"sv{ipar}"] = ve_hm
            shm_done = ve_hm
            for ib in range(NIB):
                par = ib % 2
                def em_ldsd(g, ib=ib, par=par):
                    srcw = sdb.rearrange("(i p) n -> p i n", p=128)[
                        :, 2 * ib:2 * ib + 2, :]
                    return g.dma_start(out=d_sb4[par].rearrange("p (i n) -> p i n", i=2),
                                       in_=srcw)
                ldd = P.step("gpsimd", em_ldsd,
                             waits=[("sMM", st4["d_rd"].get(par, 0))], inc=("sWS", 16))
                def em_sdn(tn, ib=ib, par=par):
                    ins = None
                    for i2 in range(2):
                        icg = 2 * ib + i2
                        for hc in range(DC):
                            ins = tn.matmul(
                                out=PS[:, hc * 512:(hc + 1) * 512],
                                lhsT=d_sb4[par][:, i2 * D + hc * 128: i2 * D + (hc + 1) * 128],
                                rhs=hm_sh[:, icg * 512:(icg + 1) * 512],
                                start=(icg == 0), stop=(icg == 15))
                    return ins
                w_d = [("sWS", ldd)]
                if ib == 0:
                    w_d += [("sVE", shm_done), ("sSE", st4.get("ss0", 0)),
                            ("sSE", st4.get("ss1", 0)), ("sVE", st4.get("sd_ve", 0)),
                            ("sSE", st4.get("fin_se", 0)), ("sVE", st4["fin_ve"])]
                mm_d = P.step("tensor", em_sdn, waits=w_d, inc=("sMM", 1))
                st4["d_rd"][par] = mm_d
            ve_sd = P.step("vector", lambda v: v.tensor_copy(out=sd_stage[:], in_=PS[:]),
                           waits=[("sMM", mm_d), ("sST", rc_all)], inc=("sVE", 1))
            st4["sd_ve"] = ve_sd
            for tcl in range(4):
                tc = th * 4 + tcl
                cpar = tc % 2
                at = acc_tc[cpar]
                def em_str(tn, tcl=tcl):
                    ins = None
                    for hc in range(DC):
                        ins = tn.transpose(
                            out=trS_ps[:, hc * 128:(hc + 1) * 128],
                            in_=sd_stage[:, hc * 512 + tcl * 128: hc * 512 + tcl * 128 + 128],
                            identity=c_id_f[:])
                    return ins
                mm_st = P.step("tensor", em_str,
                               waits=[("sVE", ve_sd), ("sVE", st4.get("fintr_rd", 0))],
                               inc=("sMM", 1))
                ve_at = P.step("vector", lambda v, at=at: v.tensor_copy(
                    out=at[:], in_=trS_ps[:]),
                    waits=[("sMM", mm_st), ("sWT", st4.get("ga_done", 0))],
                    inc=("sVE", 1))
                st4["fintr_rd"] = ve_at
                ga = P.step("gpsimd", lambda g, tc=tc, at=at: g.indirect_dma_start(
                    out=at[:], out_offset=None, in_=rc_scr,
                    in_offset=IndirectOffsetOnAxis(ap=gidx_i[:, tc:tc + 1], axis=0),
                    compute_op=ALU.add),
                    waits=[("sVE", ve_at), ("sST", rc_all)], inc=("sWT", 16))
                st4["ga_done"] = 0
                ld_x4 = P.step("sync", lambda e, t0=t0, tc=tc: e.dma_start(
                    out=x_c[:], in_=xp[128 + t0 + tc * 128: 128 + t0 + (tc + 1) * 128, :]),
                    waits=[("sVE", st4["fin_ve"])], inc=("sX", 16))
                ld_s4 = P.step("sync", lambda e, t0=t0, tc=tc: e.dma_start(
                    out=s_c[:], in_=s_scr[t0 + tc * 128:t0 + (tc + 1) * 128, :]),
                    inc=("sX", 16))
                ld_xn4 = P.step("sync", lambda e, t0=t0, tc=tc: e.dma_start(
                    out=xn_c[:], in_=xn_scr[t0 + tc * 128:t0 + (tc + 1) * 128, :]),
                    inc=("sX", 16))
                xw5 = P.counts["sX"]
                if DEV:
                    P.step("sync", lambda e, t0=t0, tc=tc, at=at: e.dma_start(
                        out=moe2_scr[t0 + tc * 128:t0 + (tc + 1) * 128, :], in_=at[:]),
                        waits=[("sWT", P.counts["sWT"])], inc=("sST", 16))
                def em_fu(v, at=at):
                    return v.tensor_tensor(out=sc1[:], in0=at[:], in1=xn_c[:], op=ALU.add)
                ve_fu = P.step("vector", em_fu,
                               waits=[("sWT", P.counts["sWT"]), ("sX", xw5),
                                      ("sSE", st4["fin_se"])],
                               inc=("sVE", 1))
                st4["ga_done"] = P.counts["sWT"]
                se_ft = P.step("scalar", lambda s: s.activation(
                    out=sc2[:], in_=sc1[:], func=AF.Tanh, scale=0.5),
                    waits=[("sVE", ve_fu)], inc=("sSE", 1))
                st4["fin_se"] = se_ft
                def em_fy(v):
                    guard(v, v.scalar_tensor_tensor(out=sc3[:], in0=sc2[:], scalar=1.0,
                                                    in1=sc1[:], op0=ALU.add,
                                                    op1=ALU.mult), "sCV")
                    guard(v, v.scalar_tensor_tensor(out=sc3[:], in0=sc3[:], scalar=0.5,
                                                    in1=s_c[:], op0=ALU.mult,
                                                    op1=ALU.mult), "sCV")
                    return v.tensor_tensor(out=sc3[:], in0=sc3[:], in1=x_c[:], op=ALU.add)
                ve_fy = P.step("vector", em_fy,
                               waits=[("sSE", se_ft), ("sOUT", st4["y_st"])],
                               inc=("sVE", 1))
                st4["fin_ve"] = ve_fy
                st_y = P.step("sync", lambda e, t0=t0, tc=tc: e.dma_start(
                    out=y[t0 + tc * 128:t0 + (tc + 1) * 128, :], in_=sc3[:]),
                    waits=[("sVE", ve_fy)], inc=("sOUT", 16))
                st4["y_st"] = st_y
        st4["nrm_rd"] = mm_d
        st4["xg_rd"] = mm_d
        st4["ga_last"] = st4["ga_done"]
        st4["acc_last"] = st4["fin_ve"]
    P.step("sync", lambda e: e.wait_ge(sems["sOUT"], P.val("sOUT")))

    with es:
        block = es.enter_context(nc.Block())
        final_out = P.val("sOUT")

        @block.sync
        def _(e):
            P.emit_for("sync", e, sems)

        @block.vector
        def _(e):
            P.emit_for("vector", e, sems)

        @block.scalar
        def _(e):
            P.emit_for("scalar", e, sems)

        @block.tensor
        def _(e):
            P.emit_for("tensor", e, sems)

        @block.gpsimd
        def _(e):
            P.emit_for("gpsimd", e, sems)

    return nc, dbg


def make_inputs(x_full, weights, core):
    b, hf = core // 2, core % 2
    xb = np.asarray(x_full[b])
    own = xb[hf * SH:(hf + 1) * SH]
    partner = xb[(1 - hf) * SH:(2 - hf) * SH]
    z96 = np.zeros((96, D), np.float32)
    lhalo = np.zeros((32, D), np.float32) if hf == 0 else xb[SH - 32:SH]
    rhalo = xb[SH:SH + 32] if hf == 0 else np.zeros((32, D), np.float32)
    xp = np.concatenate([z96, lhalo, own, rhalo, z96, partner], axis=0)
    assert xp.shape == (NPT, D)
    lmask = np.zeros((1, 192), np.float32)
    rmask = np.zeros((1, 192), np.float32)
    if hf == 0:
        lmask[0, :32] = -1e30
    else:
        rmask[0, 160:] = -1e30
    m = {"xp": np.ascontiguousarray(xp), "lmask": lmask, "rmask": rmask}
    m.update(weights)
    return m


def kernel(**inputs):
    x = inputs["L0_attn_x"]
    w = {
        "wq_lin": inputs["L0_attn_wq_lin"], "bq_lin": inputs["L0_attn_bq_lin"].reshape(1, D),
        "wk_lin": inputs["L0_attn_wk_lin"], "bk_lin": inputs["L0_attn_bk_lin"].reshape(1, D),
        "wv_lin": inputs["L0_attn_wv_lin"], "bv_lin": inputs["L0_attn_bv_lin"].reshape(1, D),
        "wq_loc": inputs["L0_attn_wq_loc"], "bq_loc": inputs["L0_attn_bq_loc"].reshape(1, D),
        "wk_loc": inputs["L0_attn_wk_loc"], "bk_loc": inputs["L0_attn_bk_loc"].reshape(1, D),
        "wv_loc": inputs["L0_attn_wv_loc"], "bv_loc": inputs["L0_attn_bv_loc"].reshape(1, D),
        "wo": inputs["L0_attn_wo"], "bo": inputs["L0_attn_bo"].reshape(1, D),
        "wg1": inputs["L0_attn_wg1"], "bg1": inputs["L0_attn_bg1"].reshape(1, D),
        "wg2": inputs["L0_attn_wg2"], "bg2": inputs["L0_attn_bg2"].reshape(1, D),
        "rel_bias": inputs["L0_attn_rel_bias"],
        "ln_g": inputs["L0_attn_ln_g"].reshape(1, D), "ln_b": inputs["L0_attn_ln_b"].reshape(1, D),
        "fln_w": inputs["L0_ffn_ln_w"].reshape(1, D), "fln_b": inputs["L0_ffn_ln_b"].reshape(1, D),
        "router_w": inputs["L0_ffn_router_w"],
        "gate_w": inputs["L0_ffn_gate_w"], "up_w": inputs["L0_ffn_up_w"],
        "down_w": inputs["L0_ffn_down_w"],
        "sh_gate": inputs["L0_ffn_sh_gate"], "sh_up": inputs["L0_ffn_sh_up"],
        "sh_down": inputs["L0_ffn_sh_down"],
    }
    w = {k: np.ascontiguousarray(v, dtype=np.float32) for k, v in w.items()}
    nc, _ = build_program()
    in_maps = [make_inputs(x, w, c) for c in range(8)]
    res = run_bass_kernel_spmd(nc, in_maps, core_ids=list(range(8)))
    out = np.empty((4, S, D), np.float32)
    for c in range(8):
        b, hf = c // 2, c % 2
        out[b, hf * SH:(hf + 1) * SH] = res.results[c]["y"]
    return out
